# version 09
# baseline (speedup 1.0000x reference)
import numpy as np
import ml_dtypes

import concourse.bass as bass
import concourse.mybir as mybir
from concourse.bass_utils import run_bass_kernel_spmd

F32 = mybir.dt.float32
BF16 = mybir.dt.bfloat16
I32 = mybir.dt.int32
AF = mybir.ActivationFunctionType
ALU = mybir.AluOpType
BF_NP = ml_dtypes.bfloat16

D = 1024
DI = 2048
NH = 16
NST = 64
CONVD = 2176
HID = 3072
T = 2048
HALO = 3
TF = T + HALO
CH = 128
NCH = T // CH
KD = D // 128
KI = DI // 128
KX = CONVD // 128
NOT = 35
KH = HID // 128
LN_EPS = 1e-5
RMS_EPS = 1e-5
DIMSCALE = float(D) ** -0.5
QSCALE = 1.0 / 127.0
ENGS = ("tensor", "vector", "scalar", "gpsimd", "sync")

NBLK = [(0, 512), (512, 512), (1024, 512), (1536, 512), (2048, 3)]
NBLKG = [(0, 3), (3, 512), (515, 512), (1027, 512), (1539, 512)]
CBLK = [(HALO + 1536, 512), (HALO + 1024, 512), (HALO + 512, 512), (HALO, 512)]


SEM_OWNER = {"t": "tensor", "v": "vector", "s": "scalar", "g": "gpsimd",
             "qg": "gpsimd", "qs": "sync", "cc": "gpsimd"}


class Sched:
    def __init__(self):
        self.ops = {e: [] for e in ENGS}
        self.cnt = {s: 0 for s in ("t", "v", "s", "g", "qg", "qs", "cc")}
        self.handles = {}
        self.lastw = {e: {} for e in ENGS}

    def truncate(self, keep):
        for e in ENGS:
            self.ops[e] = [op for op in self.ops[e] if op[4] < keep]

    def emit(self, eng, fn, waits=(), inc=None, inc_by=1):
        wl = []
        for w in waits:
            if w is None:
                continue
            s, v = w
            assert v <= self.cnt[s], f"deadlock: {eng} waits {s}>={v}, scheduled {self.cnt[s]}"
            if SEM_OWNER[s] == eng:
                if s in ("t", "v", "s", "g"):
                    continue
                v = self.cnt[s]
            if self.lastw[eng].get(s, -1) < v:
                self.lastw[eng][s] = v
                wl.append((s, v))
        ev = None
        if inc is not None:
            self.cnt[inc] += inc_by
            ev = (inc, self.cnt[inc])
        self.ops[eng].append((fn, tuple(wl), inc, inc_by, sum(len(v) for v in self.ops.values())))
        return ev

    def run(self, eng_name, eng):
        for fn, wl, inc, inc_by, _gi in self.ops[eng_name]:
            for s, v in wl:
                eng.wait_ge(self.handles[s], v)
            ins = fn(eng)
            if inc is not None:
                ins.then_inc(self.handles[inc], inc_by)


class Banks:
    def __init__(self):
        self.last_w = [None] * 8
        self.readers = [[] for _ in range(8)]

    def w_waits(self, banks):
        w = []
        for b in banks:
            w += [ev for (_, ev) in self.readers[b]]
            if self.last_w[b] is not None:
                w.append(self.last_w[b])
        return w

    def wrote(self, banks, ev):
        for b in banks:
            self.last_w[b] = ev
            self.readers[b] = []

    def r_waits(self, banks, eng=None):
        w = [self.last_w[b] for b in banks if self.last_w[b] is not None]
        if eng is not None:
            for b in banks:
                w += [ev for (re, ev) in self.readers[b] if re != eng]
        return w

    def read(self, banks, ev, eng=None):
        for b in banks:
            self.readers[b].append((eng, ev))


def pa_banks(lo, n):
    return list(range(lo // 512, (lo + n - 1) // 512 + 1))


def pb_banks(lo, n):
    return [4 + b for b in pa_banks(lo, n)]


class Arena:
    def __init__(self, nc, total_f32):
        self.t = nc.alloc_sbuf_tensor("ARENA", [128, total_f32], F32)
        self.total = total_f32
        self.allocs = []

    def reserve(self, name, f32_elems, p0, p1, align=16):
        f32_elems = -(-f32_elems // align) * align
        self.allocs.append([name, None, f32_elems, p0, p1])

    def _try(self, order):
        placed = []
        offs = {}
        hi = 0
        for name, _, sz, p0, p1 in order:
            spans = sorted((o, o + s) for (o, s, b0, b1) in placed
                           if not (b1 < p0 or b0 > p1))
            off = 0
            for s, e in spans:
                if off + sz <= s:
                    break
                off = max(off, e)
            placed.append((off, sz, p0, p1))
            offs[name] = off
            hi = max(hi, off + sz)
        return offs, hi

    def finalize(self):
        best = None
        for key in [lambda a: (-(a[4] - a[3]), -a[2]),
                    lambda a: (-a[2], -(a[4] - a[3])),
                    lambda a: (a[3], -a[2]),
                    lambda a: (a[3], a[4], -a[2]),
                    lambda a: (-a[2], a[3])]:
            offs, hi = self._try(sorted(self.allocs, key=key))
            if best is None or hi < best[1]:
                best = (offs, hi)
        assert best[1] <= self.total, \
            f"SBUF arena overflow: best packing {best[1]} > {self.total}"
        return best[0]


def build_program(debug=False, trunc=None):
    nc = bass.Bass("TRN2", target_bir_lowering=False, detect_race_conditions=False)

    uT = nc.dram_tensor("uT", [D, TF], F32, kind="ExternalInput").ap()
    wip = nc.dram_tensor("wip", [NOT, 128, D], F32, kind="ExternalInput").ap()
    wopb = nc.dram_tensor("wopb", [8, 128, DI], F32, kind="ExternalInput").ap()
    wgu = nc.dram_tensor("wgu", [KH, 2, 128, D], I32, kind="ExternalInput").ap()
    wdb = nc.dram_tensor("wdb", [8, 128, HID], I32, kind="ExternalInput").ap()
    convd = nc.dram_tensor("convd", [128, 4 * 18 * 128], F32, kind="ExternalInput").ap()
    convbi = nc.dram_tensor("convbi", [128, 18], F32, kind="ExternalInput").ap()
    xnw = nc.dram_tensor("xnw", [128, KD], F32, kind="ExternalInput").ap()
    xnb = nc.dram_tensor("xnb", [128, KD], F32, kind="ExternalInput").ap()
    rmsw = nc.dram_tensor("rmsw", [128, KI], F32, kind="ExternalInput").ap()
    dtbi = nc.dram_tensor("dtbi", [16, 1], F32, kind="ExternalInput").ap()
    alogi = nc.dram_tensor("alogi", [16, 1], F32, kind="ExternalInput").ap()
    dDt = nc.dram_tensor("dDt", [128, NH * 128], F32, kind="ExternalInput").ap()
    maskt = nc.dram_tensor("maskt", [128, 1], F32, kind="ExternalInput").ap()
    out = nc.dram_tensor("out", [D, T], F32, kind="ExternalOutput").ap()

    e_in = nc.dram_tensor("e_in", [1, NST * DI], F32, kind="Internal")
    e_out = nc.dram_tensor("e_out", [2, NST * DI], F32, kind="Internal")

    taps = {}
    if debug:
        for name, shp in [("t_xf", [128, KD * 1024]), ("t_h", [128, KH * 1024]),
                          ("t_stg", [128, KD * 1024]),
                          ("t_sip1", [128, NOT]), ("t_sip2", [128, NOT]),
                          ("t_sip3", [128, NOT]), ("t_negA", [16, 1]),
                          ("t_dabf", [16, T]), ("t_sop", [128, 8]),
                          ("t_wf7", [128, DI]),
                          ("t_u2", [128, KD * TF]), ("t_z", [128, KI * TF]),
                          ("t_xbc", [128, KX * TF]), ("t_x", [128, KX * TF]),
                          ("t_sz", [128, KI * TF]), ("t_c", [64, TF]),
                          ("t_dt", [16, TF]), ("t_cum", [16, T]),
                          ("t_E", [64, NH * 128]), ("t_gn", [128, 16 * TF]),
                          ("t_attn", [128, KD * T]), ("t_r1", [128, KD * T])]:
            taps[name] = nc.dram_tensor(name, shp, F32, kind="ExternalOutput").ap()

    trin = np.tril(np.ones((128, 128), np.float32)).T.copy()
    TRIb = nc.inline_tensor(trin.astype(BF_NP), "TRIb").ap()
    tm_ = np.zeros((128, 512), np.float32)
    for h in range(4):
        tm_[:, h * 128:(h + 1) * 128] = -1000.0 * np.tril(np.ones((128, 128)), -1)
    TMASKb = nc.inline_tensor(tm_.astype(BF_NP), "TMASKb").ap()
    eye = np.eye(128, dtype=np.float32)
    I128b = nc.inline_tensor(eye.astype(BF_NP), "I128b").ap()
    I128f = nc.inline_tensor(eye.copy(), "I128f").ap()
    ONESb1 = nc.inline_tensor(np.ones((1, 128), BF_NP), "ONESb1").ap()
    ONESf1 = nc.inline_tensor(np.ones((1, 128), np.float32), "ONESf1").ap()
    ONESbc = nc.inline_tensor(np.ones((128, 1), BF_NP), "ONESbc").ap()
    ONESpp = nc.inline_tensor(np.ones((128, 128), BF_NP), "ONESpp").ap()
    ONESf11 = nc.inline_tensor(np.ones((1, 1), np.float32), "ONESf11").ap()
    CVALS = [0.0, 1.0, LN_EPS * LN_EPS, 1.0 + LN_EPS, -0.5, 0.5, QSCALE,
             LN_EPS, RMS_EPS, 1.0 / DI]
    CVN = nc.inline_tensor(np.tile(np.array(CVALS, np.float32), (128, 1)), "CVN").ap()

    A = Arena(nc, 52160)
    P = {}

    def AL(name, parts, n, dt_, p0, p1):
        f32e = n if dt_ == F32 else (n + 1) // 2
        A.reserve(name, f32e, p0, p1)
        P[name] = (None, parts, n, dt_)

    def V(name, lo=0, n=None, parts=None, p0=0):
        off, pts, tot, dt_ = P[name]
        if n is None:
            n = tot - lo
        if parts is None:
            parts = pts
        base = A.t.ap() if dt_ == F32 else A.t.ap().bitcast(BF16)
        o = off if dt_ == F32 else 2 * off
        return base[p0:p0 + parts, o + lo:o + lo + n]

    AL("cvals", 128, 16, F32, 0, 79)
    AL("ubf", 128, KD * TF, BF16, 0, 9)
    AL("u2", 128, KD * TF, BF16, 0, 19)
    AL("zsz", 128, KI * TF, BF16, 10, 59)
    AL("xar", 128, KX * TF, BF16, 10, 79)
    AL("cfm", 64, TF, BF16, 10, 59)
    AL("r1", 128, KD * T, F32, 60, 79)
    for nm, pts, n, dt_, pp0, pp1 in [
            ("ctri", 128, 128, BF16, 30, 59), ("ci128b", 128, 128, BF16, 0, 79),
            ("ci128f", 128, 128, F32, 0, 79), ("conesb1", 1, 128, BF16, 0, 79),
            ("conesf1", 1, 128, F32, 0, 79), ("conesbc", 128, 1, BF16, 0, 79),
            ("conespp", 128, 128, BF16, 0, 79), ("conesf11", 1, 1, F32, 0, 79),
            ("cdD", 128, NH * 128, BF16, 30, 59), ("cconvb", 128, 18, F32, 20, 29),
            ("cxnw", 128, KD, F32, 0, 19), ("cxnb", 128, KD, F32, 0, 19),
            ("crmsw", 128, KI, F32, 60, 69), ("cdtb", 16, 1, F32, 0, 19),
            ("calog", 16, 1, F32, 0, 39), ("cmask", 128, 1, F32, 0, 49)]:
        AL(nm, pts, n, dt_, pp0, pp1)
    AL("negA", 16, 1, F32, 10, 39)
    AL("vdummy", 128, 1, F32, 0, 79)
    AL("p0a", 128, TF, F32, 0, 9)
    AL("p0b", 128, TF, F32, 0, 9)
    AL("p0c", 128, TF, F32, 0, 9)
    AL("wipbuf", 128, 2 * D, BF16, 10, 19)
    AL("wsq", 128, D, BF16, 10, 19)
    AL("wopnb", 128, DI, BF16, 10, 29)
    AL("snrow", 1, 128, F32, 10, 69)
    AL("sip", 128, NOT, F32, 10, 19)
    AL("convdg", 128, 4 * 18 * 128, BF16, 20, 29)
    AL("dtfm", 16, TF, F32, 10, 39)
    AL("cumfm", 16, T, F32, 30, 39)
    AL("lntmp", 16, T, F32, 30, 33)
    AL("dabf", 16, T, BF16, 30, 39)
    AL("datm", 128, NCH * 16, BF16, 30, 39)
    AL("cumhi", 16, T, BF16, 30, 59)
    AL("najptm", 128, NCH * 16, F32, 30, 59)
    AL("cumlo", 16, T, BF16, 30, 59)
    AL("najpfm", 16, T, F32, 30, 36)
    AL("ctmask", 128, 512, BF16, 30, 59)
    AL("wfm", 16, T, F32, 34, 39)
    AL("wtm", 128, NCH * 16, BF16, 30, 59)
    AL("cdrow", 1, NH * NCH, F32, 30, 59)
    AL("S2", 64, NH * 128, F32, 30, 59)
    AL("Sbf", 64, NH * 128, BF16, 50, 59)
    AL("Erecv", 64, NH * 128, F32, 40, 50)
    AL("xw", 128, NH * 128, BF16, 30, 59)
    AL("btm", 128, 64, BF16, 30, 59)
    AL("rowhl", 2, NH * 128, BF16, 51, 59)
    AL("xtm", 128, NH * 128, BF16, 51, 59)
    AL("es", 128, NH * 128, BF16, 51, 59)
    AL("eb", 64, NH * 128, BF16, 51, 59)
    AL("ce", 64, NH * 128, BF16, 51, 59)
    AL("gbuf", 128, NH * 128, BF16, 51, 59)
    AL("rrow", 1, 128, F32, 51, 59)
    AL("attnbf", 128, KD * T, BF16, 60, 69)
    AL("sq6", 128, KD * 512, BF16, 62, 65)
    AL("al6", 128, T, F32, 60, 69)
    AL("mu6", 128, T, F32, 60, 69)
    AL("tb6", 128, 512, F32, 66, 69)
    AL("wopbf", 128, 2 * DI, BF16, 60, 69)
    AL("sop", 128, 8, F32, 10, 69)
    AL("r1bfh", 128, KD * 1024, BF16, 70, 79)
    AL("sqstg", 128, KD * 1024, BF16, 70, 79)
    AL("al7", 128, 1024, F32, 70, 79)
    AL("mu7", 128, 1024, F32, 70, 79)
    AL("tb7", 128, 512, F32, 70, 79)
    AL("wgubuf", 128, 2 * 2 * D, BF16, 70, 79)
    AL("wdbuf", 128, 2 * KH * 128, BF16, 70, 79)
    AL("sgbuf", 128, 1024, BF16, 70, 79)

    _offs = A.finalize()
    for _nm in list(P):
        P[_nm] = (_offs[_nm],) + P[_nm][1:]
    P["g2buf"] = (P["xtm"][0], 128, NH * 128, BF16)
    for i, v in enumerate(CVALS):
        nc.const_aps.register(F32, float(v), V("cvals", i, 1))

    PA = nc.alloc_psum_tensor("PA", [128, 2048], F32)
    PB = nc.alloc_psum_tensor("PB", [128, 2048], F32)
    PAb = PA.ap().bitcast(BF16)

    S = Sched()
    bk = Banks()

    def ld(dst, src, waits=(), q="qg", slow=False):
        eng = "gpsimd" if q == "qg" else "sync"

        def f(e):
            return e.dma_start(out=dst, in_=src, allow_slow_non_contiguous=slow)
        return S.emit(eng, f, waits=waits, inc=q, inc_by=16)

    def mm_group(items, waits=()):
        banks = sorted({b for it in items for b in it[5]})
        w = list(waits) + bk.w_waits(banks)
        ev = None
        n = len(items)
        for i, (o, l, r, st, sp, _) in enumerate(items):
            def f(e, o=o, l=l, r=r, st=st, sp=sp):
                return e.matmul(o, l, r, start=st, stop=sp)
            rv = S.emit("tensor", f, waits=(w if i == 0 else ()),
                        inc=("t" if i == n - 1 else None))
            if rv:
                ev = rv
        bk.wrote(banks, ev)
        return ev

    def tr_group(items, waits=()):
        banks = sorted({b for it in items for b in it[3]})
        w = list(waits) + bk.w_waits(banks)
        ev = None
        n = len(items)
        for i, (o, src, idn, _) in enumerate(items):
            def f(e, o=o, src=src, idn=idn):
                return e.transpose(o, src, idn)
            rv = S.emit("tensor", f, waits=(w if i == 0 else ()),
                        inc=("t" if i == n - 1 else None))
            if rv:
                ev = rv
        bk.wrote(banks, ev)
        return ev

    def veg(eng, fn, rb=(), wb=(), waits=()):
        sem = {"vector": "v", "scalar": "s", "gpsimd": "g"}[eng]
        w = list(waits) + bk.r_waits(list(rb), eng) + bk.w_waits(list(wb))
        ev = S.emit(eng, fn, waits=w, inc=sem)
        if rb:
            bk.read(list(rb), ev, eng)
        if wb:
            bk.wrote(list(wb), ev)
        return ev

    def tap(name, src, waits):
        if debug:
            return ld(taps[name], src, waits=waits, q="qg")
        return None

    def rep_h(col_ap, reps=128):
        return col_ap.unsqueeze(2).broadcast_to([col_ap.shape[0], col_ap.shape[1], reps])

    def rep_o(t2d, k):
        return t2d.unsqueeze(1).broadcast_to([t2d.shape[0], k, t2d.shape[1]])

    def h3(name, k):
        return V(name).rearrange("p (k t) -> p k t", k=k)

    const_evs = []
    const_evs.append(ld(V("cvals", 0, 10), CVN))
    for dst, srcd in [(V("ci128b"), I128b), (V("ci128f"), I128f),
                      (V("conesb1"), ONESb1), (V("conesf1"), ONESf1),
                      (V("conesbc"), ONESbc), (V("conespp"), ONESpp),
                      (V("conesf11"), ONESf11),
                      (V("cxnw"), xnw), (V("cxnb"), xnb),
                      (V("cdtb"), dtbi), (V("calog"), alogi),
                      (V("cmask"), maskt)]:
        const_evs.append(ld(dst, srcd))
    ev_consts = const_evs[-1]
    ev_ubf = ld(V("ubf").rearrange("p (k t) -> p k t", k=KD),
                uT.rearrange("(k p) t -> p k t", p=128))

    ubf3 = h3("ubf", KD)
    u23 = h3("u2", KD)
    ev_sq0 = veg("vector", lambda e: e.tensor_tensor(V("u2"), V("ubf"), V("ubf"), ALU.mult),
                 waits=[ev_ubf])

    def stats_mms(src3, wt, nparts=KD):
        items = []
        for (lo, n) in NBLK:
            if lo < 2048:
                tgt, bks = PA.ap()[:, lo:lo + n], pa_banks(lo, n)
            else:
                tgt, bks = PB.ap()[:, 0:n], pb_banks(0, n)
            for k in range(nparts):
                items.append((tgt, V("conespp"), src3[:, k, lo:lo + n],
                              k == 0, k == nparts - 1, bks))
        return mm_group(items, waits=wt)

    ev_s0 = stats_mms(ubf3, [ev_ubf])
    ev_mu0a = veg("vector", lambda e: e.tensor_scalar(V("p0b", 0, 2048), PA.ap()[:, :], 1.0 / D, None, ALU.mult),
                  rb=pa_banks(0, 2048))
    ev_mu0b = veg("vector", lambda e: e.tensor_scalar(V("p0b", 2048, 3), PB.ap()[:, 0:3], 1.0 / D, None, ALU.mult),
                  rb=pb_banks(0, 3))
    ev_s1 = stats_mms(u23, [ev_sq0, ev_mu0a, ev_mu0b])
    ev_ms0a = veg("vector", lambda e: e.tensor_scalar(V("p0a", 0, 2048), PA.ap()[:, :], 1.0 / D, None, ALU.mult),
                  rb=pa_banks(0, 2048))
    ev_ms0b = veg("vector", lambda e: e.tensor_scalar(V("p0a", 2048, 3), PB.ap()[:, 0:3], 1.0 / D, None, ALU.mult),
                  rb=pb_banks(0, 3))
    ev_c0 = veg("vector", lambda e: e.tensor_tensor(V("p0c"), V("p0b"), V("p0b"), ALU.mult),
                waits=[ev_mu0a, ev_mu0b])
    ev_v1 = veg("vector", lambda e: e.tensor_tensor(V("p0a"), V("p0a"), V("p0c"), ALU.subtract),
                waits=[ev_c0, ev_ms0a, ev_ms0b])
    ev_l0 = veg("scalar", lambda e: e.activation(V("p0c"), V("p0a"), AF.Ln,
                                                 bias=LN_EPS * LN_EPS, scale=1.0 + LN_EPS),
                waits=[ev_v1])
    ev_a0 = veg("scalar", lambda e: e.activation(V("p0a"), V("p0c"), AF.Exp, scale=-0.5),
                waits=[ev_l0])
    ev_u2a = veg("vector", lambda e: e.tensor_tensor(u23, ubf3, rep_o(V("p0b"), KD), ALU.subtract),
                 waits=[ev_s1, ev_ubf])
    ev_u2b = veg("vector", lambda e: e.tensor_tensor(u23, u23, rep_o(V("p0a"), KD), ALU.mult),
                 waits=[ev_u2a, ev_a0])
    ev_u2done = None
    for k in range(KD):
        ev_u2done = veg("vector", lambda e, k=k: e.tensor_scalar(
            u23[:, k, :], u23[:, k, :], V("cxnw", k, 1), V("cxnb", k, 1),
            ALU.mult, ALU.add),
            waits=[ev_u2b, ev_consts])
    tap("t_u2", V("u2"), [ev_u2done])

    zsz3 = h3("zsz", KI)
    xar3 = h3("xar", KX)
    ev_wip_dma = [None] * NOT
    ev_wip_lastmm = [None] * NOT
    ev_wsq_ev = [None] * NOT
    ev_nm_ev = [None] * NOT
    ev_sip_parts = []
    ev_convdg = None

    for o in range(NOT):
        slot = o % 2
        wbv = V("wipbuf", slot * D, D)
        wt = [ev_consts] if o < 2 else [ev_wsq_ev[o - 2]]
        ev_wip_dma[o] = ld(wbv, wip[o], waits=wt)
        ev_wsq_ev[o] = veg("vector", lambda e, wbv=wbv: e.tensor_tensor(V("wsq"), wbv, wbv, ALU.mult),
                           waits=[ev_wip_dma[o], ev_nm_ev[o - 1] if o else None])
        nm_items = [(PB.ap()[0:1, 512:640], V("conesbc"),
                     V("wsq", k * 128, 128), k == 0, k == KD - 1, [5]) for k in range(KD)]
        ev_nm_ev[o] = mm_group(nm_items, waits=[ev_wsq_ev[o]])
        ev_row = veg("vector", lambda e: e.tensor_copy(V("snrow"), PB.ap()[0:1, 512:640]), rb=[5])
        ev_tc = mm_group([(PB.ap()[:, 1024:1025], V("snrow"), V("conesf11"), True, True, [6])],
                         waits=[ev_row])
        ev_sip_parts.append(
            veg("vector", lambda e, o=o: e.tensor_copy(V("sip", o, 1), PB.ap()[:, 1024:1025]), rb=[6]))
    veg("vector", lambda e: e.memzero(V("vdummy")))
    veg("vector", lambda e: e.memzero(V("vdummy")))
    ev_sc0 = veg("vector", lambda e: e.tensor_scalar(V("sip"), V("sip"), 1e-12, None, ALU.max),
                 waits=[ev_sip_parts[-1]])
    ev_sc1 = veg("scalar", lambda e: e.activation(V("sip"), V("sip"), AF.Ln, scale=float(D)),
                 waits=[ev_sc0])
    ev_sip = veg("scalar", lambda e: e.activation(V("sip"), V("sip"), AF.Exp, scale=-0.5),
                 waits=[ev_sc1])
    tap("t_sip1", V("sip"), [ev_sip])

    ev_z_last = ev_x_last = ev_dt_last = ev_c_last = None
    grp = 0
    ev_opn_sq = [None] * 8
    ev_opn_nmB = [None] * 8
    ev_opn_parts = []

    def emit_opnorm(o2):
        wt = [ev_u2done] if o2 == 0 else [ev_opn_sq[o2 - 1]]
        dma = ld(V("wopnb"), wopb[o2], waits=wt, q="qg")
        sqA = veg("vector", lambda e: e.tensor_tensor(
            V("wsq"), V("wopnb", 0, D), V("wopnb", 0, D), ALU.mult),
            waits=[dma, ev_nm_ev[NOT - 1] if o2 == 0 else ev_opn_nmB[o2 - 1]])
        nmA = mm_group([(PB.ap()[0:1, 512:640], V("conesbc"), V("wsq", k * 128, 128),
                         k == 0, False, [5]) for k in range(KD)], waits=[sqA])
        sqB = veg("vector", lambda e: e.tensor_tensor(
            V("wsq"), V("wopnb", D, D), V("wopnb", D, D), ALU.mult),
            waits=[nmA])
        ev_opn_sq[o2] = sqB
        ev_opn_nmB[o2] = mm_group([(PB.ap()[0:1, 512:640], V("conesbc"), V("wsq", k * 128, 128),
                                    False, k == KD - 1, [5]) for k in range(KD)], waits=[sqB])
        orow = veg("vector", lambda e: e.tensor_copy(V("snrow"), PB.ap()[0:1, 512:640]), rb=[5])
        otc = mm_group([(PB.ap()[:, 1024:1025], V("snrow"), V("conesf11"), True, True, [6])],
                       waits=[orow])
        ev_opn_parts.append(
            veg("vector", lambda e, o2=o2: e.tensor_copy(V("sop", o2, 1), PB.ap()[:, 1024:1025]), rb=[6]))

    for o in range(NOT):
        slot = o % 2
        wt = ([ev_wsq_ev[NOT - 1], ev_wsq_ev[NOT - 2]] if o < 2
              else [ev_wip_lastmm[o - 2]])
        ev_wip_dma[o] = ld(V("wipbuf", slot * D, D), wip[o], waits=wt)
        if 2 <= o < 10:
            emit_opnorm(o - 2)
        for bi, (lo, n) in enumerate(NBLKG):
            g4 = grp % 4
            grp += 1
            pslice = PA.ap()[:, g4 * 512:g4 * 512 + n]
            bks = [g4]
            items = [(pslice, V("wipbuf", slot * D + k * 128, 128),
                      u23[:, k, lo:lo + n], k == 0, k == KD - 1, bks)
                     for k in range(KD)]
            ev_mm = mm_group(items, waits=[ev_wip_dma[o], ev_u2done])
            ev_wip_lastmm[o] = ev_mm
            if o < 16:
                ev_z_last = veg("vector", lambda e, o=o, lo=lo, n=n, p=pslice: e.tensor_scalar(
                    zsz3[:, o, lo:lo + n], p, V("sip", o, 1), None, ALU.mult),
                    rb=bks, waits=[ev_sip])
            elif o < 33:
                tx = o - 16
                if bi > 0:
                    ev_x_last = veg("vector", lambda e, tx=tx, lo=lo, n=n, p=pslice, o=o: e.tensor_scalar(
                        xar3[:, tx, lo:lo + n], p, V("sip", o, 1), None, ALU.mult),
                        rb=bks, waits=[ev_sip])
                else:
                    ev_x_last = veg("vector", lambda e, tx=tx, lo=lo, n=n, p=pslice, o=o: e.tensor_scalar(
                        xar3[:, tx, lo:lo + n], p, V("sip", o, 1), V("cmask"), ALU.mult, ALU.mult),
                        rb=bks, waits=[ev_sip])
            elif o == 33:
                pd = pslice[0:16, :]
                e1 = veg("scalar", lambda e, pd=pd, lo=lo, n=n: e.activation(
                    V("dtfm", lo, n), pd, AF.Exp, bias=V("cdtb"), scale=V("sip", 33, 1, parts=16)),
                    rb=bks, waits=[ev_sip, ev_consts])
                ev_dt_last = veg("scalar", lambda e, lo=lo, n=n: e.activation(
                    V("dtfm", lo, n), V("dtfm", lo, n), AF.Ln, bias=1.0), waits=[e1])
            else:
                pc = pslice[0:64, :]
                if bi > 0:
                    ev_c_last = veg("vector", lambda e, pc=pc, lo=lo, n=n: e.tensor_scalar(
                        V("cfm", lo, n), pc, V("sip", 34, 1, parts=64), None, ALU.mult),
                        rb=bks, waits=[ev_sip])
                else:
                    ev_c_last = veg("vector", lambda e, pc=pc, lo=lo, n=n: e.tensor_scalar(
                        V("cfm", lo, n), pc, V("sip", 34, 1, parts=64), V("cmask", parts=64),
                        ALU.mult, ALU.mult),
                        rb=bks, waits=[ev_sip])
    ev_so0 = veg("vector", lambda e: e.tensor_scalar(V("sop"), V("sop"), 1e-12, None, ALU.max),
                 waits=[ev_opn_parts[-1]])
    ev_so1 = veg("scalar", lambda e: e.activation(V("sop"), V("sop"), AF.Ln), waits=[ev_so0])
    ev_sop = veg("scalar", lambda e: e.activation(V("sop"), V("sop"), AF.Exp, scale=-0.5),
                 waits=[ev_so1])
    tap("t_sop", V("sop"), [ev_sop])

    ev_tap_z = tap("t_z", V("zsz"), [ev_z_last])
    ev_tap_x = tap("t_xbc", V("xar"), [ev_x_last])
    tap("t_dt", V("dtfm"), [ev_dt_last])

    p1_done = [ev_wip_lastmm[NOT - 1], ev_sip, ev_x_last, ev_z_last, ev_dt_last, ev_c_last]
    tap("t_sip2", V("sip"), p1_done)
    ev_convdg = ld(V("convdg"), convd, waits=p1_done)
    ev_convb = ld(V("cconvb"), convbi, waits=p1_done)
    ev_zsilu = veg("scalar", lambda e: e.activation(V("zsz"), V("zsz"), AF.Silu),
                   waits=[ev_z_last, ev_tap_z])
    ev_conv = None
    grp = 0
    for t in [17] + list(range(17)):
        for (lo, n) in CBLK:
            g4 = grp % 4
            grp += 1
            pslice = PA.ap()[:, g4 * 512:g4 * 512 + n]
            bks = [g4]
            if t < 17:
                rhs = [xar3[:, t, lo - 3 + k:lo - 3 + k + n] for k in range(4)]
                items = [(pslice, V("convdg", (k * 18 + t) * 128, 128),
                          rhs[k], k == 0, k == 3, bks) for k in range(4)]
            else:
                rhs = [V("xar", 16 * TF + lo - 3 + k, n) for k in range(4)]
                items = [(pslice[0:64, :], V("convdg", (k * 18 + 17) * 128, 64),
                          rhs[k], k == 0, k == 3, bks) for k in range(4)]
            ev_mm = mm_group(items, waits=[ev_convdg, ev_x_last, ev_c_last])
            if t < 17:
                ev_conv = veg("scalar", lambda e, t=t, lo=lo, n=n, p=pslice: e.activation(
                    xar3[:, t, lo:lo + n], p, AF.Silu, bias=V("cconvb", t, 1)),
                    rb=bks, waits=[ev_zsilu, ev_tap_x, ev_convb])
            else:
                ev_conv = veg("scalar", lambda e, lo=lo, n=n, p=pslice: e.activation(
                    V("cfm", lo, n), p[0:64, :], AF.Silu, bias=V("cconvb", 17, 1, parts=64)),
                    rb=bks, waits=[ev_zsilu, ev_convb])
    tap("t_x", V("xar"), [ev_conv])
    tap("t_sz", V("zsz"), [ev_conv])
    tap("t_c", V("cfm"), [ev_conv])

    dttok = V("dtfm", HALO, T)
    ev_ctri = ld(V("ctri"), TRIb, waits=[ev_conv])
    ev_cdDl = ld(V("cdD"), dDt, waits=[ev_conv])
    ev_negA = veg("scalar", lambda e: e.activation(V("negA"), V("calog"), AF.Exp), waits=[ev_consts])
    ev_dabf = veg("vector", lambda e: e.tensor_scalar(V("dabf"), dttok, V("negA"), -1.0,
                                                      ALU.mult, ALU.mult),
                  waits=[ev_dt_last, ev_negA])
    tap("t_negA", V("negA"), [ev_negA])
    tap("t_dabf", V("dabf"), [ev_dabf])
    tr_items = [(PAb[:, 2048 + c * 16:2048 + c * 16 + 16],
                 V("dabf", c * 128, 128), V("ci128b", 0, 16, parts=16), [2]) for c in range(NCH)]
    ev_datr = tr_group(tr_items, waits=[ev_dabf, ev_consts])
    ev_datm = veg("vector", lambda e: e.tensor_copy(V("datm"), PAb[:, 2048:2304]), rb=[2])
    cum_items = [(PB.ap()[0:16, c * 128:(c + 1) * 128], V("datm", c * 16, 16),
                  V("ctri"), True, True, pb_banks(c * 128, 128)) for c in range(NCH)]
    ev_cummm = mm_group(cum_items, waits=[ev_datm, ev_ctri])
    ev_cum = veg("vector", lambda e: e.tensor_copy(V("cumfm"), PB.ap()[0:16, 0:2048]),
                 rb=pb_banks(0, 2048))
    tap("t_cum", V("cumfm"), [ev_cum])
    ev_ln = veg("scalar", lambda e: e.activation(V("lntmp"), dttok, AF.Ln), waits=[ev_dt_last])
    ev_najp = veg("vector", lambda e: e.tensor_tensor(V("najpfm"), V("lntmp"), V("cumfm"), ALU.subtract),
                  waits=[ev_ln, ev_cum])
    tr_items = [(PA.ap()[:, 1024 + c * 16:1024 + c * 16 + 16],
                 V("najpfm", c * 128, 128), V("ci128f", 0, 16, parts=16), [2]) for c in range(NCH)]
    ev_najptr = tr_group(tr_items, waits=[ev_najp])
    ev_najptm = veg("vector", lambda e: e.tensor_copy(V("najptm"), PA.ap()[:, 1024:1280]), rb=[2])
    ev_tmk = ld(V("ctmask"), TMASKb, waits=[ev_conv])
    ev_chi = veg("scalar", lambda e: e.copy(V("cumhi"), V("cumfm")), waits=[ev_cum])
    ev_clo = veg("vector", lambda e: e.tensor_tensor(V("cumlo"), V("cumfm"), V("cumhi"), ALU.subtract),
                 waits=[ev_chi])
    ev_wf = None
    for c in range(NCH):
        ev_wf = veg("vector", lambda e, c=c: e.tensor_scalar(
            V("wfm", c * 128, 128), V("cumfm", c * 128, 128), -1.0,
            V("cumfm", c * 128 + 127, 1), ALU.mult, ALU.add), waits=[ev_cum])
    ev_wexp = veg("scalar", lambda e: e.activation(V("wfm"), V("wfm"), AF.Exp), waits=[ev_wf])
    ev_wmul = veg("vector", lambda e: e.tensor_tensor(V("wfm"), V("wfm"), dttok, ALU.mult),
                  waits=[ev_wexp])
    ev_wbf = veg("scalar", lambda e: e.copy(V("dabf"), V("wfm")), waits=[ev_wmul, ev_datm, ev_cummm])
    tr_items = [(PAb[:, 2048 + c * 16:2048 + c * 16 + 16],
                 V("dabf", c * 128, 128), V("ci128b", 0, 16, parts=16), [2]) for c in range(NCH)]
    ev_wtr = tr_group(tr_items, waits=[ev_wbf])
    ev_wtm = veg("vector", lambda e: e.tensor_copy(V("wtm"), PAb[:, 2048:2304]), rb=[2])
    ev_cdg = ld(V("cdrow"), V("cumfm")[:, 127::128], waits=[ev_cum], q="qs", slow=True)
    ev_cdx = veg("scalar", lambda e: e.activation(V("cdrow"), V("cdrow"), AF.Exp), waits=[ev_cdg])
    ev_s2z = veg("vector", lambda e: e.memzero(V("S2")))

    def chunk_scan(c, phase5, pv):
        ccol = HALO + c * CH
        ev = {}
        tr_items = [(PAb[:, h * 128:(h + 1) * 128],
                     xar3[:, h, ccol:ccol + 128], V("ci128b"), [0, 1]) for h in range(NH)]
        ev_xt = tr_group(tr_items, waits=[ev_conv])
        ev_bt = tr_group([(PAb[:, 3072:3136], V("xar", 16 * TF + ccol, 128, parts=64),
                           V("ci128b", 0, 64, parts=64), [3])], waits=[ev_conv])
        ev_btm = veg("vector", lambda e: e.tensor_copy(V("btm"), PAb[:, 3072:3136]),
                     rb=[3], waits=[pv.get("st")])
        xw3 = h3("xw", NH)
        pab3 = PAb[:, 0:2048].rearrange("p (h i) -> p h i", h=NH)
        ev_xw = veg("vector", lambda e: e.tensor_tensor(
            xw3, pab3, rep_h(V("wtm", c * 16, 16)), ALU.mult),
            rb=[0, 1], waits=[ev_wtm, pv.get("st")])
        if phase5:
            ev["xtm"] = veg("scalar", lambda e: e.copy(V("xtm"), PAb[:, 0:2048]),
                            rb=[0, 1], waits=[pv.get("y"), pv.get("rr"), pv.get("s2")])
        ev_cd = mm_group([(PA.ap()[0:64, 1600:1616], V("conesf1", 0, 64),
                           V("cdrow")[:, c::16], True, True, [3])], waits=[ev_cdx])
        if phase5:
            ev_rh = ld(V("rowhl", 0, NH * 128, parts=1), V("cumhi")[:, c * 128:(c + 1) * 128],
                       waits=[ev_chi, pv.get("ai"), pv.get("s2")], q="qs")
            ev_rl = ld(V("rowhl", 0, NH * 128, parts=1, p0=1), V("cumlo")[:, c * 128:(c + 1) * 128],
                       waits=[ev_clo, pv.get("ai")], q="qs")
            items = [(PB.ap()[:, ni * 512:(ni + 1) * 512], V("conespp", 0, 128, parts=2),
                      V("rowhl", ni * 512, 512), True, True,
                      pb_banks(ni * 512, 512)) for ni in range(4)]
            ev["ai"] = mm_group(items, waits=[ev_rh, ev_rl])
            ev_sc = mm_group([(PA.ap()[:, 1024:1152], V("xar", 16 * TF + ccol, 128, parts=64),
                               V("cfm", ccol, 128), True, True, [2])], waits=[ev_conv])
            ev_eb = veg("scalar", lambda e: e.activation(V("eb"), PB.ap()[0:64, 0:2048], AF.Exp),
                        rb=pb_banks(0, 2048), waits=[pv.get("ce"), pv.get("s2")])
            items = [(PB.ap()[:, ni * 512:(ni + 1) * 512], V("ci128b"),
                      V("ctmask", 0, 512), False, True, pb_banks(ni * 512, 512))
                     for ni in range(4)]
            ev_ai2 = mm_group(items, waits=[ev_tmk, ev_eb])
            es3 = h3("es", NH)
            ev_es = None
            for h in range(NH):
                ev_es = veg("scalar", lambda e, h=h: e.activation(
                    es3[:, h, :], PB.ap()[:, h * 128:(h + 1) * 128], AF.Exp,
                    bias=V("najptm", c * 16 + h, 1)),
                    rb=pb_banks(h * 128, 128),
                    waits=[ev_najptm] + ([pv.get("y"), pv.get("s2")] if h == 0 else []))
            ev["es"] = ev_es
            ev_att = veg("vector", lambda e: e.tensor_tensor(
                es3, es3, rep_o(PA.ap()[:, 1024:1152], NH), ALU.mult),
                rb=[2], waits=[ev["es"]])
            eb3 = V("eb").rearrange("p (h i) -> p h i", h=NH)
            ce3 = V("ce").rearrange("p (h i) -> p h i", h=NH)
            ev["ce"] = veg("vector", lambda e: e.tensor_tensor(
                ce3, rep_o(V("cfm", ccol, 128), NH), eb3, ALU.mult),
                waits=[ev_eb, ev_conv, pv.get("y")])
            ev_sbf = veg("scalar", lambda e: e.copy(V("Sbf"), V("S2")),
                         waits=[pv.get("s2"), pv.get("y")])
            ev["sbf"] = ev_sbf
        st_items = [(PB.ap()[0:64, h * 128:(h + 1) * 128], V("btm"),
                     xw3[:, h, :], True, True, pb_banks(h * 128, 128)) for h in range(NH)]
        ev["st"] = mm_group(st_items, waits=[ev_xw, ev_btm])
        s23 = V("S2").rearrange("p (h i) -> p h i", h=NH)
        ev_s2m = veg("vector", lambda e: e.tensor_tensor(
            s23, s23, PA.ap()[0:64, 1600:1616].unsqueeze(2).broadcast_to([64, 16, 128]), ALU.mult),
            rb=[3], waits=[pv.get("s2")] + ([ev["sbf"]] if phase5 else []))
        ev["s2"] = veg("vector", lambda e: e.tensor_tensor(
            V("S2"), V("S2"), PB.ap()[0:64, 0:2048], ALU.add),
            rb=pb_banks(0, 2048), waits=[ev_s2m])
        if phase5:
            y_items = []
            for h in range(NH):
                bks = pb_banks(h * 128, 128)
                y_items.append((PB.ap()[:, h * 128:(h + 1) * 128],
                                V("xtm", h * 128, 128), h3("es", NH)[:, h, :], True, False, bks))
                y_items.append((PB.ap()[:, h * 128:(h + 1) * 128],
                                V("Sbf", h * 128, 128, parts=64),
                                h3("ce", NH)[:, h, :], False, False, bks))
                y_items.append((PB.ap()[:, h * 128:(h + 1) * 128],
                                V("cdD", h * 128, 128), xar3[:, h, ccol:ccol + 128],
                                False, True, bks))
            ev["y"] = mm_group(y_items, waits=[ev["xtm"], ev_att, ev["ce"], ev_sbf, ev_conv,
                                               ev_cdDl])
        if phase5:
            g3 = h3("gbuf", NH)
            ev_g = veg("vector", lambda e: e.tensor_tensor(
                g3, PB.ap()[:, 0:2048].rearrange("p (h i) -> p h i", h=NH),
                zsz3[:, :, ccol:ccol + 128], ALU.mult),
                rb=pb_banks(0, 2048), waits=[ev["y"]])
            ev_g2 = veg("vector", lambda e: e.tensor_tensor(V("g2buf"), V("gbuf"), V("gbuf"), ALU.mult),
                        waits=[ev_g, pv.get("rr")])
            rr_items = [(PA.ap()[0:1, 1792:1920], V("conesbc"), V("g2buf", h * 128, 128),
                         h == 0, h == NH - 1, [3]) for h in range(NH)]
            ev["rr"] = mm_group(rr_items, waits=[ev_g2])
            ev_rl1 = veg("scalar", lambda e: e.activation(V("rrow"), PA.ap()[0:1, 1792:1920], AF.Ln,
                                                          bias=RMS_EPS, scale=1.0 / DI),
                         rb=[3], waits=[pv.get("rb")])
            ev_rl2 = veg("scalar", lambda e: e.activation(V("rrow"), V("rrow"), AF.Exp, scale=-0.5),
                         waits=[ev_rl1])
            ev["rb"] = mm_group([(PA.ap()[:, 1664:1792], V("conesf1"), V("rrow"), True, True, [3])],
                                waits=[ev_rl2])
            ev["sc"] = ev_rl2
            ev["gn"] = veg("vector", lambda e: e.tensor_tensor(
                xar3[:, 0:16, ccol:ccol + 128], g3,
                PA.ap()[:, 1664:1792].unsqueeze(1).broadcast_to([128, NH, 128]), ALU.mult),
                rb=[3], waits=[ev_g])
        return ev

    pv = {"s2": ev_s2z}
    for c in range(NCH):
        pv = chunk_scan(c, False, pv)
    ev_E = pv["s2"]
    ev_tap_E = tap("t_E", V("S2"), [ev_E])

    ev_esend = ld(e_in.ap(), V("S2"), waits=[ev_E], q="qs")

    def coll(e):
        return e.collective_compute(
            "AllGather", ALU.bypass,
            replica_groups=[[0, 1], [2, 3], [4, 5], [6, 7]],
            ins=[e_in.ap().opt()], outs=[e_out.ap().opt()])
    ev_cc = S.emit("gpsimd", coll, waits=[ev_esend], inc="cc")
    ev_erecv = ld(V("Erecv"), e_out.ap()[0:1, :], waits=[ev_cc], q="qs")
    ev_s2i = veg("vector", lambda e: e.tensor_scalar(
        V("S2"), V("Erecv"), V("cmask", parts=64), None, ALU.mult),
        waits=[ev_erecv, pv["s2"], ev_consts, ev_tap_E])

    pv = {"s2": ev_s2i, "st": pv["st"]}
    for c in range(NCH):
        pv = chunk_scan(c, True, pv)
    ev_gn_all = pv["gn"]
    p5_done = [ev_gn_all, pv["s2"], pv["sc"]]
    tap("t_sip3", V("sip"), p5_done)
    tap("t_gn", V("xar", 0, 16 * TF), [ev_gn_all])

    ev_r1u = ld(V("r1").rearrange("p (k t) -> p k t", k=KD),
                uT[:, HALO:HALO + T].rearrange("(k p) t -> p k t", p=128),
                waits=[ev_ubf] + p5_done, q="qs")
    attn3 = h3("attnbf", KD)
    ev_wop_fold = [None] * 8
    ev_wop_lastmm = [None] * 8
    ev_rmsw = ld(V("crmsw"), rmsw, waits=p5_done)

    ev_attn = None
    ev_unf = [None] * 8
    for o2 in range(8):
        slot = o2 % 2
        wfv = V("wopbf", slot * DI, DI)
        wt = ([ev_wop_lastmm[o2 - 2]] if o2 >= 2
              else p5_done + [ev_opn_nmB[7]])
        ev_unf[o2] = ld(wfv, wopb[o2], waits=wt, q="qg")
        wf3 = wfv.rearrange("p (k o) -> p k o", k=KI)
        ev_wop_fold[o2] = veg("vector", lambda e, wf3=wf3: e.tensor_tensor(
            wf3, wf3, rep_h(V("crmsw")), ALU.mult),
            waits=[ev_unf[o2], ev_rmsw])
        ps = PA.ap() if o2 % 2 == 0 else PB.ap()
        pbk = pa_banks if o2 % 2 == 0 else pb_banks
        items = []
        for k in range(KI):
            for ni in range(4):
                items.append((ps[:, ni * 512:(ni + 1) * 512],
                              V("wopbf", slot * DI + k * 128, 128),
                              xar3[:, k, HALO + ni * 512:HALO + (ni + 1) * 512],
                              k == 0, k == KI - 1, pbk(ni * 512, 512)))
        ev_wop_lastmm[o2] = mm_group(items, waits=[ev_wop_fold[o2], ev_gn_all])
        for ni in range(4):
            ev_attn = veg("vector", lambda e, o2=o2, ni=ni, ps=ps: e.tensor_scalar(
                attn3[:, o2, ni * 512:(ni + 1) * 512], ps[:, ni * 512:(ni + 1) * 512],
                V("sop", o2, 1), None, ALU.mult),
                rb=pbk(ni * 512, 512), waits=[ev_sop])
    tap("t_attn", V("attnbf"), [ev_attn])
    tap("t_wf7", V("wopbf", DI, DI), [ev_wop_fold[7], ev_wop_lastmm[7]])

    items = [(PA.ap()[:, ni * 512:(ni + 1) * 512], V("conespp"),
              attn3[:, k, ni * 512:(ni + 1) * 512], k == 0, k == KD - 1,
              pa_banks(ni * 512, 512)) for ni in range(4) for k in range(KD)]
    ev_sm6 = mm_group(items, waits=[ev_attn])
    ev_mu6 = veg("vector", lambda e: e.tensor_scalar(V("mu6"), PA.ap()[:, :], 1.0 / D, None, ALU.mult),
                 rb=pa_banks(0, 2048))
    sq6 = h3("sq6", KD)
    ev_prev_sq = None
    for ni in range(4):
        ev_sq = veg("vector", lambda e, ni=ni: e.tensor_tensor(
            sq6, attn3[:, :, ni * 512:(ni + 1) * 512],
            attn3[:, :, ni * 512:(ni + 1) * 512], ALU.mult),
            waits=[ev_attn, ev_prev_sq])
        items = [(PA.ap()[:, ni * 512:(ni + 1) * 512], V("conespp"),
                  sq6[:, k, :], k == 0, k == KD - 1, pa_banks(ni * 512, 512))
                 for k in range(KD)]
        ev_prev_sq = mm_group(items, waits=[ev_sq, ev_mu6])
    ev_ms6 = veg("vector", lambda e: e.tensor_scalar(V("al6"), PA.ap()[:, :], 1.0 / D, None, ALU.mult),
                 rb=pa_banks(0, 2048))
    ev_t6 = veg("vector", lambda e: e.tensor_tensor(V("tb6", 0, 2048), V("mu6"), V("mu6"), ALU.mult),
                waits=[ev_ms6])
    ev_v6 = veg("vector", lambda e: e.tensor_tensor(V("al6"), V("al6"), V("tb6", 0, 2048), ALU.subtract),
                waits=[ev_t6])
    ev_a6a = veg("scalar", lambda e: e.activation(V("al6"), V("al6"), AF.Ln, bias=LN_EPS), waits=[ev_v6])
    ev_a6 = veg("scalar", lambda e: e.activation(V("al6"), V("al6"), AF.Exp, scale=-0.5), waits=[ev_a6a])
    ev_mua6 = veg("vector", lambda e: e.tensor_tensor(V("mu6"), V("mu6"), V("al6"), ALU.mult),
                  waits=[ev_a6])
    r13 = V("r1").rearrange("p (k t) -> p k t", k=KD)
    ev_r1 = veg("vector", lambda e: e.tensor_tensor(r13, r13, rep_o(V("mu6"), KD), ALU.subtract),
                waits=[ev_mua6, ev_r1u])
    for k in range(KD):
        for ni in range(4):
            ev_tm = veg("vector", lambda e, k=k, ni=ni: e.tensor_tensor(
                V("tb6"), attn3[:, k, ni * 512:(ni + 1) * 512],
                V("al6", ni * 512, 512), ALU.mult), waits=[ev_r1])
            ev_r1 = veg("vector", lambda e, k=k, ni=ni: e.tensor_tensor(
                r13[:, k, ni * 512:(ni + 1) * 512], r13[:, k, ni * 512:(ni + 1) * 512],
                V("tb6"), ALU.add), waits=[ev_tm])
    tap("t_r1", V("r1"), [ev_r1])

    def r1h3(hf):
        return V("r1").rearrange("p (k t) -> p k t", k=KD)[:, :, hf * 1024:(hf + 1) * 1024]

    hv = V("xar", 0, KH * 1024).rearrange("p (o t) -> p o t", o=KH)
    xf3 = V("xar", KH * 1024, KD * 1024).rearrange("p (k t) -> p k t", k=KD)
    prev_gu = {}
    prev_dn = {}
    ev_hf_done = None
    ev_out_last = None
    for hf in range(2):
        ev_rbf = ld(V("r1bfh"), r1h3(hf), waits=[ev_r1, ev_hf_done], q="qg")
        rb3 = h3("r1bfh", KD)
        ev_sq7 = veg("vector", lambda e: e.tensor_tensor(V("sqstg"), V("r1bfh"), V("r1bfh"), ALU.mult),
                     waits=[ev_rbf, ev_hf_done])
        items = [(PA.ap()[:, ni * 512:(ni + 1) * 512], V("conespp"),
                  rb3[:, k, ni * 512:(ni + 1) * 512], k == 0, k == KD - 1,
                  pa_banks(ni * 512, 512)) for ni in range(2) for k in range(KD)]
        ev_sm7 = mm_group(items, waits=[ev_rbf])
        ev_mu7 = veg("vector", lambda e: e.tensor_scalar(V("mu7"), PA.ap()[:, 0:1024], 1.0 / D, None, ALU.mult),
                     rb=pa_banks(0, 1024))
        stg3 = h3("sqstg", KD)
        items = [(PA.ap()[:, ni * 512:(ni + 1) * 512], V("conespp"),
                  stg3[:, k, ni * 512:(ni + 1) * 512], k == 0, k == KD - 1,
                  pa_banks(ni * 512, 512)) for ni in range(2) for k in range(KD)]
        ev_sm7b = mm_group(items, waits=[ev_sq7, ev_mu7])
        ev_ms7 = veg("vector", lambda e: e.tensor_scalar(V("al7"), PA.ap()[:, 0:1024], 1.0 / D, None, ALU.mult),
                     rb=pa_banks(0, 1024))
        ev_t7 = veg("vector", lambda e: e.tensor_tensor(V("mu7"), V("mu7"), V("mu7"), ALU.mult) if False
                    else e.tensor_tensor(V("tb7"), V("mu7", 0, 512), V("mu7", 0, 512), ALU.mult),
                    waits=[ev_ms7])
        ev_v7 = veg("vector", lambda e: e.tensor_tensor(V("al7", 0, 512), V("al7", 0, 512), V("tb7"), ALU.subtract),
                    waits=[ev_t7])
        ev_t7b = veg("vector", lambda e: e.tensor_tensor(V("tb7"), V("mu7", 512, 512), V("mu7", 512, 512), ALU.mult),
                     waits=[ev_v7])
        ev_v7b = veg("vector", lambda e: e.tensor_tensor(V("al7", 512, 512), V("al7", 512, 512), V("tb7"), ALU.subtract),
                     waits=[ev_t7b])
        ev_a7a = veg("scalar", lambda e: e.activation(V("al7"), V("al7"), AF.Ln, bias=LN_EPS), waits=[ev_v7b])
        ev_a7 = veg("scalar", lambda e: e.activation(V("al7"), V("al7"), AF.Exp, scale=-0.5), waits=[ev_a7a])
        ev_mua7 = veg("vector", lambda e: e.tensor_tensor(V("mu7"), V("mu7"), V("al7"), ALU.mult),
                      waits=[ev_a7])
        ev_xf1 = veg("vector", lambda e: e.tensor_tensor(xf3, rb3, rep_o(V("mu7"), KD), ALU.subtract),
                     waits=[ev_mua7, prev_dn.get(7)])
        ev_xf = veg("vector", lambda e: e.tensor_tensor(xf3, xf3, rep_o(V("al7"), KD), ALU.mult),
                    waits=[ev_xf1])
        if hf == 0:
            tap("t_xf", V("xar", KH * 1024, KD * 1024), [ev_xf])
        ev_h_last = None
        ev_sg_prev = None
        for o in range(KH):
            slot = o % 2
            gu = V("wgubuf", slot * 2 * D, 2 * D)
            wt = [ev_r1] if (hf == 0 and o < 2) else [prev_gu.get((o - 2) % KH)]
            ev_gud = ld(gu.rearrange("p (g d) -> p g d", g=2),
                        wgu[o].rearrange("g p d -> p g d"), waits=wt)
            psg = PA.ap() if o % 2 == 0 else PB.ap()
            bksg = (pa_banks if o % 2 == 0 else pb_banks)(0, 1024)
            bksu = (pa_banks if o % 2 == 0 else pb_banks)(1024, 1024)
            items = []
            for k in range(KD):
                for nn in range(2):
                    items.append((psg[:, nn * 512:(nn + 1) * 512],
                                  gu[:, k * 128:(k + 1) * 128],
                                  xf3[:, k, nn * 512:(nn + 1) * 512],
                                  k == 0, k == KD - 1, bksg))
            for k in range(KD):
                for nn in range(2):
                    items.append((psg[:, 1024 + nn * 512:1024 + (nn + 1) * 512],
                                  gu[:, D + k * 128:D + (k + 1) * 128],
                                  xf3[:, k, nn * 512:(nn + 1) * 512],
                                  k == 0, k == KD - 1, bksu))
            prev_gu[o] = mm_group(items, waits=[ev_gud, ev_xf])
            ev_sg = veg("scalar", lambda e, psg=psg: e.activation(
                V("sgbuf"), psg[:, 0:1024], AF.Silu, scale=QSCALE),
                rb=bksg, waits=[ev_h_last])
            ev_h_last = veg("vector", lambda e, psg=psg, o=o: e.scalar_tensor_tensor(
                hv[:, o, :], psg[:, 1024:2048], QSCALE, V("sgbuf"), ALU.mult, ALU.mult),
                rb=bksu, waits=[ev_sg])
        ev_stg_last = None
        for o2 in range(8):
            slot = o2 % 2
            wd = V("wdbuf", slot * KH * 128, KH * 128)
            wt = [prev_dn.get((o2 - 2) % 8)] if (hf > 0 or o2 >= 2) else [ev_h_last]
            ev_wdd = ld(wd, wdb[o2], waits=wt)
            ps = PA.ap() if o2 % 2 == 0 else PB.ap()
            bks = (pa_banks if o2 % 2 == 0 else pb_banks)(0, 1024)
            items = []
            for k in range(KH):
                for ni in range(2):
                    items.append((ps[:, ni * 512:(ni + 1) * 512], wd[:, k * 128:(k + 1) * 128],
                                  hv[:, k, ni * 512:(ni + 1) * 512], k == 0, k == KH - 1, bks))
            prev_dn[o2] = mm_group(items, waits=[ev_wdd, ev_h_last])
            ev_stg_last = veg("vector", lambda e, o2=o2, ps=ps: e.tensor_scalar(
                stg3[:, o2, :], ps[:, 0:1024], QSCALE, None, ALU.mult),
                rb=bks, waits=[ev_sm7b])
        if hf == 0:
            tap("t_h", V("xar", 0, KH * 1024), [ev_h_last])
            tap("t_stg", V("sqstg"), [ev_stg_last])
        items = [(PA.ap()[:, ni * 512:(ni + 1) * 512], V("conespp"),
                  stg3[:, k, ni * 512:(ni + 1) * 512], k == 0, k == KD - 1,
                  pa_banks(ni * 512, 512)) for ni in range(2) for k in range(KD)]
        ev_sm7c = mm_group(items, waits=[ev_stg_last])
        ev_mu7b = veg("vector", lambda e: e.tensor_scalar(V("mu7"), PA.ap()[:, 0:1024], 1.0 / D, None, ALU.mult),
                      rb=pa_banks(0, 1024))
        ev_sqp = veg("vector", lambda e: e.tensor_tensor(V("r1bfh"), V("sqstg"), V("sqstg"), ALU.mult),
                     waits=[ev_stg_last, ev_sm7b])
        sqp3 = h3("r1bfh", KD)
        items = [(PA.ap()[:, ni * 512:(ni + 1) * 512], V("conespp"),
                  sqp3[:, k, ni * 512:(ni + 1) * 512], k == 0, k == KD - 1,
                  pa_banks(ni * 512, 512)) for ni in range(2) for k in range(KD)]
        ev_sm7d = mm_group(items, waits=[ev_sqp, ev_mu7b])
        ev_ms7b = veg("vector", lambda e: e.tensor_scalar(V("al7"), PA.ap()[:, 0:1024], 1.0 / D, None, ALU.mult),
                      rb=pa_banks(0, 1024))
        ev_t7c = veg("vector", lambda e: e.tensor_tensor(V("tb7"), V("mu7", 0, 512), V("mu7", 0, 512), ALU.mult),
                     waits=[ev_ms7b])
        ev_v7c = veg("vector", lambda e: e.tensor_tensor(V("al7", 0, 512), V("al7", 0, 512), V("tb7"), ALU.subtract),
                     waits=[ev_t7c])
        ev_t7d = veg("vector", lambda e: e.tensor_tensor(V("tb7"), V("mu7", 512, 512), V("mu7", 512, 512), ALU.mult),
                     waits=[ev_v7c])
        ev_v7d = veg("vector", lambda e: e.tensor_tensor(V("al7", 512, 512), V("al7", 512, 512), V("tb7"), ALU.subtract),
                     waits=[ev_t7d])
        ev_a7c = veg("scalar", lambda e: e.activation(V("al7"), V("al7"), AF.Ln, bias=LN_EPS), waits=[ev_v7d])
        ev_a7d = veg("scalar", lambda e: e.activation(V("al7"), V("al7"), AF.Exp, scale=-0.5), waits=[ev_a7c])
        ev_mua7b = veg("vector", lambda e: e.tensor_tensor(V("mu7"), V("mu7"), V("al7"), ALU.mult),
                       waits=[ev_a7d])
        rh = r1h3(hf)
        ev_fin = veg("vector", lambda e, rh=rh: e.tensor_tensor(rh, rh, rep_o(V("mu7"), KD), ALU.subtract),
                     waits=[ev_mua7b])
        for k in range(KD):
            for ni in range(2):
                ev_tm = veg("vector", lambda e, k=k, ni=ni, stg3=stg3: e.tensor_tensor(
                    V("tb7"), stg3[:, k, ni * 512:(ni + 1) * 512],
                    V("al7", ni * 512, 512), ALU.mult), waits=[ev_fin])
                ev_fin = veg("vector", lambda e, k=k, ni=ni, rh=rh: e.tensor_tensor(
                    rh[:, k, ni * 512:(ni + 1) * 512], rh[:, k, ni * 512:(ni + 1) * 512],
                    V("tb7"), ALU.add), waits=[ev_tm])
        ev_hf_done = ev_fin
        ev_out_last = ld(out[:, hf * 1024:(hf + 1) * 1024].rearrange("(k p) t -> p k t", p=128),
                         r1h3(hf), waits=[ev_fin], q="qs")

    S.emit("sync", lambda e: None, waits=[("qs", S.cnt["qs"]), ("qg", S.cnt["qg"])])

    if trunc is not None:
        S.truncate(trunc)

    with (
        nc.semaphore("sem_t") as s_t,
        nc.semaphore("sem_v") as s_v,
        nc.semaphore("sem_s") as s_s,
        nc.semaphore("sem_g") as s_g,
        nc.semaphore("sem_qg") as s_qg,
        nc.semaphore("sem_qs") as s_qs,
        nc.semaphore("sem_cc") as s_cc,
        nc.Block() as block,
    ):
        S.handles.update(t=s_t, v=s_v, s=s_s, g=s_g, qg=s_qg, qs=s_qs, cc=s_cc)

        @block.tensor
        def _(e):
            S.run("tensor", e)

        @block.vector
        def _(e):
            S.run("vector", e)

        @block.scalar
        def _(e):
            S.run("scalar", e)

        @block.gpsimd
        def _(e):
            S.run("gpsimd", e)

        @block.sync
        def _(e):
            S.run("sync", e)

    return nc


def prep_inputs(L0_attn_u, L0_attn_in_proj_w, L0_attn_conv_w, L0_attn_conv_b,
                L0_attn_dt_bias, L0_attn_A_log, L0_attn_D, L0_attn_x_norm_w,
                L0_attn_x_norm_b, L0_attn_rms_w, L0_attn_out_proj_w,
                L0_ffn_w_gate, L0_ffn_w_up, L0_ffn_w_down):
    u = np.ascontiguousarray(np.asarray(L0_attn_u, dtype=np.float32))
    ipw = np.asarray(L0_attn_in_proj_w, dtype=np.float32)
    opw = np.asarray(L0_attn_out_proj_w, dtype=np.float32)
    cw = np.asarray(L0_attn_conv_w, dtype=np.float32)
    cb = np.asarray(L0_attn_conv_b, dtype=np.float32)
    wg = np.asarray(L0_ffn_w_gate, dtype=np.int32)
    wu = np.asarray(L0_ffn_w_up, dtype=np.int32)
    wd = np.asarray(L0_ffn_w_down, dtype=np.int32)

    blocks = [ipw[t * 128:(t + 1) * 128] for t in range(33)]
    dtblk = np.zeros((128, D), np.float32)
    dtblk[:16] = ipw[4224:4240]
    blocks.append(dtblk)
    cdup = np.zeros((128, D), np.float32)
    cdup[:64] = ipw[4160:4224]
    blocks.append(cdup)
    wipb = np.stack(blocks)
    wipd = np.ascontiguousarray(
        wipb.reshape(NOT, 128, KD, 128).transpose(0, 3, 2, 1)).reshape(NOT, 128, D)

    wopd = np.ascontiguousarray(
        opw.reshape(8, 128, KI, 128).transpose(0, 3, 2, 1)).reshape(8, 128, DI)
    wgb = wg.reshape(KH, 128, KD, 128).transpose(0, 3, 2, 1).reshape(KH, 128, D)
    wub = wu.reshape(KH, 128, KD, 128).transpose(0, 3, 2, 1).reshape(KH, 128, D)
    wgud = np.ascontiguousarray(np.stack([wgb, wub], axis=1))
    wdd = np.ascontiguousarray(
        wd.reshape(8, 128, KH, 128).transpose(0, 3, 2, 1)).reshape(8, 128, HID)

    convdd = np.zeros((128, 4, 18, 128), np.float32)
    idx = np.arange(128)
    for k in range(4):
        for t in range(17):
            convdd[idx, k, t, idx] = cw[t * 128 + idx, k]
        q = np.arange(64)
        convdd[64 + q, k, 17, q] = cw[2112 + q, k]
    convdd = convdd.reshape(128, 4 * 18 * 128)
    convbd = np.zeros((128, 18), np.float32)
    for t in range(17):
        convbd[:, t] = cb[t * 128:(t + 1) * 128]
    convbd[:64, 17] = cb[2112:2176]

    xnw_ = np.ascontiguousarray(np.asarray(L0_attn_x_norm_w, np.float32).reshape(KD, 128).T)
    xnb_ = np.ascontiguousarray(np.asarray(L0_attn_x_norm_b, np.float32).reshape(KD, 128).T)
    rmsw_ = np.ascontiguousarray(np.asarray(L0_attn_rms_w, np.float32).reshape(KI, 128).T)
    dtb_ = np.ascontiguousarray(np.asarray(L0_attn_dt_bias, np.float32).reshape(16, 1))
    alog_ = np.ascontiguousarray(np.asarray(L0_attn_A_log, np.float32).reshape(16, 1))
    Dv = np.asarray(L0_attn_D, np.float32)
    dDd = np.zeros((128, NH * 128), np.float32)
    for h in range(NH):
        dDd[idx, h * 128 + idx] = Dv[h]

    shared = dict(wip=wipd, wopb=wopd, wgu=wgud, wdb=wdd, convd=convdd,
                  convbi=convbd, xnw=xnw_, xnb=xnb_, rmsw=rmsw_, dtbi=dtb_,
                  alogi=alog_, dDt=dDd)
    in_maps = []
    for c in range(8):
        s, hf = c // 2, c % 2
        t0 = hf * T
        uTc = np.zeros((D, TF), np.float32)
        uTc[:, HALO:] = u[s, t0:t0 + T].T
        if hf == 1:
            uTc[:, :HALO] = u[s, t0 - HALO:t0].T
        im = dict(shared)
        im["uT"] = np.ascontiguousarray(uTc)
        im["maskt"] = np.full((128, 1), 1.0 if hf == 1 else 0.0, np.float32)
        in_maps.append(im)
    return in_maps


def kernel(**inputs):
    in_maps = prep_inputs(**inputs)
    nc = build_program(debug=False)
    res = run_bass_kernel_spmd(nc, in_maps, core_ids=list(range(8)))
    outp = np.empty((4, 4096, D), np.float32)
    for c in range(8):
        s, hf = c // 2, c % 2
        outp[s, hf * T:(hf + 1) * T, :] = res.results[c]["out"].T
    return outp


# version 10
# speedup vs baseline: 1.0129x; 1.0129x over previous
import numpy as np
import ml_dtypes

import concourse.bass as bass
import concourse.mybir as mybir
from concourse.bass_utils import run_bass_kernel_spmd

F32 = mybir.dt.float32
BF16 = mybir.dt.bfloat16
I32 = mybir.dt.int32
AF = mybir.ActivationFunctionType
ALU = mybir.AluOpType
BF_NP = ml_dtypes.bfloat16

D = 1024
DI = 2048
NH = 16
NST = 64
CONVD = 2176
HID = 3072
T = 2048
HALO = 3
TF = T + HALO
CH = 128
NCH = T // CH
KD = D // 128
KI = DI // 128
KX = CONVD // 128
NOT = 35
KH = HID // 128
LN_EPS = 1e-5
RMS_EPS = 1e-5
DIMSCALE = float(D) ** -0.5
QSCALE = 1.0 / 127.0
ENGS = ("tensor", "vector", "scalar", "gpsimd", "sync")

NBLK = [(0, 512), (512, 512), (1024, 512), (1536, 512), (2048, 3)]
NBLKG = [(0, 3), (3, 512), (515, 512), (1027, 512), (1539, 512)]
CBLK = [(HALO + 1536, 512), (HALO + 1024, 512), (HALO + 512, 512), (HALO, 512)]


SEM_OWNER = {"t": "tensor", "v": "vector", "s": "scalar", "g": "gpsimd",
             "qg": "gpsimd", "qs": "sync", "cc": "gpsimd"}


class Sched:
    def __init__(self):
        self.ops = {e: [] for e in ENGS}
        self.cnt = {s: 0 for s in ("t", "v", "s", "g", "qg", "qs", "cc")}
        self.handles = {}
        self.lastw = {e: {} for e in ENGS}

    def truncate(self, keep):
        for e in ENGS:
            self.ops[e] = [op for op in self.ops[e] if op[4] < keep]

    def emit(self, eng, fn, waits=(), inc=None, inc_by=1):
        wl = []
        for w in waits:
            if w is None:
                continue
            s, v = w
            assert v <= self.cnt[s], f"deadlock: {eng} waits {s}>={v}, scheduled {self.cnt[s]}"
            if SEM_OWNER[s] == eng:
                if s in ("t", "v", "s", "g"):
                    continue
                v = self.cnt[s]
            if self.lastw[eng].get(s, -1) < v:
                self.lastw[eng][s] = v
                wl.append((s, v))
        ev = None
        if inc is not None:
            self.cnt[inc] += inc_by
            ev = (inc, self.cnt[inc])
        self.ops[eng].append((fn, tuple(wl), inc, inc_by, sum(len(v) for v in self.ops.values())))
        return ev

    def run(self, eng_name, eng):
        for fn, wl, inc, inc_by, _gi in self.ops[eng_name]:
            for s, v in wl:
                eng.wait_ge(self.handles[s], v)
            ins = fn(eng)
            if inc is not None:
                ins.then_inc(self.handles[inc], inc_by)


class Banks:
    def __init__(self):
        self.last_w = [None] * 8
        self.readers = [[] for _ in range(8)]

    def w_waits(self, banks):
        w = []
        for b in banks:
            w += [ev for (_, ev) in self.readers[b]]
            if self.last_w[b] is not None:
                w.append(self.last_w[b])
        return w

    def wrote(self, banks, ev):
        for b in banks:
            self.last_w[b] = ev
            self.readers[b] = []

    def r_waits(self, banks, eng=None):
        w = [self.last_w[b] for b in banks if self.last_w[b] is not None]
        if eng is not None:
            for b in banks:
                w += [ev for (re, ev) in self.readers[b] if re != eng]
        return w

    def read(self, banks, ev, eng=None):
        for b in banks:
            self.readers[b].append((eng, ev))


def pa_banks(lo, n):
    return list(range(lo // 512, (lo + n - 1) // 512 + 1))


def pb_banks(lo, n):
    return [4 + b for b in pa_banks(lo, n)]


class Arena:
    def __init__(self, nc, total_f32):
        self.t = nc.alloc_sbuf_tensor("ARENA", [128, total_f32], F32)
        self.total = total_f32
        self.allocs = []

    def reserve(self, name, f32_elems, p0, p1, align=16):
        f32_elems = -(-f32_elems // align) * align
        self.allocs.append([name, None, f32_elems, p0, p1])

    def _try(self, order):
        placed = []
        offs = {}
        hi = 0
        for name, _, sz, p0, p1 in order:
            spans = sorted((o, o + s) for (o, s, b0, b1) in placed
                           if not (b1 < p0 or b0 > p1))
            off = 0
            for s, e in spans:
                if off + sz <= s:
                    break
                off = max(off, e)
            placed.append((off, sz, p0, p1))
            offs[name] = off
            hi = max(hi, off + sz)
        return offs, hi

    def finalize(self):
        best = None
        import random
        keys = [lambda a: (-(a[4] - a[3]), -a[2]),
                lambda a: (-a[2], -(a[4] - a[3])),
                lambda a: (a[3], -a[2]),
                lambda a: (a[3], a[4], -a[2]),
                lambda a: (-a[3], -a[2]),
                lambda a: (-a[4], -a[2]),
                lambda a: (-a[2], a[3])]
        rng = random.Random(0)
        for t in range(40):
            sh = list(self.allocs)
            rng.shuffle(sh)
            offs, hi = self._try(sh)
            if best is None or hi < best[1]:
                best = (offs, hi)
        for key in keys:
            offs, hi = self._try(sorted(self.allocs, key=key))
            if best is None or hi < best[1]:
                best = (offs, hi)
        assert best[1] <= self.total, \
            f"SBUF arena overflow: best packing {best[1]} > {self.total}"
        return best[0]


def build_program(debug=False, trunc=None):
    nc = bass.Bass("TRN2", target_bir_lowering=False, detect_race_conditions=False)

    uT = nc.dram_tensor("uT", [D, TF], F32, kind="ExternalInput").ap()
    wip = nc.dram_tensor("wip", [NOT, 128, D], F32, kind="ExternalInput").ap()
    wopb = nc.dram_tensor("wopb", [8, 128, DI], F32, kind="ExternalInput").ap()
    wgu = nc.dram_tensor("wgu", [KH, 2, 128, D], I32, kind="ExternalInput").ap()
    wdb = nc.dram_tensor("wdb", [8, 128, HID], I32, kind="ExternalInput").ap()
    convd = nc.dram_tensor("convd", [128, 4 * 18 * 128], F32, kind="ExternalInput").ap()
    convbi = nc.dram_tensor("convbi", [128, 18], F32, kind="ExternalInput").ap()
    xnw = nc.dram_tensor("xnw", [128, KD], F32, kind="ExternalInput").ap()
    xnb = nc.dram_tensor("xnb", [128, KD], F32, kind="ExternalInput").ap()
    rmsw = nc.dram_tensor("rmsw", [128, KI], F32, kind="ExternalInput").ap()
    dtbi = nc.dram_tensor("dtbi", [16, 1], F32, kind="ExternalInput").ap()
    alogi = nc.dram_tensor("alogi", [16, 1], F32, kind="ExternalInput").ap()
    dDt = nc.dram_tensor("dDt", [128, NH * 128], F32, kind="ExternalInput").ap()
    maskt = nc.dram_tensor("maskt", [128, 1], F32, kind="ExternalInput").ap()
    out = nc.dram_tensor("out", [D, T], F32, kind="ExternalOutput").ap()

    e_in = nc.dram_tensor("e_in", [1, NST * DI], F32, kind="Internal")
    e_out = nc.dram_tensor("e_out", [2, NST * DI], F32, kind="Internal")

    taps = {}
    if debug:
        for name, shp in [("t_xf", [128, KD * 1024]), ("t_h", [128, KH * 1024]),
                          ("t_stg", [128, KD * 1024]),
                          ("t_sip1", [128, NOT]), ("t_sip2", [128, NOT]),
                          ("t_sip3", [128, NOT]), ("t_negA", [16, 1]),
                          ("t_dabf", [16, T]), ("t_sop", [128, 8]),
                          ("t_wf7", [128, DI]),
                          ("t_u2", [128, KD * TF]), ("t_z", [128, KI * TF]),
                          ("t_xbc", [128, KX * TF]), ("t_x", [128, KX * TF]),
                          ("t_sz", [128, KI * TF]), ("t_c", [64, TF]),
                          ("t_dt", [16, TF]), ("t_cum", [16, T]),
                          ("t_E", [64, NH * 128]), ("t_gn", [128, 16 * TF]),
                          ("t_attn", [128, KD * T]), ("t_r1", [128, KD * T])]:
            taps[name] = nc.dram_tensor(name, shp, F32, kind="ExternalOutput").ap()

    trin = np.tril(np.ones((128, 128), np.float32)).T.copy()
    TRIb = nc.inline_tensor(trin.astype(BF_NP), "TRIb").ap()
    tm_ = np.zeros((128, 512), np.float32)
    for h in range(4):
        tm_[:, h * 128:(h + 1) * 128] = -1000.0 * np.tril(np.ones((128, 128)), -1)
    TMASKb = nc.inline_tensor(tm_.astype(BF_NP), "TMASKb").ap()
    eye = np.eye(128, dtype=np.float32)
    I128b = nc.inline_tensor(eye.astype(BF_NP), "I128b").ap()
    I128f = nc.inline_tensor(eye.copy(), "I128f").ap()
    ONESb1 = nc.inline_tensor(np.ones((1, 128), BF_NP), "ONESb1").ap()
    ONESf1 = nc.inline_tensor(np.ones((1, 128), np.float32), "ONESf1").ap()
    ONESbc = nc.inline_tensor(np.ones((128, 1), BF_NP), "ONESbc").ap()
    ONESpp = nc.inline_tensor(np.ones((128, 128), BF_NP), "ONESpp").ap()
    ONESf11 = nc.inline_tensor(np.ones((1, 1), np.float32), "ONESf11").ap()
    CVALS = [0.0, 1.0, LN_EPS * LN_EPS, 1.0 + LN_EPS, -0.5, 0.5, QSCALE,
             LN_EPS, RMS_EPS, 1.0 / DI]
    CVN = nc.inline_tensor(np.tile(np.array(CVALS, np.float32), (128, 1)), "CVN").ap()

    A = Arena(nc, 52160)
    P = {}

    def AL(name, parts, n, dt_, p0, p1):
        f32e = n if dt_ == F32 else (n + 1) // 2
        A.reserve(name, f32e, p0, p1)
        P[name] = (None, parts, n, dt_)

    def V(name, lo=0, n=None, parts=None, p0=0):
        off, pts, tot, dt_ = P[name]
        if n is None:
            n = tot - lo
        if parts is None:
            parts = pts
        base = A.t.ap() if dt_ == F32 else A.t.ap().bitcast(BF16)
        o = off if dt_ == F32 else 2 * off
        return base[p0:p0 + parts, o + lo:o + lo + n]

    AL("cvals", 128, 16, F32, 0, 79)
    AL("ubf", 128, KD * TF, BF16, 0, 9)
    AL("u2", 128, KD * TF, BF16, 0, 19)
    AL("zsz", 128, KI * TF, BF16, 10, 59)
    AL("xar", 128, KX * TF, BF16, 10, 79)
    AL("cfm", 64, TF, BF16, 10, 59)
    AL("r1", 128, KD * T, F32, 60, 79)
    for nm, pts, n, dt_, pp0, pp1 in [
            ("ctri", 128, 128, BF16, 30, 59), ("ci128b", 128, 128, BF16, 0, 79),
            ("ci128f", 128, 128, F32, 0, 79), ("conesb1", 1, 128, BF16, 0, 79),
            ("conesf1", 1, 128, F32, 0, 79), ("conesbc", 128, 1, BF16, 0, 79),
            ("conespp", 128, 128, BF16, 0, 79), ("conesf11", 1, 1, F32, 0, 79),
            ("cdD", 128, NH * 128, BF16, 30, 59), ("cconvb", 128, 18, F32, 20, 29),
            ("cxnw", 128, KD, F32, 0, 19), ("cxnb", 128, KD, F32, 0, 19),
            ("crmsw", 128, KI, F32, 60, 69), ("cdtb", 16, 1, F32, 0, 19),
            ("calog", 16, 1, F32, 0, 39), ("cmask", 128, 1, F32, 0, 49)]:
        AL(nm, pts, n, dt_, pp0, pp1)
    AL("negA", 16, 1, F32, 10, 39)
    AL("vdummy", 128, 1, F32, 0, 79)
    AL("p0a", 128, TF, F32, 0, 9)
    AL("p0b", 128, TF, F32, 0, 9)
    AL("p0c", 128, TF, F32, 0, 9)
    AL("wipbuf", 128, 2 * D, BF16, 10, 19)
    AL("wsq", 128, D, BF16, 10, 19)
    AL("wopnb", 128, DI, BF16, 10, 29)
    AL("snrow", 1, 128, F32, 10, 69)
    AL("sip", 128, NOT, F32, 10, 19)
    AL("convdg", 128, 4 * 18 * 128, BF16, 20, 29)
    AL("dtfm", 16, TF, F32, 10, 39)
    AL("cumfm", 16, T, F32, 30, 39)
    AL("lntmp", 16, T, F32, 30, 33)
    AL("dabf", 16, T, BF16, 30, 39)
    AL("datm", 128, NCH * 16, BF16, 30, 39)
    AL("cumhi", 16, T, BF16, 30, 59)
    AL("najptm", 128, NCH * 16, F32, 30, 59)
    AL("cumlo", 16, T, BF16, 30, 59)
    AL("najpfm", 16, T, F32, 30, 36)
    AL("ctmask", 128, 512, BF16, 30, 59)
    AL("wfm", 16, T, F32, 34, 39)
    AL("wtm", 128, NCH * 16, BF16, 30, 59)
    AL("cdrow", 1, NH * NCH, F32, 30, 59)
    AL("S2", 64, NH * 128, F32, 30, 59)
    AL("Sbf", 64, NH * 128, BF16, 50, 59)
    AL("Erecv", 64, NH * 128, F32, 40, 50)
    AL("xw", 128, NH * 128, BF16, 30, 59)
    AL("btm", 128, 64, BF16, 30, 59)
    AL("rowhl", 2, NH * 128, BF16, 51, 59)
    AL("xtm", 128, NH * 128, BF16, 51, 59)
    AL("es", 128, NH * 128, BF16, 51, 59)
    AL("eb", 64, NH * 128, BF16, 51, 59)
    AL("ce", 64, NH * 128, BF16, 51, 59)
    AL("gbuf", 128, NH * 128, BF16, 51, 59)
    AL("rrow", 1, 128, F32, 51, 59)
    AL("attnbf", 128, KD * T, BF16, 60, 69)
    AL("sq6", 128, KD * 512, BF16, 62, 65)
    AL("al6", 128, T, F32, 60, 69)
    AL("mu6", 128, T, F32, 60, 69)
    AL("tb6", 128, 512, F32, 66, 69)
    AL("wopbf", 128, 2 * DI, BF16, 60, 69)
    AL("sop", 128, 8, F32, 10, 69)
    AL("r1bfh", 128, KD * 1024, BF16, 70, 79)
    AL("sqstg", 128, KD * 1024, BF16, 70, 79)
    AL("al7", 128, 1024, F32, 70, 79)
    AL("mu7", 128, 1024, F32, 70, 79)
    AL("tb7", 128, 512, F32, 70, 79)
    AL("wgubuf", 128, 2 * 2 * D, BF16, 70, 79)
    AL("wdbuf", 128, 2 * KH * 128, BF16, 70, 79)
    AL("sgbuf", 128, 1024, BF16, 70, 79)
    AL("wgux", 128, 2 * D, BF16, 70, 79)

    _offs = A.finalize()
    for _nm in list(P):
        P[_nm] = (_offs[_nm],) + P[_nm][1:]
    P["g2buf"] = (P["xtm"][0], 128, NH * 128, BF16)
    for i, v in enumerate(CVALS):
        nc.const_aps.register(F32, float(v), V("cvals", i, 1))

    PA = nc.alloc_psum_tensor("PA", [128, 2048], F32)
    PB = nc.alloc_psum_tensor("PB", [128, 2048], F32)
    PAb = PA.ap().bitcast(BF16)

    S = Sched()
    bk = Banks()

    def ld(dst, src, waits=(), q="qg", slow=False):
        eng = "gpsimd" if q == "qg" else "sync"

        def f(e):
            return e.dma_start(out=dst, in_=src, allow_slow_non_contiguous=slow)
        return S.emit(eng, f, waits=waits, inc=q, inc_by=16)

    def mm_group(items, waits=()):
        banks = sorted({b for it in items for b in it[5]})
        w = list(waits) + bk.w_waits(banks)
        ev = None
        n = len(items)
        for i, (o, l, r, st, sp, _) in enumerate(items):
            def f(e, o=o, l=l, r=r, st=st, sp=sp):
                return e.matmul(o, l, r, start=st, stop=sp)
            rv = S.emit("tensor", f, waits=(w if i == 0 else ()),
                        inc=("t" if i == n - 1 else None))
            if rv:
                ev = rv
        bk.wrote(banks, ev)
        return ev

    def tr_group(items, waits=()):
        banks = sorted({b for it in items for b in it[3]})
        w = list(waits) + bk.w_waits(banks)
        ev = None
        n = len(items)
        for i, (o, src, idn, _) in enumerate(items):
            def f(e, o=o, src=src, idn=idn):
                return e.transpose(o, src, idn)
            rv = S.emit("tensor", f, waits=(w if i == 0 else ()),
                        inc=("t" if i == n - 1 else None))
            if rv:
                ev = rv
        bk.wrote(banks, ev)
        return ev

    def veg(eng, fn, rb=(), wb=(), waits=()):
        sem = {"vector": "v", "scalar": "s", "gpsimd": "g"}[eng]
        w = list(waits) + bk.r_waits(list(rb), eng) + bk.w_waits(list(wb))
        ev = S.emit(eng, fn, waits=w, inc=sem)
        if rb:
            bk.read(list(rb), ev, eng)
        if wb:
            bk.wrote(list(wb), ev)
        return ev

    def tap(name, src, waits):
        if debug:
            return ld(taps[name], src, waits=waits, q="qg")
        return None

    def rep_h(col_ap, reps=128):
        return col_ap.unsqueeze(2).broadcast_to([col_ap.shape[0], col_ap.shape[1], reps])

    def rep_o(t2d, k):
        return t2d.unsqueeze(1).broadcast_to([t2d.shape[0], k, t2d.shape[1]])

    def h3(name, k):
        return V(name).rearrange("p (k t) -> p k t", k=k)

    const_evs = []
    const_evs.append(ld(V("cvals", 0, 10), CVN))
    for dst, srcd in [(V("ci128b"), I128b), (V("ci128f"), I128f),
                      (V("conesb1"), ONESb1), (V("conesf1"), ONESf1),
                      (V("conesbc"), ONESbc), (V("conespp"), ONESpp),
                      (V("conesf11"), ONESf11),
                      (V("cxnw"), xnw), (V("cxnb"), xnb),
                      (V("cdtb"), dtbi), (V("calog"), alogi),
                      (V("cmask"), maskt)]:
        const_evs.append(ld(dst, srcd))
    ev_consts = const_evs[-1]
    ev_ubf = ld(V("ubf").rearrange("p (k t) -> p k t", k=KD),
                uT.rearrange("(k p) t -> p k t", p=128))

    ubf3 = h3("ubf", KD)
    u23 = h3("u2", KD)
    ev_sq0 = veg("vector", lambda e: e.tensor_tensor(V("u2"), V("ubf"), V("ubf"), ALU.mult),
                 waits=[ev_ubf])

    def stats_mms(src3, wt, nparts=KD):
        items = []
        for (lo, n) in NBLK:
            if lo < 2048:
                tgt, bks = PA.ap()[:, lo:lo + n], pa_banks(lo, n)
            else:
                tgt, bks = PB.ap()[:, 0:n], pb_banks(0, n)
            for k in range(nparts):
                items.append((tgt, V("conespp"), src3[:, k, lo:lo + n],
                              k == 0, k == nparts - 1, bks))
        return mm_group(items, waits=wt)

    ev_s0 = stats_mms(ubf3, [ev_ubf])
    ev_mu0a = veg("vector", lambda e: e.tensor_scalar(V("p0b", 0, 2048), PA.ap()[:, :], 1.0 / D, None, ALU.mult),
                  rb=pa_banks(0, 2048))
    ev_mu0b = veg("vector", lambda e: e.tensor_scalar(V("p0b", 2048, 3), PB.ap()[:, 0:3], 1.0 / D, None, ALU.mult),
                  rb=pb_banks(0, 3))
    ev_s1 = stats_mms(u23, [ev_sq0, ev_mu0a, ev_mu0b])
    ev_ms0a = veg("vector", lambda e: e.tensor_scalar(V("p0a", 0, 2048), PA.ap()[:, :], 1.0 / D, None, ALU.mult),
                  rb=pa_banks(0, 2048))
    ev_ms0b = veg("vector", lambda e: e.tensor_scalar(V("p0a", 2048, 3), PB.ap()[:, 0:3], 1.0 / D, None, ALU.mult),
                  rb=pb_banks(0, 3))
    ev_c0 = veg("vector", lambda e: e.tensor_tensor(V("p0c"), V("p0b"), V("p0b"), ALU.mult),
                waits=[ev_mu0a, ev_mu0b])
    ev_v1 = veg("vector", lambda e: e.tensor_tensor(V("p0a"), V("p0a"), V("p0c"), ALU.subtract),
                waits=[ev_c0, ev_ms0a, ev_ms0b])
    ev_l0 = veg("scalar", lambda e: e.activation(V("p0c"), V("p0a"), AF.Ln,
                                                 bias=LN_EPS * LN_EPS, scale=1.0 + LN_EPS),
                waits=[ev_v1])
    ev_a0 = veg("scalar", lambda e: e.activation(V("p0a"), V("p0c"), AF.Exp, scale=-0.5),
                waits=[ev_l0])
    ev_u2a = veg("vector", lambda e: e.tensor_tensor(u23, ubf3, rep_o(V("p0b"), KD), ALU.subtract),
                 waits=[ev_s1, ev_ubf])
    ev_u2b = veg("vector", lambda e: e.tensor_tensor(u23, u23, rep_o(V("p0a"), KD), ALU.mult),
                 waits=[ev_u2a, ev_a0])
    ev_u2done = None
    for k in range(KD):
        ev_u2done = veg("vector", lambda e, k=k: e.tensor_scalar(
            u23[:, k, :], u23[:, k, :], V("cxnw", k, 1), V("cxnb", k, 1),
            ALU.mult, ALU.add),
            waits=[ev_u2b, ev_consts])
    tap("t_u2", V("u2"), [ev_u2done])

    zsz3 = h3("zsz", KI)
    xar3 = h3("xar", KX)
    ev_wip_dma = [None] * NOT
    ev_wip_lastmm = [None] * NOT
    ev_wsq_ev = [None] * NOT
    ev_nm_ev = [None] * NOT
    ev_sip_parts = []
    ev_convdg = None

    for o in range(NOT):
        slot = o % 2
        wbv = V("wipbuf", slot * D, D)
        wt = [ev_consts] if o < 2 else [ev_wsq_ev[o - 2]]
        ev_wip_dma[o] = ld(wbv, wip[o], waits=wt)
        ev_wsq_ev[o] = veg("scalar", lambda e, wbv=wbv: e.square(V("wsq"), wbv),
                           waits=[ev_wip_dma[o], ev_nm_ev[o - 1] if o else None])
        nm_items = [(PB.ap()[0:1, 512:640], V("conesbc"),
                     V("wsq", k * 128, 128), k == 0, k == KD - 1, [5]) for k in range(KD)]
        ev_nm_ev[o] = mm_group(nm_items, waits=[ev_wsq_ev[o]])
        ev_row = veg("vector", lambda e: e.tensor_copy(V("snrow"), PB.ap()[0:1, 512:640]), rb=[5])
        ev_tc = mm_group([(PB.ap()[:, 1024:1025], V("snrow"), V("conesf11"), True, True, [6])],
                         waits=[ev_row])
        ev_sip_parts.append(
            veg("vector", lambda e, o=o: e.tensor_copy(V("sip", o, 1), PB.ap()[:, 1024:1025]), rb=[6]))
    veg("vector", lambda e: e.memzero(V("vdummy")))
    veg("vector", lambda e: e.memzero(V("vdummy")))
    ev_sc0 = veg("vector", lambda e: e.tensor_scalar(V("sip"), V("sip"), 1e-12, None, ALU.max),
                 waits=[ev_sip_parts[-1]])
    ev_sc1 = veg("scalar", lambda e: e.activation(V("sip"), V("sip"), AF.Ln, scale=float(D)),
                 waits=[ev_sc0])
    ev_sip = veg("scalar", lambda e: e.activation(V("sip"), V("sip"), AF.Exp, scale=-0.5),
                 waits=[ev_sc1])
    tap("t_sip1", V("sip"), [ev_sip])

    ev_z_last = ev_x_last = ev_dt_last = ev_c_last = None
    grp = 0
    ev_opn_sq = [None] * 8
    ev_opn_nmB = [None] * 8
    ev_opn_parts = []

    def emit_opnorm(o2):
        wt = [ev_u2done] if o2 == 0 else [ev_opn_sq[o2 - 1]]
        dma = ld(V("wopnb"), wopb[o2], waits=wt, q="qg")
        sqA = veg("scalar", lambda e: e.square(V("wsq"), V("wopnb", 0, D)),
                  waits=[dma, ev_nm_ev[NOT - 1] if o2 == 0 else ev_opn_nmB[o2 - 1]])
        nmA = mm_group([(PB.ap()[0:1, 512:640], V("conesbc"), V("wsq", k * 128, 128),
                         k == 0, False, [5]) for k in range(KD)], waits=[sqA])
        sqB = veg("scalar", lambda e: e.square(V("wsq"), V("wopnb", D, D)),
                  waits=[nmA])
        ev_opn_sq[o2] = sqB
        ev_opn_nmB[o2] = mm_group([(PB.ap()[0:1, 512:640], V("conesbc"), V("wsq", k * 128, 128),
                                    False, k == KD - 1, [5]) for k in range(KD)], waits=[sqB])
        orow = veg("vector", lambda e: e.tensor_copy(V("snrow"), PB.ap()[0:1, 512:640]), rb=[5])
        otc = mm_group([(PB.ap()[:, 1024:1025], V("snrow"), V("conesf11"), True, True, [6])],
                       waits=[orow])
        ev_opn_parts.append(
            veg("vector", lambda e, o2=o2: e.tensor_copy(V("sop", o2, 1), PB.ap()[:, 1024:1025]), rb=[6]))

    for o in range(NOT):
        slot = o % 2
        wt = ([ev_wsq_ev[NOT - 1], ev_wsq_ev[NOT - 2]] if o < 2
              else [ev_wip_lastmm[o - 2]])
        ev_wip_dma[o] = ld(V("wipbuf", slot * D, D), wip[o], waits=wt)
        if 2 <= o < 10:
            emit_opnorm(o - 2)
        for bi, (lo, n) in enumerate(NBLKG):
            g4 = grp % 4
            grp += 1
            pslice = PA.ap()[:, g4 * 512:g4 * 512 + n]
            bks = [g4]
            items = [(pslice, V("wipbuf", slot * D + k * 128, 128),
                      u23[:, k, lo:lo + n], k == 0, k == KD - 1, bks)
                     for k in range(KD)]
            ev_mm = mm_group(items, waits=[ev_wip_dma[o], ev_u2done])
            ev_wip_lastmm[o] = ev_mm
            if o < 16:
                ev_z_last = veg("vector", lambda e, o=o, lo=lo, n=n, p=pslice: e.tensor_scalar(
                    zsz3[:, o, lo:lo + n], p, V("sip", o, 1), None, ALU.mult),
                    rb=bks, waits=[ev_sip])
            elif o < 33:
                tx = o - 16
                if bi > 0:
                    ev_x_last = veg("vector", lambda e, tx=tx, lo=lo, n=n, p=pslice, o=o: e.tensor_scalar(
                        xar3[:, tx, lo:lo + n], p, V("sip", o, 1), None, ALU.mult),
                        rb=bks, waits=[ev_sip])
                else:
                    ev_x_last = veg("vector", lambda e, tx=tx, lo=lo, n=n, p=pslice, o=o: e.tensor_scalar(
                        xar3[:, tx, lo:lo + n], p, V("sip", o, 1), V("cmask"), ALU.mult, ALU.mult),
                        rb=bks, waits=[ev_sip])
            elif o == 33:
                pd = pslice[0:16, :]
                e1 = veg("scalar", lambda e, pd=pd, lo=lo, n=n: e.activation(
                    V("dtfm", lo, n), pd, AF.Exp, bias=V("cdtb"), scale=V("sip", 33, 1, parts=16)),
                    rb=bks, waits=[ev_sip, ev_consts])
                ev_dt_last = veg("scalar", lambda e, lo=lo, n=n: e.activation(
                    V("dtfm", lo, n), V("dtfm", lo, n), AF.Ln, bias=1.0), waits=[e1])
            else:
                pc = pslice[0:64, :]
                if bi > 0:
                    ev_c_last = veg("vector", lambda e, pc=pc, lo=lo, n=n: e.tensor_scalar(
                        V("cfm", lo, n), pc, V("sip", 34, 1, parts=64), None, ALU.mult),
                        rb=bks, waits=[ev_sip])
                else:
                    ev_c_last = veg("vector", lambda e, pc=pc, lo=lo, n=n: e.tensor_scalar(
                        V("cfm", lo, n), pc, V("sip", 34, 1, parts=64), V("cmask", parts=64),
                        ALU.mult, ALU.mult),
                        rb=bks, waits=[ev_sip])
    ev_so0 = veg("vector", lambda e: e.tensor_scalar(V("sop"), V("sop"), 1e-12, None, ALU.max),
                 waits=[ev_opn_parts[-1]])
    ev_so1 = veg("scalar", lambda e: e.activation(V("sop"), V("sop"), AF.Ln), waits=[ev_so0])
    ev_sop = veg("scalar", lambda e: e.activation(V("sop"), V("sop"), AF.Exp, scale=-0.5),
                 waits=[ev_so1])
    tap("t_sop", V("sop"), [ev_sop])

    ev_tap_z = tap("t_z", V("zsz"), [ev_z_last])
    ev_tap_x = tap("t_xbc", V("xar"), [ev_x_last])
    tap("t_dt", V("dtfm"), [ev_dt_last])

    p1_done = [ev_wip_lastmm[NOT - 1], ev_sip, ev_x_last, ev_z_last, ev_dt_last, ev_c_last]
    tap("t_sip2", V("sip"), p1_done)
    ev_convdg = ld(V("convdg"), convd, waits=p1_done)
    ev_convb = ld(V("cconvb"), convbi, waits=p1_done)
    ev_zsilu = veg("scalar", lambda e: e.activation(V("zsz"), V("zsz"), AF.Silu),
                   waits=[ev_z_last, ev_tap_z])
    ev_conv = None
    grp = 0
    for t in [17] + list(range(17)):
        for (lo, n) in CBLK:
            g4 = grp % 4
            grp += 1
            pslice = PA.ap()[:, g4 * 512:g4 * 512 + n]
            bks = [g4]
            if t < 17:
                rhs = [xar3[:, t, lo - 3 + k:lo - 3 + k + n] for k in range(4)]
                items = [(pslice, V("convdg", (k * 18 + t) * 128, 128),
                          rhs[k], k == 0, k == 3, bks) for k in range(4)]
            else:
                rhs = [V("xar", 16 * TF + lo - 3 + k, n) for k in range(4)]
                items = [(pslice[0:64, :], V("convdg", (k * 18 + 17) * 128, 64),
                          rhs[k], k == 0, k == 3, bks) for k in range(4)]
            ev_mm = mm_group(items, waits=[ev_convdg, ev_x_last, ev_c_last])
            if t < 17:
                ev_conv = veg("scalar", lambda e, t=t, lo=lo, n=n, p=pslice: e.activation(
                    xar3[:, t, lo:lo + n], p, AF.Silu, bias=V("cconvb", t, 1)),
                    rb=bks, waits=[ev_zsilu, ev_tap_x, ev_convb])
            else:
                ev_conv = veg("scalar", lambda e, lo=lo, n=n, p=pslice: e.activation(
                    V("cfm", lo, n), p[0:64, :], AF.Silu, bias=V("cconvb", 17, 1, parts=64)),
                    rb=bks, waits=[ev_zsilu, ev_convb])
    tap("t_x", V("xar"), [ev_conv])
    tap("t_sz", V("zsz"), [ev_conv])
    tap("t_c", V("cfm"), [ev_conv])

    dttok = V("dtfm", HALO, T)
    ev_ctri = ld(V("ctri"), TRIb, waits=[ev_conv])
    ev_cdDl = ld(V("cdD"), dDt, waits=[ev_conv])
    ev_negA = veg("scalar", lambda e: e.activation(V("negA"), V("calog"), AF.Exp), waits=[ev_consts])
    ev_dabf = veg("vector", lambda e: e.tensor_scalar(V("dabf"), dttok, V("negA"), -1.0,
                                                      ALU.mult, ALU.mult),
                  waits=[ev_dt_last, ev_negA])
    tap("t_negA", V("negA"), [ev_negA])
    tap("t_dabf", V("dabf"), [ev_dabf])
    tr_items = [(PAb[:, 2048 + c * 16:2048 + c * 16 + 16],
                 V("dabf", c * 128, 128), V("ci128b", 0, 16, parts=16), [2]) for c in range(NCH)]
    ev_datr = tr_group(tr_items, waits=[ev_dabf, ev_consts])
    ev_datm = veg("vector", lambda e: e.tensor_copy(V("datm"), PAb[:, 2048:2304]), rb=[2])
    cum_items = [(PB.ap()[0:16, c * 128:(c + 1) * 128], V("datm", c * 16, 16),
                  V("ctri"), True, True, pb_banks(c * 128, 128)) for c in range(NCH)]
    ev_cummm = mm_group(cum_items, waits=[ev_datm, ev_ctri])
    ev_cum = veg("vector", lambda e: e.tensor_copy(V("cumfm"), PB.ap()[0:16, 0:2048]),
                 rb=pb_banks(0, 2048))
    tap("t_cum", V("cumfm"), [ev_cum])
    ev_ln = veg("scalar", lambda e: e.activation(V("lntmp"), dttok, AF.Ln), waits=[ev_dt_last])
    ev_najp = veg("vector", lambda e: e.tensor_tensor(V("najpfm"), V("lntmp"), V("cumfm"), ALU.subtract),
                  waits=[ev_ln, ev_cum])
    tr_items = [(PA.ap()[:, 1024 + c * 16:1024 + c * 16 + 16],
                 V("najpfm", c * 128, 128), V("ci128f", 0, 16, parts=16), [2]) for c in range(NCH)]
    ev_najptr = tr_group(tr_items, waits=[ev_najp])
    ev_najptm = veg("vector", lambda e: e.tensor_copy(V("najptm"), PA.ap()[:, 1024:1280]), rb=[2])
    ev_tmk = ld(V("ctmask"), TMASKb, waits=[ev_conv])
    ev_chi = veg("scalar", lambda e: e.copy(V("cumhi"), V("cumfm")), waits=[ev_cum])
    ev_clo = veg("vector", lambda e: e.tensor_tensor(V("cumlo"), V("cumfm"), V("cumhi"), ALU.subtract),
                 waits=[ev_chi])
    ev_wf = None
    for c in range(NCH):
        ev_wf = veg("vector", lambda e, c=c: e.tensor_scalar(
            V("wfm", c * 128, 128), V("cumfm", c * 128, 128), -1.0,
            V("cumfm", c * 128 + 127, 1), ALU.mult, ALU.add), waits=[ev_cum])
    ev_wexp = veg("scalar", lambda e: e.activation(V("wfm"), V("wfm"), AF.Exp), waits=[ev_wf])
    ev_wmul = veg("vector", lambda e: e.tensor_tensor(V("wfm"), V("wfm"), dttok, ALU.mult),
                  waits=[ev_wexp])
    ev_wbf = veg("scalar", lambda e: e.copy(V("dabf"), V("wfm")), waits=[ev_wmul, ev_datm, ev_cummm])
    tr_items = [(PAb[:, 2048 + c * 16:2048 + c * 16 + 16],
                 V("dabf", c * 128, 128), V("ci128b", 0, 16, parts=16), [2]) for c in range(NCH)]
    ev_wtr = tr_group(tr_items, waits=[ev_wbf])
    ev_wtm = veg("vector", lambda e: e.tensor_copy(V("wtm"), PAb[:, 2048:2304]), rb=[2])
    ev_cdg = ld(V("cdrow"), V("cumfm")[:, 127::128], waits=[ev_cum], q="qs", slow=True)
    ev_cdx = veg("scalar", lambda e: e.activation(V("cdrow"), V("cdrow"), AF.Exp), waits=[ev_cdg])
    ev_s2z = veg("vector", lambda e: e.memzero(V("S2")))

    def chunk_scan(c, phase5, pv):
        ccol = HALO + c * CH
        ev = {}
        tr_items = [(PAb[:, h * 128:(h + 1) * 128],
                     xar3[:, h, ccol:ccol + 128], V("ci128b"), [0, 1]) for h in range(NH)]
        ev_xt = tr_group(tr_items, waits=[ev_conv])
        ev_bt = tr_group([(PAb[:, 3072:3136], V("xar", 16 * TF + ccol, 128, parts=64),
                           V("ci128b", 0, 64, parts=64), [3])], waits=[ev_conv])
        ev_btm = veg("vector", lambda e: e.tensor_copy(V("btm"), PAb[:, 3072:3136]),
                     rb=[3], waits=[pv.get("st")])
        xw3 = h3("xw", NH)
        pab3 = PAb[:, 0:2048].rearrange("p (h i) -> p h i", h=NH)
        ev_xw = veg("vector", lambda e: e.tensor_tensor(
            xw3, pab3, rep_h(V("wtm", c * 16, 16)), ALU.mult),
            rb=[0, 1], waits=[ev_wtm, pv.get("st")])
        if phase5:
            ev["xtm"] = veg("scalar", lambda e: e.copy(V("xtm"), PAb[:, 0:2048]),
                            rb=[0, 1], waits=[pv.get("y"), pv.get("rr"), pv.get("s2")])
        ev_cd = mm_group([(PA.ap()[0:64, 1600:1616], V("conesf1", 0, 64),
                           V("cdrow")[:, c::16], True, True, [3])], waits=[ev_cdx])
        if phase5:
            ev_rh = ld(V("rowhl", 0, NH * 128, parts=1), V("cumhi")[:, c * 128:(c + 1) * 128],
                       waits=[ev_chi, pv.get("ai"), pv.get("s2")], q="qs")
            ev_rl = ld(V("rowhl", 0, NH * 128, parts=1, p0=1), V("cumlo")[:, c * 128:(c + 1) * 128],
                       waits=[ev_clo, pv.get("ai")], q="qs")
            items = [(PB.ap()[:, ni * 512:(ni + 1) * 512], V("conespp", 0, 128, parts=2),
                      V("rowhl", ni * 512, 512), True, True,
                      pb_banks(ni * 512, 512)) for ni in range(4)]
            ev["ai"] = mm_group(items, waits=[ev_rh, ev_rl])
            ev_sc = mm_group([(PA.ap()[:, 1024:1152], V("xar", 16 * TF + ccol, 128, parts=64),
                               V("cfm", ccol, 128), True, True, [2])], waits=[ev_conv])
            ev_eb = veg("scalar", lambda e: e.activation(V("eb"), PB.ap()[0:64, 0:2048], AF.Exp),
                        rb=pb_banks(0, 2048), waits=[pv.get("ce"), pv.get("s2")])
            items = [(PB.ap()[:, ni * 512:(ni + 1) * 512], V("ci128b"),
                      V("ctmask", 0, 512), False, True, pb_banks(ni * 512, 512))
                     for ni in range(4)]
            ev_ai2 = mm_group(items, waits=[ev_tmk, ev_eb])
            es3 = h3("es", NH)
            ev_es = None
            for h in range(NH):
                ev_es = veg("scalar", lambda e, h=h: e.activation(
                    es3[:, h, :], PB.ap()[:, h * 128:(h + 1) * 128], AF.Exp,
                    bias=V("najptm", c * 16 + h, 1)),
                    rb=pb_banks(h * 128, 128),
                    waits=[ev_najptm] + ([pv.get("y"), pv.get("s2")] if h == 0 else []))
            ev["es"] = ev_es
            ev_att = veg("vector", lambda e: e.tensor_tensor(
                es3, es3, rep_o(PA.ap()[:, 1024:1152], NH), ALU.mult),
                rb=[2], waits=[ev["es"]])
            eb3 = V("eb").rearrange("p (h i) -> p h i", h=NH)
            ce3 = V("ce").rearrange("p (h i) -> p h i", h=NH)
            ev["ce"] = veg("vector", lambda e: e.tensor_tensor(
                ce3, rep_o(V("cfm", ccol, 128), NH), eb3, ALU.mult),
                waits=[ev_eb, ev_conv, pv.get("y")])
            ev_sbf = veg("scalar", lambda e: e.copy(V("Sbf"), V("S2")),
                         waits=[pv.get("s2"), pv.get("y")])
            ev["sbf"] = ev_sbf
        st_items = [(PB.ap()[0:64, h * 128:(h + 1) * 128], V("btm"),
                     xw3[:, h, :], True, True, pb_banks(h * 128, 128)) for h in range(NH)]
        ev["st"] = mm_group(st_items, waits=[ev_xw, ev_btm])
        s23 = V("S2").rearrange("p (h i) -> p h i", h=NH)
        ev_s2m = veg("vector", lambda e: e.tensor_tensor(
            s23, s23, PA.ap()[0:64, 1600:1616].unsqueeze(2).broadcast_to([64, 16, 128]), ALU.mult),
            rb=[3], waits=[pv.get("s2")] + ([ev["sbf"]] if phase5 else []))
        ev["s2"] = veg("vector", lambda e: e.tensor_tensor(
            V("S2"), V("S2"), PB.ap()[0:64, 0:2048], ALU.add),
            rb=pb_banks(0, 2048), waits=[ev_s2m])
        if phase5:
            y_items = []
            for h in range(NH):
                bks = pb_banks(h * 128, 128)
                y_items.append((PB.ap()[:, h * 128:(h + 1) * 128],
                                V("xtm", h * 128, 128), h3("es", NH)[:, h, :], True, False, bks))
                y_items.append((PB.ap()[:, h * 128:(h + 1) * 128],
                                V("Sbf", h * 128, 128, parts=64),
                                h3("ce", NH)[:, h, :], False, False, bks))
                y_items.append((PB.ap()[:, h * 128:(h + 1) * 128],
                                V("cdD", h * 128, 128), xar3[:, h, ccol:ccol + 128],
                                False, True, bks))
            ev["y"] = mm_group(y_items, waits=[ev["xtm"], ev_att, ev["ce"], ev_sbf, ev_conv,
                                               ev_cdDl])
        if phase5:
            g3 = h3("gbuf", NH)
            ev_g = veg("vector", lambda e: e.tensor_tensor(
                g3, PB.ap()[:, 0:2048].rearrange("p (h i) -> p h i", h=NH),
                zsz3[:, :, ccol:ccol + 128], ALU.mult),
                rb=pb_banks(0, 2048), waits=[ev["y"]])
            ev_g2 = veg("vector", lambda e: e.tensor_tensor(V("g2buf"), V("gbuf"), V("gbuf"), ALU.mult),
                        waits=[ev_g, pv.get("rr")])
            rr_items = [(PA.ap()[0:1, 1792:1920], V("conesbc"), V("g2buf", h * 128, 128),
                         h == 0, h == NH - 1, [3]) for h in range(NH)]
            ev["rr"] = mm_group(rr_items, waits=[ev_g2])
            ev_rl1 = veg("scalar", lambda e: e.activation(V("rrow"), PA.ap()[0:1, 1792:1920], AF.Ln,
                                                          bias=RMS_EPS, scale=1.0 / DI),
                         rb=[3], waits=[pv.get("rb")])
            ev_rl2 = veg("scalar", lambda e: e.activation(V("rrow"), V("rrow"), AF.Exp, scale=-0.5),
                         waits=[ev_rl1])
            ev["rb"] = mm_group([(PA.ap()[:, 1664:1792], V("conesf1"), V("rrow"), True, True, [3])],
                                waits=[ev_rl2])
            ev["sc"] = ev_rl2
            ev["gn"] = veg("vector", lambda e: e.tensor_tensor(
                xar3[:, 0:16, ccol:ccol + 128], g3,
                PA.ap()[:, 1664:1792].unsqueeze(1).broadcast_to([128, NH, 128]), ALU.mult),
                rb=[3], waits=[ev_g])
        return ev

    pv = {"s2": ev_s2z}
    for c in range(NCH):
        pv = chunk_scan(c, False, pv)
    ev_E = pv["s2"]
    ev_tap_E = tap("t_E", V("S2"), [ev_E])

    ev_esend = ld(e_in.ap(), V("S2"), waits=[ev_E], q="qs")

    def coll(e):
        return e.collective_compute(
            "AllGather", ALU.bypass,
            replica_groups=[[0, 1], [2, 3], [4, 5], [6, 7]],
            ins=[e_in.ap().opt()], outs=[e_out.ap().opt()])
    ev_cc = S.emit("gpsimd", coll, waits=[ev_esend], inc="cc")
    ev_erecv = ld(V("Erecv"), e_out.ap()[0:1, :], waits=[ev_cc], q="qs")
    ev_s2i = veg("vector", lambda e: e.tensor_scalar(
        V("S2"), V("Erecv"), V("cmask", parts=64), None, ALU.mult),
        waits=[ev_erecv, pv["s2"], ev_consts, ev_tap_E])

    pv = {"s2": ev_s2i, "st": pv["st"]}
    for c in range(NCH):
        pv = chunk_scan(c, True, pv)
    ev_gn_all = pv["gn"]
    p5_done = [ev_gn_all, pv["s2"], pv["sc"]]
    tap("t_sip3", V("sip"), p5_done)
    tap("t_gn", V("xar", 0, 16 * TF), [ev_gn_all])

    ev_r1u = ld(V("r1").rearrange("p (k t) -> p k t", k=KD),
                uT[:, HALO:HALO + T].rearrange("(k p) t -> p k t", p=128),
                waits=[ev_ubf] + p5_done, q="qs")
    attn3 = h3("attnbf", KD)
    ev_wop_fold = [None] * 8
    ev_wop_lastmm = [None] * 8
    ev_rmsw = ld(V("crmsw"), rmsw, waits=p5_done)

    ev_attn = None
    ev_unf = [None] * 8
    for o2 in range(8):
        slot = o2 % 2
        wfv = V("wopbf", slot * DI, DI)
        wt = ([ev_wop_lastmm[o2 - 2]] if o2 >= 2
              else p5_done + [ev_opn_nmB[7]])
        ev_unf[o2] = ld(wfv, wopb[o2], waits=wt, q="qg")
        wf3 = wfv.rearrange("p (k o) -> p k o", k=KI)
        ev_wop_fold[o2] = veg("vector", lambda e, wf3=wf3: e.tensor_tensor(
            wf3, wf3, rep_h(V("crmsw")), ALU.mult),
            waits=[ev_unf[o2], ev_rmsw])
        ps = PA.ap() if o2 % 2 == 0 else PB.ap()
        pbk = pa_banks if o2 % 2 == 0 else pb_banks
        items = []
        for k in range(KI):
            for ni in range(4):
                items.append((ps[:, ni * 512:(ni + 1) * 512],
                              V("wopbf", slot * DI + k * 128, 128),
                              xar3[:, k, HALO + ni * 512:HALO + (ni + 1) * 512],
                              k == 0, k == KI - 1, pbk(ni * 512, 512)))
        ev_wop_lastmm[o2] = mm_group(items, waits=[ev_wop_fold[o2], ev_gn_all])
        for ni in range(4):
            ev_attn = veg("vector", lambda e, o2=o2, ni=ni, ps=ps: e.tensor_scalar(
                attn3[:, o2, ni * 512:(ni + 1) * 512], ps[:, ni * 512:(ni + 1) * 512],
                V("sop", o2, 1), None, ALU.mult),
                rb=pbk(ni * 512, 512), waits=[ev_sop])
    tap("t_attn", V("attnbf"), [ev_attn])
    tap("t_wf7", V("wopbf", DI, DI), [ev_wop_fold[7], ev_wop_lastmm[7]])

    items = [(PA.ap()[:, ni * 512:(ni + 1) * 512], V("conespp"),
              attn3[:, k, ni * 512:(ni + 1) * 512], k == 0, k == KD - 1,
              pa_banks(ni * 512, 512)) for ni in range(4) for k in range(KD)]
    ev_sm6 = mm_group(items, waits=[ev_attn])
    ev_mu6 = veg("vector", lambda e: e.tensor_scalar(V("mu6"), PA.ap()[:, :], 1.0 / D, None, ALU.mult),
                 rb=pa_banks(0, 2048))
    sq6 = h3("sq6", KD)
    ev_prev_sq = None
    for ni in range(4):
        ev_sq = veg("vector", lambda e, ni=ni: e.tensor_tensor(
            sq6, attn3[:, :, ni * 512:(ni + 1) * 512],
            attn3[:, :, ni * 512:(ni + 1) * 512], ALU.mult),
            waits=[ev_attn, ev_prev_sq])
        items = [(PA.ap()[:, ni * 512:(ni + 1) * 512], V("conespp"),
                  sq6[:, k, :], k == 0, k == KD - 1, pa_banks(ni * 512, 512))
                 for k in range(KD)]
        ev_prev_sq = mm_group(items, waits=[ev_sq, ev_mu6])
    ev_ms6 = veg("vector", lambda e: e.tensor_scalar(V("al6"), PA.ap()[:, :], 1.0 / D, None, ALU.mult),
                 rb=pa_banks(0, 2048))
    ev_t6 = veg("vector", lambda e: e.tensor_tensor(V("tb6", 0, 2048), V("mu6"), V("mu6"), ALU.mult),
                waits=[ev_ms6])
    ev_v6 = veg("vector", lambda e: e.tensor_tensor(V("al6"), V("al6"), V("tb6", 0, 2048), ALU.subtract),
                waits=[ev_t6])
    ev_a6a = veg("scalar", lambda e: e.activation(V("al6"), V("al6"), AF.Ln, bias=LN_EPS), waits=[ev_v6])
    ev_a6 = veg("scalar", lambda e: e.activation(V("al6"), V("al6"), AF.Exp, scale=-0.5), waits=[ev_a6a])
    ev_mua6 = veg("vector", lambda e: e.tensor_tensor(V("mu6"), V("mu6"), V("al6"), ALU.mult),
                  waits=[ev_a6])
    r13 = V("r1").rearrange("p (k t) -> p k t", k=KD)
    ev_r1 = veg("vector", lambda e: e.tensor_tensor(r13, r13, rep_o(V("mu6"), KD), ALU.subtract),
                waits=[ev_mua6, ev_r1u])
    for k in range(KD):
        for ni in range(4):
            ev_tm = veg("vector", lambda e, k=k, ni=ni: e.tensor_tensor(
                V("tb6"), attn3[:, k, ni * 512:(ni + 1) * 512],
                V("al6", ni * 512, 512), ALU.mult), waits=[ev_r1])
            ev_r1 = veg("vector", lambda e, k=k, ni=ni: e.tensor_tensor(
                r13[:, k, ni * 512:(ni + 1) * 512], r13[:, k, ni * 512:(ni + 1) * 512],
                V("tb6"), ALU.add), waits=[ev_tm])
    tap("t_r1", V("r1"), [ev_r1])

    def r1h3(hf):
        return V("r1").rearrange("p (k t) -> p k t", k=KD)[:, :, hf * 1024:(hf + 1) * 1024]

    hv = V("xar", 0, KH * 1024).rearrange("p (o t) -> p o t", o=KH)
    xf3 = V("xar", KH * 1024, KD * 1024).rearrange("p (k t) -> p k t", k=KD)
    prev_gu = {}
    prev_dn = {}
    ev_hf_done = None
    ev_out_last = None
    for hf in range(2):
        ev_rbf = ld(V("r1bfh"), r1h3(hf), waits=[ev_r1, ev_hf_done], q="qg")
        rb3 = h3("r1bfh", KD)
        ev_sq7 = veg("vector", lambda e: e.tensor_tensor(V("sqstg"), V("r1bfh"), V("r1bfh"), ALU.mult),
                     waits=[ev_rbf, ev_hf_done])
        items = [(PA.ap()[:, ni * 512:(ni + 1) * 512], V("conespp"),
                  rb3[:, k, ni * 512:(ni + 1) * 512], k == 0, k == KD - 1,
                  pa_banks(ni * 512, 512)) for ni in range(2) for k in range(KD)]
        ev_sm7 = mm_group(items, waits=[ev_rbf])
        ev_mu7 = veg("vector", lambda e: e.tensor_scalar(V("mu7"), PA.ap()[:, 0:1024], 1.0 / D, None, ALU.mult),
                     rb=pa_banks(0, 1024))
        stg3 = h3("sqstg", KD)
        items = [(PA.ap()[:, ni * 512:(ni + 1) * 512], V("conespp"),
                  stg3[:, k, ni * 512:(ni + 1) * 512], k == 0, k == KD - 1,
                  pa_banks(ni * 512, 512)) for ni in range(2) for k in range(KD)]
        ev_sm7b = mm_group(items, waits=[ev_sq7, ev_mu7])
        ev_ms7 = veg("vector", lambda e: e.tensor_scalar(V("al7"), PA.ap()[:, 0:1024], 1.0 / D, None, ALU.mult),
                     rb=pa_banks(0, 1024))
        ev_t7 = veg("vector", lambda e: e.tensor_tensor(V("mu7"), V("mu7"), V("mu7"), ALU.mult) if False
                    else e.tensor_tensor(V("tb7"), V("mu7", 0, 512), V("mu7", 0, 512), ALU.mult),
                    waits=[ev_ms7])
        ev_v7 = veg("vector", lambda e: e.tensor_tensor(V("al7", 0, 512), V("al7", 0, 512), V("tb7"), ALU.subtract),
                    waits=[ev_t7])
        ev_t7b = veg("vector", lambda e: e.tensor_tensor(V("tb7"), V("mu7", 512, 512), V("mu7", 512, 512), ALU.mult),
                     waits=[ev_v7])
        ev_v7b = veg("vector", lambda e: e.tensor_tensor(V("al7", 512, 512), V("al7", 512, 512), V("tb7"), ALU.subtract),
                     waits=[ev_t7b])
        ev_a7a = veg("scalar", lambda e: e.activation(V("al7"), V("al7"), AF.Ln, bias=LN_EPS), waits=[ev_v7b])
        ev_a7 = veg("scalar", lambda e: e.activation(V("al7"), V("al7"), AF.Exp, scale=-0.5), waits=[ev_a7a])
        ev_mua7 = veg("vector", lambda e: e.tensor_tensor(V("mu7"), V("mu7"), V("al7"), ALU.mult),
                      waits=[ev_a7])
        ev_xf1 = veg("vector", lambda e: e.tensor_tensor(xf3, rb3, rep_o(V("mu7"), KD), ALU.subtract),
                     waits=[ev_mua7, prev_dn.get(7)])
        ev_xf = veg("vector", lambda e: e.tensor_tensor(xf3, xf3, rep_o(V("al7"), KD), ALU.mult),
                    waits=[ev_xf1])
        if hf == 0:
            tap("t_xf", V("xar", KH * 1024, KD * 1024), [ev_xf])
        ev_h_last = None
        ev_sg_prev = None
        for o in range(KH):
            slot = o % 3
            gu = (V("wgubuf", slot * 2 * D, 2 * D) if slot < 2
                  else V("wgux", 0, 2 * D))
            wt = [ev_r1] if (hf == 0 and o < 3) else [prev_gu.get((o - 3) % KH)]
            ev_gud = ld(gu.rearrange("p (g d) -> p g d", g=2),
                        wgu[o].rearrange("g p d -> p g d"), waits=wt)
            psg = PA.ap() if o % 2 == 0 else PB.ap()
            bksg = (pa_banks if o % 2 == 0 else pb_banks)(0, 1024)
            bksu = (pa_banks if o % 2 == 0 else pb_banks)(1024, 1024)
            items = []
            for k in range(KD):
                for nn in range(2):
                    items.append((psg[:, nn * 512:(nn + 1) * 512],
                                  gu[:, k * 128:(k + 1) * 128],
                                  xf3[:, k, nn * 512:(nn + 1) * 512],
                                  k == 0, k == KD - 1, bksg))
            for k in range(KD):
                for nn in range(2):
                    items.append((psg[:, 1024 + nn * 512:1024 + (nn + 1) * 512],
                                  gu[:, D + k * 128:D + (k + 1) * 128],
                                  xf3[:, k, nn * 512:(nn + 1) * 512],
                                  k == 0, k == KD - 1, bksu))
            prev_gu[o] = mm_group(items, waits=[ev_gud, ev_xf])
            ev_sg = veg("scalar", lambda e, psg=psg: e.activation(
                V("sgbuf"), psg[:, 0:1024], AF.Silu, scale=QSCALE),
                rb=bksg, waits=[ev_h_last])
            ev_h_last = veg("vector", lambda e, psg=psg, o=o: e.scalar_tensor_tensor(
                hv[:, o, :], psg[:, 1024:2048], QSCALE, V("sgbuf"), ALU.mult, ALU.mult),
                rb=bksu, waits=[ev_sg])
        ev_stg_last = None
        for o2 in range(8):
            slot = o2 % 2
            wd = V("wdbuf", slot * KH * 128, KH * 128)
            wt = [prev_dn.get((o2 - 2) % 8)] if (hf > 0 or o2 >= 2) else [ev_h_last]
            ev_wdd = ld(wd, wdb[o2], waits=wt)
            ps = PA.ap() if o2 % 2 == 0 else PB.ap()
            bks = (pa_banks if o2 % 2 == 0 else pb_banks)(0, 1024)
            items = []
            for k in range(KH):
                for ni in range(2):
                    items.append((ps[:, ni * 512:(ni + 1) * 512], wd[:, k * 128:(k + 1) * 128],
                                  hv[:, k, ni * 512:(ni + 1) * 512], k == 0, k == KH - 1, bks))
            prev_dn[o2] = mm_group(items, waits=[ev_wdd, ev_h_last])
            ev_stg_last = veg("vector", lambda e, o2=o2, ps=ps: e.tensor_scalar(
                stg3[:, o2, :], ps[:, 0:1024], QSCALE, None, ALU.mult),
                rb=bks, waits=[ev_sm7b])
        if hf == 0:
            tap("t_h", V("xar", 0, KH * 1024), [ev_h_last])
            tap("t_stg", V("sqstg"), [ev_stg_last])
        items = [(PA.ap()[:, ni * 512:(ni + 1) * 512], V("conespp"),
                  stg3[:, k, ni * 512:(ni + 1) * 512], k == 0, k == KD - 1,
                  pa_banks(ni * 512, 512)) for ni in range(2) for k in range(KD)]
        ev_sm7c = mm_group(items, waits=[ev_stg_last])
        ev_mu7b = veg("vector", lambda e: e.tensor_scalar(V("mu7"), PA.ap()[:, 0:1024], 1.0 / D, None, ALU.mult),
                      rb=pa_banks(0, 1024))
        ev_sqp = veg("vector", lambda e: e.tensor_tensor(V("r1bfh"), V("sqstg"), V("sqstg"), ALU.mult),
                     waits=[ev_stg_last, ev_sm7b])
        sqp3 = h3("r1bfh", KD)
        items = [(PA.ap()[:, ni * 512:(ni + 1) * 512], V("conespp"),
                  sqp3[:, k, ni * 512:(ni + 1) * 512], k == 0, k == KD - 1,
                  pa_banks(ni * 512, 512)) for ni in range(2) for k in range(KD)]
        ev_sm7d = mm_group(items, waits=[ev_sqp, ev_mu7b])
        ev_ms7b = veg("vector", lambda e: e.tensor_scalar(V("al7"), PA.ap()[:, 0:1024], 1.0 / D, None, ALU.mult),
                      rb=pa_banks(0, 1024))
        ev_t7c = veg("vector", lambda e: e.tensor_tensor(V("tb7"), V("mu7", 0, 512), V("mu7", 0, 512), ALU.mult),
                     waits=[ev_ms7b])
        ev_v7c = veg("vector", lambda e: e.tensor_tensor(V("al7", 0, 512), V("al7", 0, 512), V("tb7"), ALU.subtract),
                     waits=[ev_t7c])
        ev_t7d = veg("vector", lambda e: e.tensor_tensor(V("tb7"), V("mu7", 512, 512), V("mu7", 512, 512), ALU.mult),
                     waits=[ev_v7c])
        ev_v7d = veg("vector", lambda e: e.tensor_tensor(V("al7", 512, 512), V("al7", 512, 512), V("tb7"), ALU.subtract),
                     waits=[ev_t7d])
        ev_a7c = veg("scalar", lambda e: e.activation(V("al7"), V("al7"), AF.Ln, bias=LN_EPS), waits=[ev_v7d])
        ev_a7d = veg("scalar", lambda e: e.activation(V("al7"), V("al7"), AF.Exp, scale=-0.5), waits=[ev_a7c])
        ev_mua7b = veg("vector", lambda e: e.tensor_tensor(V("mu7"), V("mu7"), V("al7"), ALU.mult),
                       waits=[ev_a7d])
        rh = r1h3(hf)
        ev_fin = veg("vector", lambda e, rh=rh: e.tensor_tensor(rh, rh, rep_o(V("mu7"), KD), ALU.subtract),
                     waits=[ev_mua7b])
        for k in range(KD):
            for ni in range(2):
                ev_tm = veg("vector", lambda e, k=k, ni=ni, stg3=stg3: e.tensor_tensor(
                    V("tb7"), stg3[:, k, ni * 512:(ni + 1) * 512],
                    V("al7", ni * 512, 512), ALU.mult), waits=[ev_fin])
                ev_fin = veg("vector", lambda e, k=k, ni=ni, rh=rh: e.tensor_tensor(
                    rh[:, k, ni * 512:(ni + 1) * 512], rh[:, k, ni * 512:(ni + 1) * 512],
                    V("tb7"), ALU.add), waits=[ev_tm])
        ev_hf_done = ev_fin
        ev_out_last = ld(out[:, hf * 1024:(hf + 1) * 1024].rearrange("(k p) t -> p k t", p=128),
                         r1h3(hf), waits=[ev_fin], q="qs")

    S.emit("sync", lambda e: None, waits=[("qs", S.cnt["qs"]), ("qg", S.cnt["qg"])])

    if trunc is not None:
        S.truncate(trunc)

    with (
        nc.semaphore("sem_t") as s_t,
        nc.semaphore("sem_v") as s_v,
        nc.semaphore("sem_s") as s_s,
        nc.semaphore("sem_g") as s_g,
        nc.semaphore("sem_qg") as s_qg,
        nc.semaphore("sem_qs") as s_qs,
        nc.semaphore("sem_cc") as s_cc,
        nc.Block() as block,
    ):
        S.handles.update(t=s_t, v=s_v, s=s_s, g=s_g, qg=s_qg, qs=s_qs, cc=s_cc)

        @block.tensor
        def _(e):
            S.run("tensor", e)

        @block.vector
        def _(e):
            S.run("vector", e)

        @block.scalar
        def _(e):
            S.run("scalar", e)

        @block.gpsimd
        def _(e):
            S.run("gpsimd", e)

        @block.sync
        def _(e):
            S.run("sync", e)

    return nc


def prep_inputs(L0_attn_u, L0_attn_in_proj_w, L0_attn_conv_w, L0_attn_conv_b,
                L0_attn_dt_bias, L0_attn_A_log, L0_attn_D, L0_attn_x_norm_w,
                L0_attn_x_norm_b, L0_attn_rms_w, L0_attn_out_proj_w,
                L0_ffn_w_gate, L0_ffn_w_up, L0_ffn_w_down):
    u = np.ascontiguousarray(np.asarray(L0_attn_u, dtype=np.float32))
    ipw = np.asarray(L0_attn_in_proj_w, dtype=np.float32)
    opw = np.asarray(L0_attn_out_proj_w, dtype=np.float32)
    cw = np.asarray(L0_attn_conv_w, dtype=np.float32)
    cb = np.asarray(L0_attn_conv_b, dtype=np.float32)
    wg = np.asarray(L0_ffn_w_gate, dtype=np.int32)
    wu = np.asarray(L0_ffn_w_up, dtype=np.int32)
    wd = np.asarray(L0_ffn_w_down, dtype=np.int32)

    blocks = [ipw[t * 128:(t + 1) * 128] for t in range(33)]
    dtblk = np.zeros((128, D), np.float32)
    dtblk[:16] = ipw[4224:4240]
    blocks.append(dtblk)
    cdup = np.zeros((128, D), np.float32)
    cdup[:64] = ipw[4160:4224]
    blocks.append(cdup)
    wipb = np.stack(blocks)
    wipd = np.ascontiguousarray(
        wipb.reshape(NOT, 128, KD, 128).transpose(0, 3, 2, 1)).reshape(NOT, 128, D)

    wopd = np.ascontiguousarray(
        opw.reshape(8, 128, KI, 128).transpose(0, 3, 2, 1)).reshape(8, 128, DI)
    wgb = wg.reshape(KH, 128, KD, 128).transpose(0, 3, 2, 1).reshape(KH, 128, D)
    wub = wu.reshape(KH, 128, KD, 128).transpose(0, 3, 2, 1).reshape(KH, 128, D)
    wgud = np.ascontiguousarray(np.stack([wgb, wub], axis=1))
    wdd = np.ascontiguousarray(
        wd.reshape(8, 128, KH, 128).transpose(0, 3, 2, 1)).reshape(8, 128, HID)

    convdd = np.zeros((128, 4, 18, 128), np.float32)
    idx = np.arange(128)
    for k in range(4):
        for t in range(17):
            convdd[idx, k, t, idx] = cw[t * 128 + idx, k]
        q = np.arange(64)
        convdd[64 + q, k, 17, q] = cw[2112 + q, k]
    convdd = convdd.reshape(128, 4 * 18 * 128)
    convbd = np.zeros((128, 18), np.float32)
    for t in range(17):
        convbd[:, t] = cb[t * 128:(t + 1) * 128]
    convbd[:64, 17] = cb[2112:2176]

    xnw_ = np.ascontiguousarray(np.asarray(L0_attn_x_norm_w, np.float32).reshape(KD, 128).T)
    xnb_ = np.ascontiguousarray(np.asarray(L0_attn_x_norm_b, np.float32).reshape(KD, 128).T)
    rmsw_ = np.ascontiguousarray(np.asarray(L0_attn_rms_w, np.float32).reshape(KI, 128).T)
    dtb_ = np.ascontiguousarray(np.asarray(L0_attn_dt_bias, np.float32).reshape(16, 1))
    alog_ = np.ascontiguousarray(np.asarray(L0_attn_A_log, np.float32).reshape(16, 1))
    Dv = np.asarray(L0_attn_D, np.float32)
    dDd = np.zeros((128, NH * 128), np.float32)
    for h in range(NH):
        dDd[idx, h * 128 + idx] = Dv[h]

    shared = dict(wip=wipd, wopb=wopd, wgu=wgud, wdb=wdd, convd=convdd,
                  convbi=convbd, xnw=xnw_, xnb=xnb_, rmsw=rmsw_, dtbi=dtb_,
                  alogi=alog_, dDt=dDd)
    in_maps = []
    for c in range(8):
        s, hf = c // 2, c % 2
        t0 = hf * T
        uTc = np.zeros((D, TF), np.float32)
        uTc[:, HALO:] = u[s, t0:t0 + T].T
        if hf == 1:
            uTc[:, :HALO] = u[s, t0 - HALO:t0].T
        im = dict(shared)
        im["uT"] = np.ascontiguousarray(uTc)
        im["maskt"] = np.full((128, 1), 1.0 if hf == 1 else 0.0, np.float32)
        in_maps.append(im)
    return in_maps


def kernel(**inputs):
    in_maps = prep_inputs(**inputs)
    nc = build_program(debug=False)
    res = run_bass_kernel_spmd(nc, in_maps, core_ids=list(range(8)))
    outp = np.empty((4, 4096, D), np.float32)
    for c in range(8):
        s, hf = c // 2, c % 2
        outp[s, hf * T:(hf + 1) * T, :] = res.results[c]["out"].T
    return outp


# version 11
# speedup vs baseline: 1.0290x; 1.0158x over previous
import numpy as np
import ml_dtypes

import concourse.bass as bass
import concourse.mybir as mybir
from concourse.bass_utils import run_bass_kernel_spmd

F32 = mybir.dt.float32
BF16 = mybir.dt.bfloat16
I32 = mybir.dt.int32
AF = mybir.ActivationFunctionType
ALU = mybir.AluOpType
BF_NP = ml_dtypes.bfloat16

D = 1024
DI = 2048
NH = 16
NST = 64
CONVD = 2176
HID = 3072
T = 2048
HALO = 3
TF = T + HALO
CH = 128
NCH = T // CH
KD = D // 128
KI = DI // 128
KX = CONVD // 128
NOT = 35
KH = HID // 128
LN_EPS = 1e-5
RMS_EPS = 1e-5
DIMSCALE = float(D) ** -0.5
QSCALE = 1.0 / 127.0
ENGS = ("tensor", "vector", "scalar", "gpsimd", "sync")

NBLK = [(0, 512), (512, 512), (1024, 512), (1536, 512), (2048, 3)]
NBLKG = [(0, 3), (3, 512), (515, 512), (1027, 512), (1539, 512)]
CBLK = [(HALO + 1536, 512), (HALO + 1024, 512), (HALO + 512, 512), (HALO, 512)]


SEM_OWNER = {"t": "tensor", "v": "vector", "s": "scalar", "g": "gpsimd",
             "qg": "gpsimd", "qs": "sync", "cc": "gpsimd"}


class Sched:
    def __init__(self):
        self.ops = {e: [] for e in ENGS}
        self.cnt = {s: 0 for s in ("t", "v", "s", "g", "qg", "qs", "cc")}
        self.handles = {}
        self.lastw = {e: {} for e in ENGS}

    def truncate(self, keep):
        for e in ENGS:
            self.ops[e] = [op for op in self.ops[e] if op[4] < keep]

    def emit(self, eng, fn, waits=(), inc=None, inc_by=1):
        wl = []
        for w in waits:
            if w is None:
                continue
            s, v = w
            assert v <= self.cnt[s], f"deadlock: {eng} waits {s}>={v}, scheduled {self.cnt[s]}"
            if SEM_OWNER[s] == eng:
                if s in ("t", "v", "s", "g"):
                    continue
                v = self.cnt[s]
            if self.lastw[eng].get(s, -1) < v:
                self.lastw[eng][s] = v
                wl.append((s, v))
        ev = None
        if inc is not None:
            self.cnt[inc] += inc_by
            ev = (inc, self.cnt[inc])
        self.ops[eng].append((fn, tuple(wl), inc, inc_by, sum(len(v) for v in self.ops.values())))
        return ev

    def run(self, eng_name, eng):
        for fn, wl, inc, inc_by, _gi in self.ops[eng_name]:
            for s, v in wl:
                eng.wait_ge(self.handles[s], v)
            ins = fn(eng)
            if inc is not None:
                ins.then_inc(self.handles[inc], inc_by)


class Banks:
    def __init__(self):
        self.last_w = [None] * 8
        self.readers = [[] for _ in range(8)]

    def w_waits(self, banks):
        w = []
        for b in banks:
            w += [ev for (_, ev) in self.readers[b]]
            if self.last_w[b] is not None:
                w.append(self.last_w[b])
        return w

    def wrote(self, banks, ev):
        for b in banks:
            self.last_w[b] = ev
            self.readers[b] = []

    def r_waits(self, banks, eng=None):
        w = [self.last_w[b] for b in banks if self.last_w[b] is not None]
        if eng is not None:
            for b in banks:
                w += [ev for (re, ev) in self.readers[b] if re != eng]
        return w

    def read(self, banks, ev, eng=None):
        for b in banks:
            self.readers[b].append((eng, ev))


def pa_banks(lo, n):
    return list(range(lo // 512, (lo + n - 1) // 512 + 1))


def pb_banks(lo, n):
    return [4 + b for b in pa_banks(lo, n)]


class Arena:
    def __init__(self, nc, total_f32):
        self.t = nc.alloc_sbuf_tensor("ARENA", [128, total_f32], F32)
        self.total = total_f32
        self.allocs = []

    def reserve(self, name, f32_elems, p0, p1, align=16):
        f32_elems = -(-f32_elems // align) * align
        self.allocs.append([name, None, f32_elems, p0, p1])

    def _try(self, order):
        placed = []
        offs = {}
        hi = 0
        for name, _, sz, p0, p1 in order:
            spans = sorted((o, o + s) for (o, s, b0, b1) in placed
                           if not (b1 < p0 or b0 > p1))
            off = 0
            for s, e in spans:
                if off + sz <= s:
                    break
                off = max(off, e)
            placed.append((off, sz, p0, p1))
            offs[name] = off
            hi = max(hi, off + sz)
        return offs, hi

    def finalize(self):
        best = None
        import random
        keys = [lambda a: (-(a[4] - a[3]), -a[2]),
                lambda a: (-a[2], -(a[4] - a[3])),
                lambda a: (a[3], -a[2]),
                lambda a: (a[3], a[4], -a[2]),
                lambda a: (-a[3], -a[2]),
                lambda a: (-a[4], -a[2]),
                lambda a: (-a[2], a[3])]
        rng = random.Random(0)
        for t in range(40):
            sh = list(self.allocs)
            rng.shuffle(sh)
            offs, hi = self._try(sh)
            if best is None or hi < best[1]:
                best = (offs, hi)
        for key in keys:
            offs, hi = self._try(sorted(self.allocs, key=key))
            if best is None or hi < best[1]:
                best = (offs, hi)
        assert best[1] <= self.total, \
            f"SBUF arena overflow: best packing {best[1]} > {self.total}"
        return best[0]


def build_program(debug=False, trunc=None):
    nc = bass.Bass("TRN2", target_bir_lowering=False, detect_race_conditions=False)

    uT = nc.dram_tensor("uT", [D, TF], F32, kind="ExternalInput").ap()
    wip = nc.dram_tensor("wip", [NOT, 128, D], F32, kind="ExternalInput").ap()
    wopb = nc.dram_tensor("wopb", [8, 128, DI], F32, kind="ExternalInput").ap()
    wgu = nc.dram_tensor("wgu", [KH, 2, 128, D], I32, kind="ExternalInput").ap()
    wdb = nc.dram_tensor("wdb", [8, 128, HID], I32, kind="ExternalInput").ap()
    convd = nc.dram_tensor("convd", [128, 4 * 18 * 128], F32, kind="ExternalInput").ap()
    convbi = nc.dram_tensor("convbi", [128, 18], F32, kind="ExternalInput").ap()
    xnw = nc.dram_tensor("xnw", [128, KD], F32, kind="ExternalInput").ap()
    xnb = nc.dram_tensor("xnb", [128, KD], F32, kind="ExternalInput").ap()
    rmsw = nc.dram_tensor("rmsw", [128, KI], F32, kind="ExternalInput").ap()
    dtbi = nc.dram_tensor("dtbi", [16, 1], F32, kind="ExternalInput").ap()
    alogi = nc.dram_tensor("alogi", [16, 1], F32, kind="ExternalInput").ap()
    dDt = nc.dram_tensor("dDt", [128, NH * 128], F32, kind="ExternalInput").ap()
    maskt = nc.dram_tensor("maskt", [128, 1], F32, kind="ExternalInput").ap()
    out = nc.dram_tensor("out", [D, T], F32, kind="ExternalOutput").ap()

    e_in = nc.dram_tensor("e_in", [1, NST * DI], F32, kind="Internal")
    e_out = nc.dram_tensor("e_out", [2, NST * DI], F32, kind="Internal")

    taps = {}
    if debug:
        for name, shp in [("t_xf", [128, KD * 1024]), ("t_h", [128, KH * 1024]),
                          ("t_stg", [128, KD * 1024]),
                          ("t_sip1", [128, NOT]), ("t_sip2", [128, NOT]),
                          ("t_sip3", [128, NOT]), ("t_negA", [16, 1]),
                          ("t_dabf", [16, T]), ("t_sop", [128, 8]),
                          ("t_wf7", [128, DI]),
                          ("t_u2", [128, KD * TF]), ("t_z", [128, KI * TF]),
                          ("t_xbc", [128, KX * TF]), ("t_x", [128, KX * TF]),
                          ("t_sz", [128, KI * TF]), ("t_c", [64, TF]),
                          ("t_dt", [16, TF]), ("t_cum", [16, T]),
                          ("t_E", [64, NH * 128]), ("t_gn", [128, 16 * TF]),
                          ("t_attn", [128, KD * T]), ("t_r1", [128, KD * T])]:
            taps[name] = nc.dram_tensor(name, shp, F32, kind="ExternalOutput").ap()

    trin = np.tril(np.ones((128, 128), np.float32)).T.copy()
    TRIb = nc.inline_tensor(trin.astype(BF_NP), "TRIb").ap()
    tm_ = np.zeros((128, 512), np.float32)
    for h in range(4):
        tm_[:, h * 128:(h + 1) * 128] = -1000.0 * np.tril(np.ones((128, 128)), -1)
    TMASKb = nc.inline_tensor(tm_.astype(BF_NP), "TMASKb").ap()
    eye = np.eye(128, dtype=np.float32)
    I128b = nc.inline_tensor(eye.astype(BF_NP), "I128b").ap()
    I128f = nc.inline_tensor(eye.copy(), "I128f").ap()
    ONESb1 = nc.inline_tensor(np.ones((1, 128), BF_NP), "ONESb1").ap()
    ONESf1 = nc.inline_tensor(np.ones((1, 128), np.float32), "ONESf1").ap()
    ONESbc = nc.inline_tensor(np.ones((128, 1), BF_NP), "ONESbc").ap()
    ONESpp = nc.inline_tensor(np.ones((128, 128), BF_NP), "ONESpp").ap()
    ONESf11 = nc.inline_tensor(np.ones((1, 1), np.float32), "ONESf11").ap()
    CVALS = [0.0, 1.0, LN_EPS * LN_EPS, 1.0 + LN_EPS, -0.5, 0.5, QSCALE,
             LN_EPS, RMS_EPS, 1.0 / DI]
    CVN = nc.inline_tensor(np.tile(np.array(CVALS, np.float32), (128, 1)), "CVN").ap()

    A = Arena(nc, 52160)
    P = {}

    def AL(name, parts, n, dt_, p0, p1):
        f32e = n if dt_ == F32 else (n + 1) // 2
        A.reserve(name, f32e, p0, p1)
        P[name] = (None, parts, n, dt_)

    def V(name, lo=0, n=None, parts=None, p0=0):
        off, pts, tot, dt_ = P[name]
        if n is None:
            n = tot - lo
        if parts is None:
            parts = pts
        base = A.t.ap() if dt_ == F32 else A.t.ap().bitcast(BF16)
        o = off if dt_ == F32 else 2 * off
        return base[p0:p0 + parts, o + lo:o + lo + n]

    AL("cvals", 128, 16, F32, 0, 79)
    AL("ubf", 128, KD * TF, BF16, 0, 9)
    AL("u2", 128, KD * TF, BF16, 0, 19)
    AL("zsz", 128, KI * TF, BF16, 10, 59)
    AL("xar", 128, KX * TF, BF16, 10, 79)
    AL("cfm", 64, TF, BF16, 10, 59)
    AL("r1", 128, KD * T, F32, 60, 79)
    for nm, pts, n, dt_, pp0, pp1 in [
            ("ctri", 128, 128, BF16, 30, 59), ("ci128b", 128, 128, BF16, 0, 79),
            ("ci128f", 128, 128, F32, 0, 79), ("conesb1", 1, 128, BF16, 0, 79),
            ("conesf1", 1, 128, F32, 0, 79), ("conesbc", 128, 1, BF16, 0, 79),
            ("conespp", 128, 128, BF16, 0, 79), ("conesf11", 1, 1, F32, 0, 79),
            ("cdD", 128, NH * 128, BF16, 30, 59), ("cconvb", 128, 18, F32, 20, 29),
            ("cxnw", 128, KD, F32, 0, 19), ("cxnb", 128, KD, F32, 0, 19),
            ("crmsw", 128, KI, F32, 60, 69), ("cdtb", 16, 1, F32, 0, 19),
            ("calog", 16, 1, F32, 0, 39), ("cmask", 128, 1, F32, 0, 49)]:
        AL(nm, pts, n, dt_, pp0, pp1)
    AL("negA", 16, 1, F32, 10, 39)
    AL("vdummy", 128, 1, F32, 0, 79)
    AL("p0a", 128, TF, F32, 0, 9)
    AL("p0b", 128, TF, F32, 0, 9)
    AL("p0c", 128, TF, F32, 0, 9)
    AL("wipbuf", 128, 2 * D, BF16, 10, 19)
    AL("wsq", 128, D, BF16, 10, 19)
    AL("wopnb", 128, DI, BF16, 10, 29)
    AL("snrow", 1, 128, F32, 10, 69)
    AL("sip", 128, NOT, F32, 10, 19)
    AL("convdg", 128, 4 * 18 * 128, BF16, 20, 29)
    AL("dtfm", 16, TF, F32, 10, 39)
    AL("cumfm", 16, T, F32, 30, 39)
    AL("lntmp", 16, T, F32, 30, 33)
    AL("dabf", 16, T, BF16, 30, 39)
    AL("datm", 128, NCH * 16, BF16, 30, 39)
    AL("cumhi", 16, T, BF16, 30, 59)
    AL("najptm", 128, NCH * 16, F32, 30, 59)
    AL("cumlo", 16, T, BF16, 30, 59)
    AL("najpfm", 16, T, F32, 30, 36)
    AL("ctmask", 128, 512, BF16, 30, 59)
    AL("wfm", 16, T, F32, 34, 39)
    AL("wtm", 128, NCH * 16, BF16, 30, 59)
    AL("cdrow", 1, NH * NCH, F32, 30, 59)
    AL("S2", 64, NH * 128, F32, 30, 59)
    AL("Sbf", 64, NH * 128, BF16, 50, 59)
    AL("Erecv", 64, NH * 128, F32, 40, 50)
    AL("xw", 128, NH * 128, BF16, 30, 59)
    AL("btm", 128, 64, BF16, 30, 59)
    AL("rowhl", 2, NH * 128, BF16, 51, 59)
    AL("xtm", 128, NH * 128, BF16, 51, 59)
    AL("es", 128, NH * 128, BF16, 51, 59)
    AL("eb", 64, NH * 128, BF16, 51, 59)
    AL("ce", 64, NH * 128, BF16, 51, 59)
    AL("gbuf", 128, NH * 128, BF16, 51, 59)
    AL("rrow", 1, 128, F32, 51, 59)
    AL("attnbf", 128, KD * T, BF16, 60, 69)
    AL("sq6", 128, KD * 512, BF16, 62, 65)
    AL("al6", 128, T, F32, 60, 69)
    AL("mu6", 128, T, F32, 60, 69)
    AL("tb6", 128, 512, F32, 66, 69)
    AL("wopbf", 128, 2 * DI, BF16, 60, 69)
    AL("sop", 128, 8, F32, 10, 69)
    AL("r1bfh", 128, KD * 1024, BF16, 70, 79)
    AL("sqstg", 128, KD * 1024, BF16, 70, 79)
    AL("al7", 128, 1024, F32, 70, 79)
    AL("mu7", 128, 1024, F32, 70, 79)
    AL("tb7", 128, 512, F32, 70, 79)
    AL("wgubuf", 128, 2 * 2 * D, BF16, 70, 79)
    AL("wdbuf", 128, 2 * KH * 128, BF16, 70, 79)
    AL("sgbuf", 128, 1024, BF16, 70, 79)
    AL("wgux", 128, 2 * D, BF16, 70, 79)

    _offs = A.finalize()
    for _nm in list(P):
        P[_nm] = (_offs[_nm],) + P[_nm][1:]
    P["g2buf"] = (P["xtm"][0], 128, NH * 128, BF16)
    for i, v in enumerate(CVALS):
        nc.const_aps.register(F32, float(v), V("cvals", i, 1))

    PA = nc.alloc_psum_tensor("PA", [128, 2048], F32)
    PB = nc.alloc_psum_tensor("PB", [128, 2048], F32)
    PAb = PA.ap().bitcast(BF16)

    S = Sched()
    bk = Banks()

    def ld(dst, src, waits=(), q="qg", slow=False):
        eng = "gpsimd" if q == "qg" else "sync"

        def f(e):
            return e.dma_start(out=dst, in_=src, allow_slow_non_contiguous=slow)
        return S.emit(eng, f, waits=waits, inc=q, inc_by=16)

    def mm_group(items, waits=()):
        banks = sorted({b for it in items for b in it[5]})
        w = list(waits) + bk.w_waits(banks)
        ev = None
        n = len(items)
        for i, (o, l, r, st, sp, _) in enumerate(items):
            def f(e, o=o, l=l, r=r, st=st, sp=sp):
                return e.matmul(o, l, r, start=st, stop=sp)
            rv = S.emit("tensor", f, waits=(w if i == 0 else ()),
                        inc=("t" if i == n - 1 else None))
            if rv:
                ev = rv
        bk.wrote(banks, ev)
        return ev

    def tr_group(items, waits=()):
        banks = sorted({b for it in items for b in it[3]})
        w = list(waits) + bk.w_waits(banks)
        ev = None
        n = len(items)
        for i, (o, src, idn, _) in enumerate(items):
            def f(e, o=o, src=src, idn=idn):
                return e.transpose(o, src, idn)
            rv = S.emit("tensor", f, waits=(w if i == 0 else ()),
                        inc=("t" if i == n - 1 else None))
            if rv:
                ev = rv
        bk.wrote(banks, ev)
        return ev

    def veg(eng, fn, rb=(), wb=(), waits=()):
        sem = {"vector": "v", "scalar": "s", "gpsimd": "g"}[eng]
        w = list(waits) + bk.r_waits(list(rb), eng) + bk.w_waits(list(wb))
        ev = S.emit(eng, fn, waits=w, inc=sem)
        if rb:
            bk.read(list(rb), ev, eng)
        if wb:
            bk.wrote(list(wb), ev)
        return ev

    def tap(name, src, waits):
        if debug:
            return ld(taps[name], src, waits=waits, q="qg")
        return None

    def rep_h(col_ap, reps=128):
        return col_ap.unsqueeze(2).broadcast_to([col_ap.shape[0], col_ap.shape[1], reps])

    def rep_o(t2d, k):
        return t2d.unsqueeze(1).broadcast_to([t2d.shape[0], k, t2d.shape[1]])

    def h3(name, k):
        return V(name).rearrange("p (k t) -> p k t", k=k)

    const_evs = []
    const_evs.append(ld(V("cvals", 0, 10), CVN))
    for dst, srcd in [(V("ci128b"), I128b), (V("ci128f"), I128f),
                      (V("conesb1"), ONESb1), (V("conesf1"), ONESf1),
                      (V("conesbc"), ONESbc), (V("conespp"), ONESpp),
                      (V("conesf11"), ONESf11),
                      (V("cxnw"), xnw), (V("cxnb"), xnb),
                      (V("cdtb"), dtbi), (V("calog"), alogi),
                      (V("cmask"), maskt)]:
        const_evs.append(ld(dst, srcd))
    ev_consts = const_evs[-1]
    ev_ubf = ld(V("ubf").rearrange("p (k t) -> p k t", k=KD),
                uT.rearrange("(k p) t -> p k t", p=128))

    ubf3 = h3("ubf", KD)
    u23 = h3("u2", KD)
    ev_sq0 = veg("vector", lambda e: e.tensor_tensor(V("u2"), V("ubf"), V("ubf"), ALU.mult),
                 waits=[ev_ubf])

    def stats_mms(src3, wt, nparts=KD):
        items = []
        for (lo, n) in NBLK:
            if lo < 2048:
                tgt, bks = PA.ap()[:, lo:lo + n], pa_banks(lo, n)
            else:
                tgt, bks = PB.ap()[:, 0:n], pb_banks(0, n)
            for k in range(nparts):
                items.append((tgt, V("conespp"), src3[:, k, lo:lo + n],
                              k == 0, k == nparts - 1, bks))
        return mm_group(items, waits=wt)

    ev_s0 = stats_mms(ubf3, [ev_ubf])
    ev_mu0a = veg("vector", lambda e: e.tensor_scalar(V("p0b", 0, 2048), PA.ap()[:, :], 1.0 / D, None, ALU.mult),
                  rb=pa_banks(0, 2048))
    ev_mu0b = veg("vector", lambda e: e.tensor_scalar(V("p0b", 2048, 3), PB.ap()[:, 0:3], 1.0 / D, None, ALU.mult),
                  rb=pb_banks(0, 3))
    ev_s1 = stats_mms(u23, [ev_sq0, ev_mu0a, ev_mu0b])
    ev_ms0a = veg("vector", lambda e: e.tensor_scalar(V("p0a", 0, 2048), PA.ap()[:, :], 1.0 / D, None, ALU.mult),
                  rb=pa_banks(0, 2048))
    ev_ms0b = veg("vector", lambda e: e.tensor_scalar(V("p0a", 2048, 3), PB.ap()[:, 0:3], 1.0 / D, None, ALU.mult),
                  rb=pb_banks(0, 3))
    ev_c0 = veg("vector", lambda e: e.tensor_tensor(V("p0c"), V("p0b"), V("p0b"), ALU.mult),
                waits=[ev_mu0a, ev_mu0b])
    ev_v1 = veg("vector", lambda e: e.tensor_tensor(V("p0a"), V("p0a"), V("p0c"), ALU.subtract),
                waits=[ev_c0, ev_ms0a, ev_ms0b])
    ev_l0 = veg("scalar", lambda e: e.activation(V("p0c"), V("p0a"), AF.Ln,
                                                 bias=LN_EPS * LN_EPS, scale=1.0 + LN_EPS),
                waits=[ev_v1])
    ev_a0 = veg("scalar", lambda e: e.activation(V("p0a"), V("p0c"), AF.Exp, scale=-0.5),
                waits=[ev_l0])
    ev_u2a = veg("vector", lambda e: e.tensor_tensor(u23, ubf3, rep_o(V("p0b"), KD), ALU.subtract),
                 waits=[ev_s1, ev_ubf])
    ev_u2b = veg("vector", lambda e: e.tensor_tensor(u23, u23, rep_o(V("p0a"), KD), ALU.mult),
                 waits=[ev_u2a, ev_a0])
    ev_u2done = None
    for k in range(KD):
        ev_u2done = veg("vector", lambda e, k=k: e.tensor_scalar(
            u23[:, k, :], u23[:, k, :], V("cxnw", k, 1), V("cxnb", k, 1),
            ALU.mult, ALU.add),
            waits=[ev_u2b, ev_consts])
    tap("t_u2", V("u2"), [ev_u2done])

    zsz3 = h3("zsz", KI)
    xar3 = h3("xar", KX)
    ev_wip_dma = [None] * NOT
    ev_wip_lastmm = [None] * NOT
    ev_wsq_ev = [None] * NOT
    ev_nm_ev = [None] * NOT
    ev_sip_parts = []
    ev_convdg = None

    for o in range(NOT):
        slot = o % 2
        wbv = V("wipbuf", slot * D, D)
        wt = [ev_consts] if o < 2 else [ev_wsq_ev[o - 2]]
        ev_wip_dma[o] = ld(wbv, wip[o], waits=wt)
        ev_wsq_ev[o] = veg("scalar", lambda e, wbv=wbv: e.square(V("wsq"), wbv),
                           waits=[ev_wip_dma[o], ev_nm_ev[o - 1] if o else None])
        nm_items = [(PB.ap()[0:1, 512:640], V("conesbc"),
                     V("wsq", k * 128, 128), k == 0, k == KD - 1, [5]) for k in range(KD)]
        ev_nm_ev[o] = mm_group(nm_items, waits=[ev_wsq_ev[o]])
        ev_row = veg("vector", lambda e: e.tensor_copy(V("snrow"), PB.ap()[0:1, 512:640]), rb=[5])
        ev_tc = mm_group([(PB.ap()[:, 1024:1025], V("snrow"), V("conesf11"), True, True, [6])],
                         waits=[ev_row])
        ev_sip_parts.append(
            veg("vector", lambda e, o=o: e.tensor_copy(V("sip", o, 1), PB.ap()[:, 1024:1025]), rb=[6]))
    veg("vector", lambda e: e.memzero(V("vdummy")))
    veg("vector", lambda e: e.memzero(V("vdummy")))
    ev_sc0 = veg("vector", lambda e: e.tensor_scalar(V("sip"), V("sip"), 1e-12, None, ALU.max),
                 waits=[ev_sip_parts[-1]])
    ev_sc1 = veg("scalar", lambda e: e.activation(V("sip"), V("sip"), AF.Ln, scale=float(D)),
                 waits=[ev_sc0])
    ev_sip = veg("scalar", lambda e: e.activation(V("sip"), V("sip"), AF.Exp, scale=-0.5),
                 waits=[ev_sc1])
    tap("t_sip1", V("sip"), [ev_sip])

    ev_z_last = ev_x_last = ev_dt_last = ev_c_last = None
    grp = 0
    ev_opn_sq = [None] * 8
    ev_opn_nmB = [None] * 8
    ev_opn_parts = []

    def emit_opnorm(o2):
        wt = [ev_u2done] if o2 == 0 else [ev_opn_sq[o2 - 1]]
        dma = ld(V("wopnb"), wopb[o2], waits=wt, q="qg")
        sqA = veg("scalar", lambda e: e.square(V("wsq"), V("wopnb", 0, D)),
                  waits=[dma, ev_nm_ev[NOT - 1] if o2 == 0 else ev_opn_nmB[o2 - 1]])
        nmA = mm_group([(PB.ap()[0:1, 512:640], V("conesbc"), V("wsq", k * 128, 128),
                         k == 0, False, [5]) for k in range(KD)], waits=[sqA])
        sqB = veg("scalar", lambda e: e.square(V("wsq"), V("wopnb", D, D)),
                  waits=[nmA])
        ev_opn_sq[o2] = sqB
        ev_opn_nmB[o2] = mm_group([(PB.ap()[0:1, 512:640], V("conesbc"), V("wsq", k * 128, 128),
                                    False, k == KD - 1, [5]) for k in range(KD)], waits=[sqB])
        orow = veg("vector", lambda e: e.tensor_copy(V("snrow"), PB.ap()[0:1, 512:640]), rb=[5])
        otc = mm_group([(PB.ap()[:, 1024:1025], V("snrow"), V("conesf11"), True, True, [6])],
                       waits=[orow])
        ev_opn_parts.append(
            veg("vector", lambda e, o2=o2: e.tensor_copy(V("sop", o2, 1), PB.ap()[:, 1024:1025]), rb=[6]))

    for o in range(NOT):
        slot = o % 2
        wt = ([ev_wsq_ev[NOT - 1], ev_wsq_ev[NOT - 2]] if o < 2
              else [ev_wip_lastmm[o - 2]])
        ev_wip_dma[o] = ld(V("wipbuf", slot * D, D), wip[o], waits=wt)
        if 2 <= o < 10:
            emit_opnorm(o - 2)
        for bi, (lo, n) in enumerate(NBLKG):
            g4 = grp % 4
            grp += 1
            pslice = PA.ap()[:, g4 * 512:g4 * 512 + n]
            bks = [g4]
            items = [(pslice, V("wipbuf", slot * D + k * 128, 128),
                      u23[:, k, lo:lo + n], k == 0, k == KD - 1, bks)
                     for k in range(KD)]
            ev_mm = mm_group(items, waits=[ev_wip_dma[o], ev_u2done])
            ev_wip_lastmm[o] = ev_mm
            if o < 16:
                ev_z_last = veg("vector", lambda e, o=o, lo=lo, n=n, p=pslice: e.tensor_scalar(
                    zsz3[:, o, lo:lo + n], p, V("sip", o, 1), None, ALU.mult),
                    rb=bks, waits=[ev_sip])
            elif o < 33:
                tx = o - 16
                if bi > 0:
                    ev_x_last = veg("vector", lambda e, tx=tx, lo=lo, n=n, p=pslice, o=o: e.tensor_scalar(
                        xar3[:, tx, lo:lo + n], p, V("sip", o, 1), None, ALU.mult),
                        rb=bks, waits=[ev_sip])
                else:
                    ev_x_last = veg("vector", lambda e, tx=tx, lo=lo, n=n, p=pslice, o=o: e.tensor_scalar(
                        xar3[:, tx, lo:lo + n], p, V("sip", o, 1), V("cmask"), ALU.mult, ALU.mult),
                        rb=bks, waits=[ev_sip])
            elif o == 33:
                pd = pslice[0:16, :]
                e1 = veg("scalar", lambda e, pd=pd, lo=lo, n=n: e.activation(
                    V("dtfm", lo, n), pd, AF.Exp, bias=V("cdtb"), scale=V("sip", 33, 1, parts=16)),
                    rb=bks, waits=[ev_sip, ev_consts])
                ev_dt_last = veg("scalar", lambda e, lo=lo, n=n: e.activation(
                    V("dtfm", lo, n), V("dtfm", lo, n), AF.Ln, bias=1.0), waits=[e1])
            else:
                pc = pslice[0:64, :]
                if bi > 0:
                    ev_c_last = veg("vector", lambda e, pc=pc, lo=lo, n=n: e.tensor_scalar(
                        V("cfm", lo, n), pc, V("sip", 34, 1, parts=64), None, ALU.mult),
                        rb=bks, waits=[ev_sip])
                else:
                    ev_c_last = veg("vector", lambda e, pc=pc, lo=lo, n=n: e.tensor_scalar(
                        V("cfm", lo, n), pc, V("sip", 34, 1, parts=64), V("cmask", parts=64),
                        ALU.mult, ALU.mult),
                        rb=bks, waits=[ev_sip])
    ev_so0 = veg("vector", lambda e: e.tensor_scalar(V("sop"), V("sop"), 1e-12, None, ALU.max),
                 waits=[ev_opn_parts[-1]])
    ev_so1 = veg("scalar", lambda e: e.activation(V("sop"), V("sop"), AF.Ln), waits=[ev_so0])
    ev_sop = veg("scalar", lambda e: e.activation(V("sop"), V("sop"), AF.Exp, scale=-0.5),
                 waits=[ev_so1])
    tap("t_sop", V("sop"), [ev_sop])

    ev_tap_z = tap("t_z", V("zsz"), [ev_z_last])
    ev_tap_x = tap("t_xbc", V("xar"), [ev_x_last])
    tap("t_dt", V("dtfm"), [ev_dt_last])

    p1_done = [ev_wip_lastmm[NOT - 1], ev_sip, ev_x_last, ev_z_last, ev_dt_last, ev_c_last]
    tap("t_sip2", V("sip"), p1_done)
    ev_convdg = ld(V("convdg"), convd, waits=p1_done)
    ev_convb = ld(V("cconvb"), convbi, waits=p1_done)
    ev_zsilu = veg("scalar", lambda e: e.activation(V("zsz"), V("zsz"), AF.Silu),
                   waits=[ev_z_last, ev_tap_z])
    ev_conv = None
    grp = 0
    for t in [17] + list(range(17)):
        for (lo, n) in CBLK:
            g4 = grp % 4
            grp += 1
            pslice = PA.ap()[:, g4 * 512:g4 * 512 + n]
            bks = [g4]
            if t < 17:
                rhs = [xar3[:, t, lo - 3 + k:lo - 3 + k + n] for k in range(4)]
                items = [(pslice, V("convdg", (k * 18 + t) * 128, 128),
                          rhs[k], k == 0, k == 3, bks) for k in range(4)]
            else:
                rhs = [V("xar", 16 * TF + lo - 3 + k, n) for k in range(4)]
                items = [(pslice[0:64, :], V("convdg", (k * 18 + 17) * 128, 64),
                          rhs[k], k == 0, k == 3, bks) for k in range(4)]
            ev_mm = mm_group(items, waits=[ev_convdg, ev_x_last, ev_c_last])
            if t < 17:
                ev_conv = veg("scalar", lambda e, t=t, lo=lo, n=n, p=pslice: e.activation(
                    xar3[:, t, lo:lo + n], p, AF.Silu, bias=V("cconvb", t, 1)),
                    rb=bks, waits=[ev_zsilu, ev_tap_x, ev_convb])
            else:
                ev_conv = veg("scalar", lambda e, lo=lo, n=n, p=pslice: e.activation(
                    V("cfm", lo, n), p[0:64, :], AF.Silu, bias=V("cconvb", 17, 1, parts=64)),
                    rb=bks, waits=[ev_zsilu, ev_convb])
    tap("t_x", V("xar"), [ev_conv])
    tap("t_sz", V("zsz"), [ev_conv])
    tap("t_c", V("cfm"), [ev_conv])

    dttok = V("dtfm", HALO, T)
    ev_ctri = ld(V("ctri"), TRIb, waits=[ev_conv])
    ev_cdDl = ld(V("cdD"), dDt, waits=[ev_conv])
    ev_negA = veg("scalar", lambda e: e.activation(V("negA"), V("calog"), AF.Exp), waits=[ev_consts])
    ev_dabf = veg("vector", lambda e: e.tensor_scalar(V("dabf"), dttok, V("negA"), -1.0,
                                                      ALU.mult, ALU.mult),
                  waits=[ev_dt_last, ev_negA])
    tap("t_negA", V("negA"), [ev_negA])
    tap("t_dabf", V("dabf"), [ev_dabf])
    tr_items = [(PAb[:, 2048 + c * 16:2048 + c * 16 + 16],
                 V("dabf", c * 128, 128), V("ci128b", 0, 16, parts=16), [2]) for c in range(NCH)]
    ev_datr = tr_group(tr_items, waits=[ev_dabf, ev_consts])
    ev_datm = veg("vector", lambda e: e.tensor_copy(V("datm"), PAb[:, 2048:2304]), rb=[2])
    cum_items = [(PB.ap()[0:16, c * 128:(c + 1) * 128], V("datm", c * 16, 16),
                  V("ctri"), True, True, pb_banks(c * 128, 128)) for c in range(NCH)]
    ev_cummm = mm_group(cum_items, waits=[ev_datm, ev_ctri])
    ev_cum = veg("vector", lambda e: e.tensor_copy(V("cumfm"), PB.ap()[0:16, 0:2048]),
                 rb=pb_banks(0, 2048))
    tap("t_cum", V("cumfm"), [ev_cum])
    ev_ln = veg("scalar", lambda e: e.activation(V("lntmp"), dttok, AF.Ln), waits=[ev_dt_last])
    ev_najp = veg("vector", lambda e: e.tensor_tensor(V("najpfm"), V("lntmp"), V("cumfm"), ALU.subtract),
                  waits=[ev_ln, ev_cum])
    tr_items = [(PA.ap()[:, 1024 + c * 16:1024 + c * 16 + 16],
                 V("najpfm", c * 128, 128), V("ci128f", 0, 16, parts=16), [2]) for c in range(NCH)]
    ev_najptr = tr_group(tr_items, waits=[ev_najp])
    ev_najptm = veg("vector", lambda e: e.tensor_copy(V("najptm"), PA.ap()[:, 1024:1280]), rb=[2])
    ev_tmk = ld(V("ctmask"), TMASKb, waits=[ev_conv])
    ev_chi = veg("scalar", lambda e: e.copy(V("cumhi"), V("cumfm")), waits=[ev_cum])
    ev_clo = veg("vector", lambda e: e.tensor_tensor(V("cumlo"), V("cumfm"), V("cumhi"), ALU.subtract),
                 waits=[ev_chi])
    ev_wf = None
    for c in range(NCH):
        ev_wf = veg("vector", lambda e, c=c: e.tensor_scalar(
            V("wfm", c * 128, 128), V("cumfm", c * 128, 128), -1.0,
            V("cumfm", c * 128 + 127, 1), ALU.mult, ALU.add), waits=[ev_cum])
    ev_wexp = veg("scalar", lambda e: e.activation(V("wfm"), V("wfm"), AF.Exp), waits=[ev_wf])
    ev_wmul = veg("vector", lambda e: e.tensor_tensor(V("wfm"), V("wfm"), dttok, ALU.mult),
                  waits=[ev_wexp])
    ev_wbf = veg("scalar", lambda e: e.copy(V("dabf"), V("wfm")), waits=[ev_wmul, ev_datm, ev_cummm])
    tr_items = [(PAb[:, 2048 + c * 16:2048 + c * 16 + 16],
                 V("dabf", c * 128, 128), V("ci128b", 0, 16, parts=16), [2]) for c in range(NCH)]
    ev_wtr = tr_group(tr_items, waits=[ev_wbf])
    ev_wtm = veg("vector", lambda e: e.tensor_copy(V("wtm"), PAb[:, 2048:2304]), rb=[2])
    ev_cdg = ld(V("cdrow"), V("cumfm")[:, 127::128], waits=[ev_cum], q="qs", slow=True)
    ev_cdx = veg("scalar", lambda e: e.activation(V("cdrow"), V("cdrow"), AF.Exp), waits=[ev_cdg])
    ev_s2z = veg("vector", lambda e: e.memzero(V("S2")))

    def chunk_scan(c, phase5, pv):
        ccol = HALO + c * CH
        ev = {}
        tr_items = [(PAb[:, h * 128:(h + 1) * 128],
                     xar3[:, h, ccol:ccol + 128], V("ci128b"), [0, 1]) for h in range(NH)]
        ev_xt = tr_group(tr_items, waits=[ev_conv])
        ev_bt = tr_group([(PAb[:, 3072:3136], V("xar", 16 * TF + ccol, 128, parts=64),
                           V("ci128b", 0, 64, parts=64), [3])], waits=[ev_conv])
        ev_btm = veg("vector", lambda e: e.tensor_copy(V("btm"), PAb[:, 3072:3136]),
                     rb=[3], waits=[pv.get("st")])
        xw3 = h3("xw", NH)
        pab3 = PAb[:, 0:2048].rearrange("p (h i) -> p h i", h=NH)
        ev_xw = veg("vector", lambda e: e.tensor_tensor(
            xw3, pab3, rep_h(V("wtm", c * 16, 16)), ALU.mult),
            rb=[0, 1], waits=[ev_wtm, pv.get("st")])
        if phase5:
            ev["xtm"] = veg("scalar", lambda e: e.copy(V("xtm"), PAb[:, 0:2048]),
                            rb=[0, 1], waits=[pv.get("y"), pv.get("rr"), pv.get("s2")])
        ev_cd = mm_group([(PA.ap()[0:64, 1600:1616], V("conesf1", 0, 64),
                           V("cdrow")[:, c::16], True, True, [3])], waits=[ev_cdx])
        if phase5:
            ev_rh = ld(V("rowhl", 0, NH * 128, parts=1), V("cumhi")[:, c * 128:(c + 1) * 128],
                       waits=[ev_chi, pv.get("ai"), pv.get("s2")], q="qs")
            ev_rl = ld(V("rowhl", 0, NH * 128, parts=1, p0=1), V("cumlo")[:, c * 128:(c + 1) * 128],
                       waits=[ev_clo, pv.get("ai")], q="qs")
            items = [(PB.ap()[:, ni * 512:(ni + 1) * 512], V("conespp", 0, 128, parts=2),
                      V("rowhl", ni * 512, 512), True, True,
                      pb_banks(ni * 512, 512)) for ni in range(4)]
            ev["ai"] = mm_group(items, waits=[ev_rh, ev_rl])
            ev_sc = mm_group([(PA.ap()[:, 1024:1152], V("xar", 16 * TF + ccol, 128, parts=64),
                               V("cfm", ccol, 128), True, True, [2])], waits=[ev_conv])
            ev_eb = veg("scalar", lambda e: e.activation(V("eb"), PB.ap()[0:64, 0:2048], AF.Exp),
                        rb=pb_banks(0, 2048), waits=[pv.get("ce"), pv.get("s2")])
            items = [(PB.ap()[:, ni * 512:(ni + 1) * 512], V("ci128b"),
                      V("ctmask", 0, 512), False, True, pb_banks(ni * 512, 512))
                     for ni in range(4)]
            ev_ai2 = mm_group(items, waits=[ev_tmk, ev_eb])
            es3 = h3("es", NH)
            ev_es = None
            for h in range(NH):
                ev_es = veg("scalar", lambda e, h=h: e.activation(
                    es3[:, h, :], PB.ap()[:, h * 128:(h + 1) * 128], AF.Exp,
                    bias=V("najptm", c * 16 + h, 1)),
                    rb=pb_banks(h * 128, 128),
                    waits=[ev_najptm] + ([pv.get("y"), pv.get("s2")] if h == 0 else []))
            ev["es"] = ev_es
            ev_att = veg("vector", lambda e: e.tensor_tensor(
                es3, es3, rep_o(PA.ap()[:, 1024:1152], NH), ALU.mult),
                rb=[2], waits=[ev["es"]])
            eb3 = V("eb").rearrange("p (h i) -> p h i", h=NH)
            ce3 = V("ce").rearrange("p (h i) -> p h i", h=NH)
            ev["ce"] = veg("vector", lambda e: e.tensor_tensor(
                ce3, rep_o(V("cfm", ccol, 128), NH), eb3, ALU.mult),
                waits=[ev_eb, ev_conv, pv.get("y")])
            ev_sbf = veg("scalar", lambda e: e.copy(V("Sbf"), V("S2")),
                         waits=[pv.get("s2"), pv.get("y")])
            ev["sbf"] = ev_sbf
        st_items = [(PB.ap()[0:64, h * 128:(h + 1) * 128], V("btm"),
                     xw3[:, h, :], True, True, pb_banks(h * 128, 128)) for h in range(NH)]
        ev["st"] = mm_group(st_items, waits=[ev_xw, ev_btm])
        s23 = V("S2").rearrange("p (h i) -> p h i", h=NH)
        ev_s2m = veg("vector", lambda e: e.tensor_tensor(
            s23, s23, PA.ap()[0:64, 1600:1616].unsqueeze(2).broadcast_to([64, 16, 128]), ALU.mult),
            rb=[3], waits=[pv.get("s2")] + ([ev["sbf"]] if phase5 else []))
        ev["s2"] = veg("vector", lambda e: e.tensor_tensor(
            V("S2"), V("S2"), PB.ap()[0:64, 0:2048], ALU.add),
            rb=pb_banks(0, 2048), waits=[ev_s2m])
        if phase5:
            y_items = []
            for h in range(NH):
                bks = pb_banks(h * 128, 128)
                y_items.append((PB.ap()[:, h * 128:(h + 1) * 128],
                                V("xtm", h * 128, 128), h3("es", NH)[:, h, :], True, False, bks))
                y_items.append((PB.ap()[:, h * 128:(h + 1) * 128],
                                V("Sbf", h * 128, 128, parts=64),
                                h3("ce", NH)[:, h, :], False, False, bks))
                y_items.append((PB.ap()[:, h * 128:(h + 1) * 128],
                                V("cdD", h * 128, 128), xar3[:, h, ccol:ccol + 128],
                                False, True, bks))
            ev["y"] = mm_group(y_items, waits=[ev["xtm"], ev_att, ev["ce"], ev_sbf, ev_conv,
                                               ev_cdDl])
        if phase5:
            g3 = h3("gbuf", NH)
            ev_g = veg("vector", lambda e: e.tensor_tensor(
                g3, PB.ap()[:, 0:2048].rearrange("p (h i) -> p h i", h=NH),
                zsz3[:, :, ccol:ccol + 128], ALU.mult),
                rb=pb_banks(0, 2048), waits=[ev["y"]])
            ev_g2 = veg("vector", lambda e: e.tensor_tensor(V("g2buf"), V("gbuf"), V("gbuf"), ALU.mult),
                        waits=[ev_g, pv.get("rr")])
            rr_items = [(PA.ap()[0:1, 1792:1920], V("conesbc"), V("g2buf", h * 128, 128),
                         h == 0, h == NH - 1, [3]) for h in range(NH)]
            ev["rr"] = mm_group(rr_items, waits=[ev_g2])
            ev_rl1 = veg("scalar", lambda e: e.activation(V("rrow"), PA.ap()[0:1, 1792:1920], AF.Ln,
                                                          bias=RMS_EPS, scale=1.0 / DI),
                         rb=[3], waits=[pv.get("rb")])
            ev_rl2 = veg("scalar", lambda e: e.activation(V("rrow"), V("rrow"), AF.Exp, scale=-0.5),
                         waits=[ev_rl1])
            ev["rb"] = mm_group([(PA.ap()[:, 1664:1792], V("conesf1"), V("rrow"), True, True, [3])],
                                waits=[ev_rl2])
            ev["sc"] = ev_rl2
            ev["gn"] = veg("vector", lambda e: e.tensor_tensor(
                xar3[:, 0:16, ccol:ccol + 128], g3,
                PA.ap()[:, 1664:1792].unsqueeze(1).broadcast_to([128, NH, 128]), ALU.mult),
                rb=[3], waits=[ev_g])
        return ev

    pv = {"s2": ev_s2z}
    for c in range(NCH):
        pv = chunk_scan(c, False, pv)
    ev_E = pv["s2"]
    ev_tap_E = tap("t_E", V("S2"), [ev_E])

    ev_esend = ld(e_in.ap(), V("S2"), waits=[ev_E], q="qs")

    def coll(e):
        return e.collective_compute(
            "AllGather", ALU.bypass,
            replica_groups=[[0, 1], [2, 3], [4, 5], [6, 7]],
            ins=[e_in.ap().opt()], outs=[e_out.ap().opt()])
    ev_cc = S.emit("gpsimd", coll, waits=[ev_esend], inc="cc")
    ev_erecv = ld(V("Erecv"), e_out.ap()[0:1, :], waits=[ev_cc], q="qs")
    ev_s2i = veg("vector", lambda e: e.tensor_scalar(
        V("S2"), V("Erecv"), V("cmask", parts=64), None, ALU.mult),
        waits=[ev_erecv, pv["s2"], ev_consts, ev_tap_E])

    pv = {"s2": ev_s2i, "st": pv["st"]}
    for c in range(NCH):
        pv = chunk_scan(c, True, pv)
    ev_gn_all = pv["gn"]
    p5_done = [ev_gn_all, pv["s2"], pv["sc"]]
    tap("t_sip3", V("sip"), p5_done)
    tap("t_gn", V("xar", 0, 16 * TF), [ev_gn_all])

    ev_r1u = ld(V("r1").rearrange("p (k t) -> p k t", k=KD),
                uT[:, HALO:HALO + T].rearrange("(k p) t -> p k t", p=128),
                waits=[ev_ubf] + p5_done, q="qs")
    attn3 = h3("attnbf", KD)
    ev_wop_fold = [None] * 8
    ev_wop_lastmm = [None] * 8
    ev_rmsw = ld(V("crmsw"), rmsw, waits=p5_done)

    ev_attn = None
    ev_unf = [None] * 8
    for o2 in range(8):
        slot = o2 % 2
        wfv = V("wopbf", slot * DI, DI)
        wt = ([ev_wop_lastmm[o2 - 2]] if o2 >= 2
              else p5_done + [ev_opn_nmB[7]])
        ev_unf[o2] = ld(wfv, wopb[o2], waits=wt, q="qg")
        wf3 = wfv.rearrange("p (k o) -> p k o", k=KI)
        ev_wop_fold[o2] = veg("vector", lambda e, wf3=wf3: e.tensor_tensor(
            wf3, wf3, rep_h(V("crmsw")), ALU.mult),
            waits=[ev_unf[o2], ev_rmsw])
        ps = PA.ap() if o2 % 2 == 0 else PB.ap()
        pbk = pa_banks if o2 % 2 == 0 else pb_banks
        items = []
        for k in range(KI):
            for ni in range(4):
                items.append((ps[:, ni * 512:(ni + 1) * 512],
                              V("wopbf", slot * DI + k * 128, 128),
                              xar3[:, k, HALO + ni * 512:HALO + (ni + 1) * 512],
                              k == 0, k == KI - 1, pbk(ni * 512, 512)))
        ev_wop_lastmm[o2] = mm_group(items, waits=[ev_wop_fold[o2], ev_gn_all])
        for ni in range(4):
            ev_attn = veg("vector", lambda e, o2=o2, ni=ni, ps=ps: e.tensor_scalar(
                attn3[:, o2, ni * 512:(ni + 1) * 512], ps[:, ni * 512:(ni + 1) * 512],
                V("sop", o2, 1), None, ALU.mult),
                rb=pbk(ni * 512, 512), waits=[ev_sop])
    tap("t_attn", V("attnbf"), [ev_attn])
    tap("t_wf7", V("wopbf", DI, DI), [ev_wop_fold[7], ev_wop_lastmm[7]])

    items = [(PA.ap()[:, ni * 512:(ni + 1) * 512], V("conespp"),
              attn3[:, k, ni * 512:(ni + 1) * 512], k == 0, k == KD - 1,
              pa_banks(ni * 512, 512)) for ni in range(4) for k in range(KD)]
    ev_sm6 = mm_group(items, waits=[ev_attn])
    ev_mu6 = veg("vector", lambda e: e.tensor_scalar(V("mu6"), PA.ap()[:, :], 1.0 / D, None, ALU.mult),
                 rb=pa_banks(0, 2048))
    sq6 = h3("sq6", KD)
    ev_prev_sq = None
    for ni in range(4):
        ev_sq = veg("vector", lambda e, ni=ni: e.tensor_tensor(
            sq6, attn3[:, :, ni * 512:(ni + 1) * 512],
            attn3[:, :, ni * 512:(ni + 1) * 512], ALU.mult),
            waits=[ev_attn, ev_prev_sq])
        items = [(PA.ap()[:, ni * 512:(ni + 1) * 512], V("conespp"),
                  sq6[:, k, :], k == 0, k == KD - 1, pa_banks(ni * 512, 512))
                 for k in range(KD)]
        ev_prev_sq = mm_group(items, waits=[ev_sq, ev_mu6])
    ev_ms6 = veg("vector", lambda e: e.tensor_scalar(V("al6"), PA.ap()[:, :], 1.0 / D, None, ALU.mult),
                 rb=pa_banks(0, 2048))
    ev_t6 = veg("vector", lambda e: e.tensor_tensor(V("tb6", 0, 2048), V("mu6"), V("mu6"), ALU.mult),
                waits=[ev_ms6])
    ev_v6 = veg("vector", lambda e: e.tensor_tensor(V("al6"), V("al6"), V("tb6", 0, 2048), ALU.subtract),
                waits=[ev_t6])
    ev_a6a = veg("scalar", lambda e: e.activation(V("al6"), V("al6"), AF.Ln, bias=LN_EPS), waits=[ev_v6])
    ev_a6 = veg("scalar", lambda e: e.activation(V("al6"), V("al6"), AF.Exp, scale=-0.5), waits=[ev_a6a])
    ev_mua6 = veg("vector", lambda e: e.tensor_tensor(V("mu6"), V("mu6"), V("al6"), ALU.mult),
                  waits=[ev_a6])
    r13 = V("r1").rearrange("p (k t) -> p k t", k=KD)
    ev_r1 = veg("vector", lambda e: e.tensor_tensor(r13, r13, rep_o(V("mu6"), KD), ALU.subtract),
                waits=[ev_mua6, ev_r1u])
    for k in range(KD):
        for ni in range(4):
            ev_tm = veg("vector", lambda e, k=k, ni=ni: e.tensor_tensor(
                V("tb6"), attn3[:, k, ni * 512:(ni + 1) * 512],
                V("al6", ni * 512, 512), ALU.mult), waits=[ev_r1])
            ev_r1 = veg("vector", lambda e, k=k, ni=ni: e.tensor_tensor(
                r13[:, k, ni * 512:(ni + 1) * 512], r13[:, k, ni * 512:(ni + 1) * 512],
                V("tb6"), ALU.add), waits=[ev_tm])
    tap("t_r1", V("r1"), [ev_r1])

    def r1h3(hf):
        return V("r1").rearrange("p (k t) -> p k t", k=KD)[:, :, hf * 1024:(hf + 1) * 1024]

    hv = V("xar", 0, KH * 1024).rearrange("p (o t) -> p o t", o=KH)
    xf3 = V("xar", KH * 1024, KD * 1024).rearrange("p (k t) -> p k t", k=KD)
    prev_gu = {}
    prev_dn = {}
    ev_hf_done = None
    ev_out_last = None
    for hf in range(2):
        ev_rbf = ld(V("r1bfh"), r1h3(hf), waits=[ev_r1, ev_hf_done], q="qg")
        rb3 = h3("r1bfh", KD)
        ev_sq7 = veg("vector", lambda e: e.tensor_tensor(V("sqstg"), V("r1bfh"), V("r1bfh"), ALU.mult),
                     waits=[ev_rbf, ev_hf_done])
        items = [(PA.ap()[:, ni * 512:(ni + 1) * 512], V("conespp"),
                  rb3[:, k, ni * 512:(ni + 1) * 512], k == 0, k == KD - 1,
                  pa_banks(ni * 512, 512)) for ni in range(2) for k in range(KD)]
        ev_sm7 = mm_group(items, waits=[ev_rbf])
        ev_mu7 = veg("vector", lambda e: e.tensor_scalar(V("mu7"), PA.ap()[:, 0:1024], 1.0 / D, None, ALU.mult),
                     rb=pa_banks(0, 1024))
        stg3 = h3("sqstg", KD)
        items = [(PA.ap()[:, ni * 512:(ni + 1) * 512], V("conespp"),
                  stg3[:, k, ni * 512:(ni + 1) * 512], k == 0, k == KD - 1,
                  pa_banks(ni * 512, 512)) for ni in range(2) for k in range(KD)]
        ev_sm7b = mm_group(items, waits=[ev_sq7, ev_mu7])
        ev_ms7 = veg("vector", lambda e: e.tensor_scalar(V("al7"), PA.ap()[:, 0:1024], 1.0 / D, None, ALU.mult),
                     rb=pa_banks(0, 1024))
        ev_t7 = veg("vector", lambda e: e.tensor_tensor(V("mu7"), V("mu7"), V("mu7"), ALU.mult) if False
                    else e.tensor_tensor(V("tb7"), V("mu7", 0, 512), V("mu7", 0, 512), ALU.mult),
                    waits=[ev_ms7])
        ev_v7 = veg("vector", lambda e: e.tensor_tensor(V("al7", 0, 512), V("al7", 0, 512), V("tb7"), ALU.subtract),
                    waits=[ev_t7])
        ev_t7b = veg("vector", lambda e: e.tensor_tensor(V("tb7"), V("mu7", 512, 512), V("mu7", 512, 512), ALU.mult),
                     waits=[ev_v7])
        ev_v7b = veg("vector", lambda e: e.tensor_tensor(V("al7", 512, 512), V("al7", 512, 512), V("tb7"), ALU.subtract),
                     waits=[ev_t7b])
        ev_a7a = veg("scalar", lambda e: e.activation(V("al7"), V("al7"), AF.Ln, bias=LN_EPS), waits=[ev_v7b])
        ev_a7 = veg("scalar", lambda e: e.activation(V("al7"), V("al7"), AF.Exp, scale=-0.5), waits=[ev_a7a])
        ev_mua7 = veg("vector", lambda e: e.tensor_tensor(V("mu7"), V("mu7"), V("al7"), ALU.mult),
                      waits=[ev_a7])
        ev_xf1 = veg("vector", lambda e: e.tensor_tensor(xf3, rb3, rep_o(V("mu7"), KD), ALU.subtract),
                     waits=[ev_mua7, prev_dn.get(7)])
        ev_xf = veg("vector", lambda e: e.tensor_tensor(xf3, xf3, rep_o(V("al7"), KD), ALU.mult),
                    waits=[ev_xf1])
        if hf == 0:
            tap("t_xf", V("xar", KH * 1024, KD * 1024), [ev_xf])
        ev_h_last = None
        ev_sg_prev = None
        for o in range(KH):
            slot = o % 3
            gu = (V("wgubuf", slot * 2 * D, 2 * D) if slot < 2
                  else V("wgux", 0, 2 * D))
            wt = [ev_r1] if (hf == 0 and o < 3) else [prev_gu.get((o - 3) % KH)]
            ev_gud = ld(gu.rearrange("p (g d) -> p g d", g=2),
                        wgu[o].rearrange("g p d -> p g d"), waits=wt)
            psg = PA.ap() if o % 2 == 0 else PB.ap()
            bksg = (pa_banks if o % 2 == 0 else pb_banks)(0, 1024)
            bksu = (pa_banks if o % 2 == 0 else pb_banks)(1024, 1024)
            items = []
            for k in range(KD):
                for nn in range(2):
                    items.append((psg[:, nn * 512:(nn + 1) * 512],
                                  gu[:, k * 128:(k + 1) * 128],
                                  xf3[:, k, nn * 512:(nn + 1) * 512],
                                  k == 0, k == KD - 1, bksg))
            for k in range(KD):
                for nn in range(2):
                    items.append((psg[:, 1024 + nn * 512:1024 + (nn + 1) * 512],
                                  gu[:, D + k * 128:D + (k + 1) * 128],
                                  xf3[:, k, nn * 512:(nn + 1) * 512],
                                  k == 0, k == KD - 1, bksu))
            prev_gu[o] = mm_group(items, waits=[ev_gud, ev_xf])
            ev_sg = veg("scalar", lambda e, psg=psg: e.activation(
                V("sgbuf"), psg[:, 0:1024], AF.Silu, scale=QSCALE),
                rb=bksg, waits=[ev_h_last])
            ev_h_last = veg("vector", lambda e, psg=psg, o=o: e.scalar_tensor_tensor(
                hv[:, o, :], psg[:, 1024:2048], QSCALE, V("sgbuf"), ALU.mult, ALU.mult),
                rb=bksu, waits=[ev_sg])
        ev_stg_last = None
        for o2 in range(8):
            slot = o2 % 2
            wd = V("wdbuf", slot * KH * 128, KH * 128)
            wt = [prev_dn.get((o2 - 2) % 8)] if (hf > 0 or o2 >= 2) else [ev_h_last]
            ev_wdd = ld(wd, wdb[o2], waits=wt)
            ps = PA.ap() if o2 % 2 == 0 else PB.ap()
            bks = (pa_banks if o2 % 2 == 0 else pb_banks)(0, 1024)
            items = []
            for k in range(KH):
                for ni in range(2):
                    items.append((ps[:, ni * 512:(ni + 1) * 512], wd[:, k * 128:(k + 1) * 128],
                                  hv[:, k, ni * 512:(ni + 1) * 512], k == 0, k == KH - 1, bks))
            prev_dn[o2] = mm_group(items, waits=[ev_wdd, ev_h_last])
            ev_stg_last = veg("vector", lambda e, o2=o2, ps=ps: e.tensor_scalar(
                stg3[:, o2, :], ps[:, 0:1024], QSCALE, None, ALU.mult),
                rb=bks, waits=[ev_sm7b])
        if hf == 0:
            tap("t_h", V("xar", 0, KH * 1024), [ev_h_last])
            tap("t_stg", V("sqstg"), [ev_stg_last])
        items = [(PA.ap()[:, ni * 512:(ni + 1) * 512], V("conespp"),
                  stg3[:, k, ni * 512:(ni + 1) * 512], k == 0, k == KD - 1,
                  pa_banks(ni * 512, 512)) for ni in range(2) for k in range(KD)]
        ev_sm7c = mm_group(items, waits=[ev_stg_last])
        ev_mu7b = veg("vector", lambda e: e.tensor_scalar(V("mu7"), PA.ap()[:, 0:1024], 1.0 / D, None, ALU.mult),
                      rb=pa_banks(0, 1024))
        ev_sqp = veg("vector", lambda e: e.tensor_tensor(V("r1bfh"), V("sqstg"), V("sqstg"), ALU.mult),
                     waits=[ev_stg_last, ev_sm7b])
        sqp3 = h3("r1bfh", KD)
        items = [(PA.ap()[:, ni * 512:(ni + 1) * 512], V("conespp"),
                  sqp3[:, k, ni * 512:(ni + 1) * 512], k == 0, k == KD - 1,
                  pa_banks(ni * 512, 512)) for ni in range(2) for k in range(KD)]
        ev_sm7d = mm_group(items, waits=[ev_sqp, ev_mu7b])
        ev_ms7b = veg("vector", lambda e: e.tensor_scalar(V("al7"), PA.ap()[:, 0:1024], 1.0 / D, None, ALU.mult),
                      rb=pa_banks(0, 1024))
        ev_t7c = veg("vector", lambda e: e.tensor_tensor(V("tb7"), V("mu7", 0, 512), V("mu7", 0, 512), ALU.mult),
                     waits=[ev_ms7b])
        ev_v7c = veg("vector", lambda e: e.tensor_tensor(V("al7", 0, 512), V("al7", 0, 512), V("tb7"), ALU.subtract),
                     waits=[ev_t7c])
        ev_t7d = veg("vector", lambda e: e.tensor_tensor(V("tb7"), V("mu7", 512, 512), V("mu7", 512, 512), ALU.mult),
                     waits=[ev_v7c])
        ev_v7d = veg("vector", lambda e: e.tensor_tensor(V("al7", 512, 512), V("al7", 512, 512), V("tb7"), ALU.subtract),
                     waits=[ev_t7d])
        ev_a7c = veg("scalar", lambda e: e.activation(V("al7"), V("al7"), AF.Ln, bias=LN_EPS), waits=[ev_v7d])
        ev_a7d = veg("scalar", lambda e: e.activation(V("al7"), V("al7"), AF.Exp, scale=-0.5), waits=[ev_a7c])
        ev_mua7b = veg("vector", lambda e: e.tensor_tensor(V("mu7"), V("mu7"), V("al7"), ALU.mult),
                       waits=[ev_a7d])
        rh = r1h3(hf)
        out3 = out[:, hf * 1024:(hf + 1) * 1024].rearrange("(k p) t -> p k t", p=128)
        ev_fin = veg("vector", lambda e, rh=rh: e.tensor_tensor(rh, rh, rep_o(V("mu7"), KD), ALU.subtract),
                     waits=[ev_mua7b])
        ev_out_last = None
        for k in range(KD):
            for ni in range(2):
                ev_tm = veg("vector", lambda e, k=k, ni=ni, stg3=stg3: e.tensor_tensor(
                    V("tb7"), stg3[:, k, ni * 512:(ni + 1) * 512],
                    V("al7", ni * 512, 512), ALU.mult), waits=[ev_fin])
                ev_fin = veg("vector", lambda e, k=k, ni=ni, rh=rh: e.tensor_tensor(
                    rh[:, k, ni * 512:(ni + 1) * 512], rh[:, k, ni * 512:(ni + 1) * 512],
                    V("tb7"), ALU.add), waits=[ev_tm])
            ev_out_last = ld(out3[:, k, :], rh[:, k, :], waits=[ev_fin], q="qs")
        ev_hf_done = ev_fin

    S.emit("sync", lambda e: None, waits=[("qs", S.cnt["qs"]), ("qg", S.cnt["qg"])])

    if trunc is not None:
        S.truncate(trunc)

    with (
        nc.semaphore("sem_t") as s_t,
        nc.semaphore("sem_v") as s_v,
        nc.semaphore("sem_s") as s_s,
        nc.semaphore("sem_g") as s_g,
        nc.semaphore("sem_qg") as s_qg,
        nc.semaphore("sem_qs") as s_qs,
        nc.semaphore("sem_cc") as s_cc,
        nc.Block() as block,
    ):
        S.handles.update(t=s_t, v=s_v, s=s_s, g=s_g, qg=s_qg, qs=s_qs, cc=s_cc)

        @block.tensor
        def _(e):
            S.run("tensor", e)

        @block.vector
        def _(e):
            S.run("vector", e)

        @block.scalar
        def _(e):
            S.run("scalar", e)

        @block.gpsimd
        def _(e):
            S.run("gpsimd", e)

        @block.sync
        def _(e):
            S.run("sync", e)

    return nc


def prep_inputs(L0_attn_u, L0_attn_in_proj_w, L0_attn_conv_w, L0_attn_conv_b,
                L0_attn_dt_bias, L0_attn_A_log, L0_attn_D, L0_attn_x_norm_w,
                L0_attn_x_norm_b, L0_attn_rms_w, L0_attn_out_proj_w,
                L0_ffn_w_gate, L0_ffn_w_up, L0_ffn_w_down):
    u = np.ascontiguousarray(np.asarray(L0_attn_u, dtype=np.float32))
    ipw = np.asarray(L0_attn_in_proj_w, dtype=np.float32)
    opw = np.asarray(L0_attn_out_proj_w, dtype=np.float32)
    cw = np.asarray(L0_attn_conv_w, dtype=np.float32)
    cb = np.asarray(L0_attn_conv_b, dtype=np.float32)
    wg = np.asarray(L0_ffn_w_gate, dtype=np.int32)
    wu = np.asarray(L0_ffn_w_up, dtype=np.int32)
    wd = np.asarray(L0_ffn_w_down, dtype=np.int32)

    blocks = [ipw[t * 128:(t + 1) * 128] for t in range(33)]
    dtblk = np.zeros((128, D), np.float32)
    dtblk[:16] = ipw[4224:4240]
    blocks.append(dtblk)
    cdup = np.zeros((128, D), np.float32)
    cdup[:64] = ipw[4160:4224]
    blocks.append(cdup)
    wipb = np.stack(blocks)
    wipd = np.ascontiguousarray(
        wipb.reshape(NOT, 128, KD, 128).transpose(0, 3, 2, 1)).reshape(NOT, 128, D)

    wopd = np.ascontiguousarray(
        opw.reshape(8, 128, KI, 128).transpose(0, 3, 2, 1)).reshape(8, 128, DI)
    wgb = wg.reshape(KH, 128, KD, 128).transpose(0, 3, 2, 1).reshape(KH, 128, D)
    wub = wu.reshape(KH, 128, KD, 128).transpose(0, 3, 2, 1).reshape(KH, 128, D)
    wgud = np.ascontiguousarray(np.stack([wgb, wub], axis=1))
    wdd = np.ascontiguousarray(
        wd.reshape(8, 128, KH, 128).transpose(0, 3, 2, 1)).reshape(8, 128, HID)

    convdd = np.zeros((128, 4, 18, 128), np.float32)
    idx = np.arange(128)
    for k in range(4):
        for t in range(17):
            convdd[idx, k, t, idx] = cw[t * 128 + idx, k]
        q = np.arange(64)
        convdd[64 + q, k, 17, q] = cw[2112 + q, k]
    convdd = convdd.reshape(128, 4 * 18 * 128)
    convbd = np.zeros((128, 18), np.float32)
    for t in range(17):
        convbd[:, t] = cb[t * 128:(t + 1) * 128]
    convbd[:64, 17] = cb[2112:2176]

    xnw_ = np.ascontiguousarray(np.asarray(L0_attn_x_norm_w, np.float32).reshape(KD, 128).T)
    xnb_ = np.ascontiguousarray(np.asarray(L0_attn_x_norm_b, np.float32).reshape(KD, 128).T)
    rmsw_ = np.ascontiguousarray(np.asarray(L0_attn_rms_w, np.float32).reshape(KI, 128).T)
    dtb_ = np.ascontiguousarray(np.asarray(L0_attn_dt_bias, np.float32).reshape(16, 1))
    alog_ = np.ascontiguousarray(np.asarray(L0_attn_A_log, np.float32).reshape(16, 1))
    Dv = np.asarray(L0_attn_D, np.float32)
    dDd = np.zeros((128, NH * 128), np.float32)
    for h in range(NH):
        dDd[idx, h * 128 + idx] = Dv[h]

    shared = dict(wip=wipd, wopb=wopd, wgu=wgud, wdb=wdd, convd=convdd,
                  convbi=convbd, xnw=xnw_, xnb=xnb_, rmsw=rmsw_, dtbi=dtb_,
                  alogi=alog_, dDt=dDd)
    in_maps = []
    for c in range(8):
        s, hf = c // 2, c % 2
        t0 = hf * T
        uTc = np.zeros((D, TF), np.float32)
        uTc[:, HALO:] = u[s, t0:t0 + T].T
        if hf == 1:
            uTc[:, :HALO] = u[s, t0 - HALO:t0].T
        im = dict(shared)
        im["uT"] = np.ascontiguousarray(uTc)
        im["maskt"] = np.full((128, 1), 1.0 if hf == 1 else 0.0, np.float32)
        in_maps.append(im)
    return in_maps


def kernel(**inputs):
    in_maps = prep_inputs(**inputs)
    nc = build_program(debug=False)
    res = run_bass_kernel_spmd(nc, in_maps, core_ids=list(range(8)))
    outp = np.empty((4, 4096, D), np.float32)
    for c in range(8):
        s, hf = c // 2, c % 2
        outp[s, hf * T:(hf + 1) * T, :] = res.results[c]["out"].T
    return outp
